# version 14
# baseline (speedup 1.0000x reference)
import contextlib
import numpy as np
import concourse.bass as bass
import concourse.mybir as mybir
from concourse.ap import AP
from concourse.bass_utils import run_bass_kernel_spmd

F32 = mybir.dt.float32
BF16 = mybir.dt.bfloat16
AF = mybir.ActivationFunctionType
OP = mybir.AluOpType
AX = mybir.AxisListType

MAGIC = float(np.float32(3 * 2**22))
EPS = 1e-5
NCORES = 8
TOK = 1024
TH = 512
D = 2048
DC = 16
HID = 5632
HC = 44
HP = 22
INTER = 4096
IC = 32
GTS = 128
CONV = 4352
CCH = 34
XW = 1027
PROJ = 8512
HEADS = 64
PD = 64
NJ = 8
BLK = 128 * 256
NW_ELEMS = float(HID * D)

W1_SHARD = 2 * HP * BLK
HWH = W1_SHARD // 2
W2_SHARD = HC * BLK
RANK_SH = 2 * W1_SHARD + W2_SHARD
W1_OFF, W3_OFF, W2_OFF = 0, W1_SHARD, 2 * W1_SHARD

STAGE = 4
FFN_PARTS = 9


def K_FFN_PARTS():
    return FFN_PARTS


class Sem:
    def __init__(self, handle, step=1):
        self.h = handle
        self.step = step
        self.n = 0

    def inc(self, instr):
        self.n += self.step
        instr.then_inc(self.h, self.step)
        return self.n

    def wait(self, eng, n=None):
        n = self.n if n is None else n
        if n > 0:
            eng.wait_ge(self.h, n)


class Dma:
    def __init__(self, sems):
        self.sems = sems
        self.i = 0

    def issue(self, eng, out, in_, **kw):
        s = self.sems[self.i % len(self.sems)]
        if s.n > 0:
            eng.wait_ge(s.h, s.n)
        s.inc(eng.dma_start(out=out, in_=in_, **kw))
        self.i += 1
        return s, s.n

    def drain(self, eng):
        for s in self.sems:
            s.wait(eng)


class MK:
    def __init__(self):
        nc = self.nc = bass.Bass("TRN2", target_bir_lowering=False)
        DT = nc.dram_tensor
        self.x_in = DT("x_fm", [D, TOK], F32, kind="ExternalInput").ap()
        self.w1s_in = DT("w1s", [HID, 256], F32, kind="ExternalInput").ap()
        self.w3s_in = DT("w3s", [HID, 256], F32, kind="ExternalInput").ap()
        self.w2s_in = DT("w2s", [256, HID], F32, kind="ExternalInput").ap()
        self.ipw_in = DT("ipw_sh", [256, PROJ], F32, kind="ExternalInput").ap()
        self.opw_in = DT("opw_sh", [512, D], F32, kind="ExternalInput").ap()
        self.convw_in = DT("convw", [CONV, 4], F32, kind="ExternalInput").ap()
        self.convb_in = DT("convb", [CONV, 1], F32, kind="ExternalInput").ap()
        self.alog_in = DT("alog", [HEADS, 1], F32, kind="ExternalInput").ap()
        self.dtb_in = DT("dtb", [HEADS, 1], F32, kind="ExternalInput").ap()
        self.drep_in = DT("drep", [INTER, 1], F32, kind="ExternalInput").ap()
        self.nw_in = DT("nw", [INTER, 1], F32, kind="ExternalInput").ap()
        self.flag_in = DT("flag", [128, 1], F32, kind="ExternalInput").ap()
        self.out_ap = DT("out_fm", [D, TOK], F32, kind="ExternalOutput").ap()

        self.q_in = DT("q_in", [RANK_SH], BF16, kind="Internal")
        self.q1_out = DT("q1_out", [NCORES * W1_SHARD], BF16, kind="Internal")
        self.q3_out = DT("q3_out", [NCORES * W1_SHARD], BF16, kind="Internal")
        self.q2_out = DT("q2_out", [NCORES * W2_SHARD], BF16, kind="Internal")
        self.m_in = DT("m_in", [1, 4], F32, kind="Internal")
        self.m_out = DT("m_out", [1, 4], F32, kind="Internal")
        self.hl_in = DT("hl_in", [CONV, 3], BF16, kind="Internal")
        self.hl_out = DT("hl_out", [2 * CONV, 3], BF16, kind="Internal")
        self.s_in = DT("s_in", [GTS, INTER], F32, kind="Internal")
        self.s_out = DT("s_out", [2 * GTS, INTER], F32, kind="Internal")
        self.z_spill = DT("z_spill", [INTER, TOK], BF16, kind="Internal")
        self.rowb = DT("rowb", [192, TOK], F32, kind="Internal")
        self.ipw_bf = DT("ipw_bf", [D, PROJ], BF16, kind="Internal")
        self.opw_bf = DT("opw_bf", [INTER, D], BF16, kind="Internal")
        self.pc_in = DT("pc_in", [256 * PROJ + 512 * D], BF16, kind="Internal")

    def SB(self, nm, sh, dt=F32):
        return self.ctx.enter_context(self.nc.sbuf_tensor(nm, sh, dt))

    def SEM(self, nm, step=1):
        return Sem(self.ctx.enter_context(self.nc.semaphore(nm)), step)

    def build(self):
        self.ctx = contextlib.ExitStack()
        with self.ctx:
            self.alloc_persistent()
            self.p_consts()
            self.p_quant()
            self.p_ffn()
            if STAGE >= 2:
                with contextlib.ExitStack() as mar:
                    A = lambda nm, sh, dt=F32: mar.enter_context(self.nc.sbuf_tensor(nm, sh, dt))
                    self.xbc = A("xbc", [128, CCH * XW], BF16)
                    self.dtsp = A("dtsp", [HEADS, TOK])
                    self.p_proj_conv()
                    if STAGE >= 3:
                        with contextlib.ExitStack() as sar:
                            self.state = sar.enter_context(
                                self.nc.sbuf_tensor("state", [128, INTER], BF16))
                            self.p_scan()
                            self.p_carry2()
                        self.p_gate_out()
            if STAGE >= 4:
                self.p_ffn()
            self.p_store()
        return self.nc

    def alloc_persistent(self):
        SB, SEM = self.SB, self.SEM
        self.resid = SB("resid", [128, DC * TOK])
        self.ident_b = SB("ident_b", [128, 128], BF16)
        self.ident_f = SB("ident_f", [128, 128], F32)
        self.ones_b = SB("ones_b", [128, 128], BF16)
        self.ones1 = SB("ones1", [1, 128], F32)
        self.ones_f = SB("ones_f", [128, 128], F32)
        self.epscol = SB("epscol", [128, 1], F32)
        self.convw_s = SB("convw_s", [128, CCH * 4])
        self.convb_s = SB("convb_s", [128, CCH])
        self.acol = SB("acol", [HEADS, 1])
        self.dtb_s = SB("dtb_s", [HEADS, 1])
        self.dtbn_s = SB("dtbn_s", [HEADS, 1])
        self.drep_s = SB("drep_s", [128, IC])
        self.nw_s = SB("nw_s", [128, IC])
        self.flag_s = SB("flag_s", [128, 1])
        self.qcols = SB("qcols", [128, 12])
        self.meansn = SB("meansn", [128, 4])
        self.rows = SB("rows", [1, 6 * TH])
        self.rows2 = SB("rows2", [1, 16])
        self.zeros64 = SB("zeros64", [HEADS, 128])

        self.S_PE = SEM("S_PE")
        self.S_DV = SEM("S_DV")
        self.S_AC = SEM("S_AC")
        self.S_GP = SEM("S_GP")
        self.S_CC = SEM("S_CC")
        self.Q_SY = Dma([SEM("sy%d" % i, 16) for i in range(3)])
        self.Q_SC = Dma([SEM("sc%d" % i, 16) for i in range(2)])
        self.Q_GP = Dma([SEM("gp%d" % i, 16) for i in range(2)])
        self.Q_ST = Dma([SEM("qst%d" % i, 16) for i in range(2)])

    ROWMAP = {0: 0, 1: 1, 2: 2, 3: 3, 4: 0, 5: 5, 6: 1, 7: 2, 8: 3, 9: 0,
              10: 1, 11: 4, 12: 0, 13: 2, 14: 3, 15: 0, 16: 2, 17: 1}

    def row(self, k, w=TH):
        k = self.ROWMAP[k]
        return self.rows[0:1, k * TH:k * TH + w]

    def drain_all(self, blk, extra=()):
        s = self

        @blk.gpsimd
        def _(g):
            for q in (s.Q_SY, s.Q_SC, s.Q_GP, s.Q_ST) + tuple(extra):
                q.drain(g)

    def p_consts(self):
        s = self
        with s.nc.Block() as blk:
            @blk.sync
            def _(sy):
                src = AP(s.x_in.tensor, 0, [[TOK, 128], [TOK * 128, DC], [1, TOK]])
                s.Q_SY.issue(sy, s.resid.ap().rearrange("p (c t) -> p c t", c=DC), src)
                s.Q_SY.issue(sy, s.convw_s.ap().rearrange("p (c k) -> p c k", c=CCH),
                             AP(s.convw_in.tensor, 0, [[4, 128], [4 * 128, CCH], [1, 4]]))
                s.Q_SY.issue(sy, s.convb_s.ap().unsqueeze(2),
                             AP(s.convb_in.tensor, 0, [[1, 128], [128, CCH], [1, 1]]), allow_slow_non_contiguous=True)
                s.Q_SY.issue(sy, s.dtb_s[:], s.dtb_in)
                s.Q_SY.issue(sy, s.drep_s.ap().unsqueeze(2),
                             AP(s.drep_in.tensor, 0, [[1, 128], [128, IC], [1, 1]]), allow_slow_non_contiguous=True)
                s.Q_SY.issue(sy, s.nw_s.ap().unsqueeze(2),
                             AP(s.nw_in.tensor, 0, [[1, 128], [128, IC], [1, 1]]), allow_slow_non_contiguous=True)
                s.Q_SY.issue(sy, s.flag_s[:], s.flag_in)

            @blk.scalar
            def _(sc):
                s.Q_SC.issue(sc, s.acol[:], s.alog_in)
                s.Q_SC.drain(sc)
                s.S_AC.inc(sc.activation(s.acol[:], s.acol[:], AF.Exp))

            @blk.gpsimd
            def _(g):
                g.memset(s.ones_b[:], 1.0)
                g.memset(s.ones1[:], 1.0)
                g.memset(s.ones_f[:], 1.0)
                g.memset(s.epscol[:], EPS)
                g.memset(s.zeros64[:], 0.0)
                s.S_GP.inc(g.memset(s.ident_f[:], 1.0))
                g.wait_ge(s.S_GP.h, s.S_GP.n)
                s.S_GP.inc(g.affine_select(s.ident_b[:], s.ones_b[:], [[1, 128]],
                                           OP.is_equal, 0.0, base=0, channel_multiplier=-1))
                s.S_GP.inc(g.affine_select(s.ident_f[:], s.ident_f[:], [[1, 128]],
                                           OP.is_equal, 0.0, base=0, channel_multiplier=-1))

            @blk.vector
            def _(v):
                s.S_AC.wait(v)
                s.S_DV.inc(v.tensor_scalar_mul(s.acol[:], s.acol[:], -1.0))
                s.Q_SY.drain(v)
                v.wait_ge(s.S_DV.h, s.S_DV.n)
                s.S_DV.inc(v.tensor_scalar_mul(s.dtbn_s[:], s.dtb_s[:], -1.0))
        self.gp_idents = s.S_GP.n

    def p_quant(self):
        s = self
        nc = s.nc
        w13 = ([("w1", i) for i in range(HC // 2)] + [("w3", i) for i in range(HC // 2)] +
               [("w1", i) for i in range(HC // 2, HC)] + [("w3", i) for i in range(HC // 2, HC)])

        def w13_src(mat, i):
            t = s.w1s_in.tensor if mat == "w1" else s.w3s_in.tensor
            return AP(t, i * 128 * 256, [[256, 128], [1, 256]])

        def w2_src(r, hhalf):
            return AP(s.w2s_in.tensor, r * 128 * HID + hhalf * (HID // 2),
                      [[HID, 128], [1, HID // 2]])

        with nc.Block() as blk, contextlib.ExitStack() as ar:
            A = lambda nm, sh, dt=F32: ar.enter_context(nc.sbuf_tensor(nm, sh, dt))
            qp = A("qp1b", [128, 2 * 5632])
            qsum = A("qsum", [128, 4])
            qred = A("qred", [128, 4])
            ps_m = ar.enter_context(nc.psum_tensor("ps_m", [128, 512], F32))
            loads = [(m, h, AP(t, h * 5632, [[11264, 128], [1, 5632]]), 5632)
                     for m, t in (("w1", s.w1s_in.tensor), ("w3", s.w3s_in.tensor),
                                  ("w2", s.w2s_in.tensor))
                     for h in range(2)]
            col = {"w1": 0, "w3": 1, "w2": 2}
            ldm, dvm = {}, {}

            @blk.gpsimd
            def _(g):
                s.S_GP.inc(g.memset(qsum[:], 0.0))
            gp_zero = s.S_GP.n

            for k, (mat, i, src, w) in enumerate(loads):
                @blk.sync
                def _(sy, k=k, src=src, w=w):
                    if k - 2 in dvm:
                        sy.wait_ge(s.S_DV.h, dvm[k - 2])
                    _, n = s.Q_SY.issue(sy, qp[:, (k % 2) * 5632:(k % 2) * 5632 + w], src)
                    ldm[k] = (s.Q_SY.sems[(s.Q_SY.i - 1) % 3], n)

                @blk.vector
                def _(v, k=k, mat=mat, w=w):
                    sem, n = ldm[k]
                    v.wait_ge(sem.h, n)
                    if k == 0:
                        v.wait_ge(s.S_GP.h, gp_zero)
                    m = col[mat]
                    v.wait_ge(s.S_DV.h, s.S_DV.n)
                    s.S_DV.inc(v.tensor_reduce(qred[:, m:m + 1],
                                               qp[:, (k % 2) * 5632:(k % 2) * 5632 + w],
                                               AX.X, OP.add, apply_absolute_value=True))
                    v.wait_ge(s.S_DV.h, s.S_DV.n)
                    s.S_DV.inc(v.tensor_tensor(qsum[:, m:m + 1], qsum[:, m:m + 1],
                                               qred[:, m:m + 1], OP.add))
                    dvm[k] = s.S_DV.n

            @blk.tensor
            def _(t):
                t.wait_ge(s.S_DV.h, s.S_DV.n)
                s.S_PE.inc(t.matmul(ps_m[:, 0:4], s.ones_f[:], qsum[:], start=True, stop=True))

            @blk.vector
            def _(v):
                s.S_PE.wait(v)
                s.S_DV.inc(v.tensor_copy(qred[0:1, 0:4], ps_m[0:1, 0:4]))

            @blk.gpsimd
            def _(g):
                g.wait_ge(s.S_DV.h, s.S_DV.n)
                s.Q_GP.issue(g, AP(s.m_in, 0, [[4, 1], [1, 4]]), qred[0:1, 0:4])
                s.Q_GP.drain(g)
                s.S_CC.inc(g.collective_compute(
                    "AllReduce", OP.add, replica_groups=[list(range(NCORES))],
                    ins=[s.m_in.ap().opt()], outs=[s.m_out.ap().opt()]))
                g.wait_ge(s.S_CC.h, s.S_CC.n)
                s.Q_GP.issue(g, s.rows2[0:1, 0:4], AP(s.m_out, 0, [[4, 1], [1, 4]]))
                s.Q_GP.drain(g)
                s.S_GP.inc(g.engine_nop())

            @blk.vector
            def _(v):
                s.S_GP.wait(v)
                s.S_DV.inc(v.tensor_scalar(s.rows2[0:1, 4:8], s.rows2[0:1, 0:4],
                                           1.0 / NW_ELEMS, EPS, OP.mult, OP.max))
                v.wait_ge(s.S_DV.h, s.S_DV.n)
                s.S_DV.inc(v.tensor_scalar_mul(s.rows2[0:1, 8:12], s.rows2[0:1, 4:8], 0.5))
                s.S_DV.inc(v.tensor_scalar_mul(s.rows2[0:1, 12:16], s.rows2[0:1, 4:8], -0.5))

            @blk.tensor
            def _(t):
                t.wait_ge(s.S_DV.h, s.S_DV.n)
                s.S_PE.inc(t.matmul(ps_m[:, 0:12], s.ones1[:], s.rows2[0:1, 4:16],
                                    start=True, stop=True))

            @blk.vector
            def _(v):
                s.S_PE.wait(v)
                s.S_DV.inc(v.tensor_copy(s.qcols[:], ps_m[:, 0:12]))
                v.wait_ge(s.S_DV.h, s.S_DV.n)
                s.S_DV.inc(v.tensor_scalar_mul(s.meansn[:, 0:3], s.qcols[:, 0:3], -1.0))
                s.S_DV.inc(v.tensor_scalar_mul(s.meansn[:, 3:4], s.qcols[:, 2:3], 0.5))

            pcf = ar.enter_context(nc.sbuf_tensor("pcf", [128, 2 * 4256], F32))
            pcb = ar.enter_context(nc.sbuf_tensor("pcb", [128, 2 * 4256], BF16))
            QPC = Dma([s.SEM("qpc%d" % i, 16) for i in range(2)])
            pieces_pc = []
            for dc in range(2):
                for pc in range(2):
                    off = dc * 128 * PROJ + pc * 4256
                    pieces_pc.append((s.ipw_in.tensor, s.pc_in, off, off, PROJ, 4256))
            for k2 in range(2):
                off = k2 * 128 * 4096
                pieces_pc.append((s.opw_in.tensor, s.pc_in, off, 256 * PROJ + off,
                                  4096, 4096))
            pcm = {}
            for k, (srcT, dstT, off, doff, rs, w) in enumerate(pieces_pc):
                pp = k % 2

                @blk.sync
                def _(sy, k=k, srcT=srcT, off=off, rs=rs, w=w, pp=pp):
                    if ("pc_dv", k - 2) in pcm:
                        sy.wait_ge(s.S_DV.h, pcm[("pc_dv", k - 2)])
                    _, n = QPC.issue(sy, pcf.ap()[:, pp * 4256:pp * 4256 + w],
                                     AP(srcT, off, [[rs, 128], [1, w]]))
                    pcm[("pc_ld", k)] = (QPC.sems[(QPC.i - 1) % 2], n)

                @blk.vector
                def _(v, k=k, w=w, pp=pp):
                    sem, n = pcm[("pc_ld", k)]
                    v.wait_ge(sem.h, n)
                    if ("pc_st", k - 2) in pcm:
                        sem2, n2 = pcm[("pc_st", k - 2)]
                        v.wait_ge(sem2.h, n2)
                    v.wait_ge(s.S_DV.h, s.S_DV.n)
                    s.S_DV.inc(v.tensor_copy(pcb[:, pp * 4256:pp * 4256 + w],
                                             pcf[:, pp * 4256:pp * 4256 + w]))
                    pcm[("pc_dv", k)] = s.S_DV.n

                @blk.scalar
                def _(sc, k=k, dstT=dstT, doff=doff, rs=rs, w=w, pp=pp):
                    sc.wait_ge(s.S_DV.h, pcm[("pc_dv", k)])
                    _, n = s.Q_ST.issue(sc, AP(dstT, doff, [[rs, 128], [1, w]]),
                                        pcb.ap()[:, pp * 4256:pp * 4256 + w])
                    pcm[("pc_st", k)] = (s.Q_ST.sems[(s.Q_ST.i - 1) % 2], n)

            self.drain_all(blk, extra=(QPC,))

        with nc.Block() as blk, contextlib.ExitStack() as ar:
            A = lambda nm, sh, dt=F32: ar.enter_context(nc.sbuf_tensor(nm, sh, dt))
            qp = A("qp2", [128, 2 * 256])
            qt = A("qt2", [128, 2 * 256], BF16)
            qa = A("qa2", [128, 256], BF16)
            qasm = A("qasm", [128, 2 * 512], BF16)
            ps_q = ar.enter_context(nc.psum_tensor("ps_q", [128, 1024], BF16))
            ldm, dvm, pem, evm, stm = {}, {}, {}, {}, {}

            for k, (mat, i) in enumerate(w13):
                pp = k % 2

                @blk.sync
                def _(sy, k=k, mat=mat, i=i, pp=pp):
                    if k - 2 in dvm:
                        sy.wait_ge(s.S_DV.h, dvm[k - 2])
                    _, n = s.Q_SY.issue(sy, qp[:, pp * 256:pp * 256 + 256], w13_src(mat, i))
                    ldm[k] = (s.Q_SY.sems[(s.Q_SY.i - 1) % 3], n)

                @blk.vector
                def _(v, k=k, mat=mat, pp=pp):
                    sem, n = ldm[k]
                    v.wait_ge(sem.h, n)
                    m = {"w1": 0, "w3": 1}[mat]
                    if k - 1 in pem:
                        v.wait_ge(s.S_PE.h, pem[k - 1])
                    v.wait_ge(s.S_DV.h, s.S_DV.n)
                    s.S_DV.inc(v.tensor_single_scalar(qa[:], qp[:, pp * 256:pp * 256 + 256],
                                                      s.qcols[:, 4 + m:5 + m], OP.is_gt))
                    v.wait_ge(s.S_DV.h, s.S_DV.n)
                    s.S_DV.inc(v.scalar_tensor_tensor(qt[:, pp * 256:pp * 256 + 256],
                                                      qp[:, pp * 256:pp * 256 + 256],
                                                      s.qcols[:, 8 + m:9 + m], qa[:],
                                                      OP.is_lt, OP.subtract))
                    dvm[k] = s.S_DV.n

                @blk.tensor
                def _(t, k=k, i=i, pp=pp):
                    t.wait_ge(s.S_DV.h, dvm[k])
                    if k == 0:
                        t.wait_ge(s.S_GP.h, self.gp_idents)
                    if i % 2 == 0 and k - 1 in evm:
                        t.wait_ge(s.S_DV.h, evm[k - 1])
                    for dcl in range(2):
                        mm = t.transpose(ps_q[:, dcl * 256 + (i % 2) * 128:
                                              dcl * 256 + (i % 2) * 128 + 128],
                                         qt[:, pp * 256 + dcl * 128:pp * 256 + dcl * 128 + 128],
                                         s.ident_b[:])
                    pem[k] = s.S_PE.inc(mm)

                if i % 2 == 1:
                    @blk.vector
                    def _(v, k=k, i=i):
                        v.wait_ge(s.S_PE.h, pem[k])
                        hp = i // 2
                        half = hp % 2
                        if ("st", half) in stm:
                            sem, n = stm[("st", half)]
                            v.wait_ge(sem.h, n)
                        v.wait_ge(s.S_DV.h, s.S_DV.n)
                        s.S_DV.inc(v.tensor_copy(qasm[:, half * 512:half * 512 + 512], ps_q[:, 0:512]))
                        evm[k] = s.S_DV.n

                    @blk.scalar
                    def _(sc, k=k, mat=mat, i=i):
                        sc.wait_ge(s.S_DV.h, evm[k])
                        hp = i // 2
                        half = hp % 2
                        base = (W1_OFF if mat == "w1" else W3_OFF) + hp * 2 * BLK
                        dst = AP(s.q_in, base, [[512, 128], [256, 2], [1, 256]])
                        _, n = s.Q_ST.issue(sc, dst,
                                            qasm.ap()[:, half * 512:half * 512 + 512]
                                            .rearrange("p (a b) -> p a b", a=2))
                        stm[("st", half)] = (s.Q_ST.sems[(s.Q_ST.i - 1) % 2], n)

                seg = {("w1", HC // 2 - 1): ("q1", 0), ("w3", HC // 2 - 1): ("q3", 0),
                       ("w1", HC - 1): ("q1", 1), ("w3", HC - 1): ("q3", 1)}
                if (mat, i) in seg:
                    which, gb = seg[(mat, i)]

                    @blk.gpsimd
                    def _(g, which=which, gb=gb, mat=mat):
                        for q in s.Q_ST.sems:
                            q.wait(g)
                        src_off = (W1_OFF if mat == "w1" else W3_OFF) + gb * HWH
                        outt = s.q1_out if which == "q1" else s.q3_out
                        s.S_CC.inc(g.collective_compute(
                            "AllGather", OP.bypass, replica_groups=[list(range(NCORES))],
                            ins=[AP(s.q_in, src_off, [[1, HWH]])],
                            outs=[AP(outt, gb * NCORES * HWH, [[1, NCORES * HWH]]).opt()]))
                        setattr(s, "cc_%s%s" % (which, "ab"[gb]), s.S_CC.n)

            self.drain_all(blk)

        with nc.Block() as blk, contextlib.ExitStack() as ar:
            A = lambda nm, sh, dt=F32: ar.enter_context(nc.sbuf_tensor(nm, sh, dt))
            qp = A("qp3", [128, 2816])
            qt0 = A("qt30", [128, 2816], BF16)
            qt1 = A("qt31", [128, 2816], BF16)
            qa = A("qa3", [128, 2816], BF16)
            qasm = A("qasm3", [128, 2 * 256], BF16)
            ps_q = ar.enter_context(nc.psum_tensor("ps_q3", [128, 2048], BF16))
            ldm, dvm, pem, evm, stm = {}, {}, {}, {}, {}
            for hhalf in range(2):
                for r in range(2):
                    @blk.sync
                    def _(sy, r=r, hhalf=hhalf, dvn=None):
                        sy.wait_ge(s.S_DV.h, s.S_DV.n) if s.S_DV.n else None
                        _, n = s.Q_SY.issue(sy, qp[:], w2_src(r, hhalf))
                        ldm[(hhalf, r)] = (s.Q_SY.sems[(s.Q_SY.i - 1) % 3], n)

                    @blk.vector
                    def _(v, r=r, hhalf=hhalf, pen=None):
                        sem, n = ldm[(hhalf, r)]
                        v.wait_ge(sem.h, n)
                        if s.S_PE.n:
                            v.wait_ge(s.S_PE.h, s.S_PE.n)
                        v.wait_ge(s.S_DV.h, s.S_DV.n)
                        s.S_DV.inc(v.tensor_single_scalar(qa[:], qp[:],
                                                          s.qcols[:, 6:7], OP.is_gt))
                        v.wait_ge(s.S_DV.h, s.S_DV.n)
                        qt = qt0 if r == 0 else qt1
                        s.S_DV.inc(v.scalar_tensor_tensor(qt[:], qp[:], s.qcols[:, 10:11],
                                                          qa[:], OP.is_lt, OP.subtract))
                        dvm[(hhalf, r)] = s.S_DV.n

                for sl in range(HP):
                    pp = sl % 2

                    @blk.tensor
                    def _(t, sl=sl, pp=pp, hhalf=hhalf):
                        t.wait_ge(s.S_DV.h, dvm[(hhalf, 1)])
                        if (hhalf, sl - 2) in evm:
                            t.wait_ge(s.S_DV.h, evm[(hhalf, sl - 2)])
                        elif hhalf and (hhalf - 1, HP - 2 + sl) in evm:
                            t.wait_ge(s.S_DV.h, evm[(hhalf - 1, HP - 2 + sl)])
                        mm = t.transpose(ps_q[:, pp * 1024:pp * 1024 + 128],
                                         qt0[:, sl * 128:sl * 128 + 128], s.ident_b[:])
                        mm = t.transpose(ps_q[:, pp * 1024 + 128:pp * 1024 + 256],
                                         qt1[:, sl * 128:sl * 128 + 128], s.ident_b[:])
                        pem[(hhalf, sl)] = s.S_PE.inc(mm)

                    @blk.vector
                    def _(v, sl=sl, pp=pp, hhalf=hhalf):
                        v.wait_ge(s.S_PE.h, pem[(hhalf, sl)])
                        if ("st", pp) in stm:
                            sem, n = stm[("st", pp)]
                            v.wait_ge(sem.h, n)
                        v.wait_ge(s.S_DV.h, s.S_DV.n)
                        s.S_DV.inc(v.tensor_copy(qasm[:, pp * 256:pp * 256 + 256],
                                                 ps_q[:, pp * 1024:pp * 1024 + 256]))
                        evm[(hhalf, sl)] = s.S_DV.n

                    @blk.scalar
                    def _(sc, sl=sl, pp=pp, hhalf=hhalf):
                        sc.wait_ge(s.S_DV.h, evm[(hhalf, sl)])
                        hc = hhalf * HP + sl
                        dst = AP(s.q_in, W2_OFF + (hc // 2) * 2 * BLK + (hc % 2) * 256,
                                 [[512, 128], [1, 256]])
                        _, n = s.Q_ST.issue(sc, dst, qasm[:, pp * 256:pp * 256 + 256])
                        stm[("st", pp)] = (s.Q_ST.sems[(s.Q_ST.i - 1) % 2], n)

            @blk.gpsimd
            def _(g):
                for q in s.Q_ST.sems:
                    q.wait(g)
                s.S_CC.inc(g.collective_compute(
                    "AllGather", OP.bypass, replica_groups=[list(range(NCORES))],
                    ins=[AP(s.q_in, W2_OFF, [[1, W2_SHARD]])],
                    outs=[s.q2_out.ap().opt()]))
                s.cc_w2 = s.S_CC.n
                s.S_CC.inc(g.collective_compute(
                    "AllGather", OP.bypass, replica_groups=[list(range(NCORES))],
                    ins=[AP(s.pc_in, 0, [[1, 256 * PROJ]])],
                    outs=[AP(s.ipw_bf, 0, [[1, D * PROJ]]).opt()]))
                s.cc_ipw = s.S_CC.n

            self.drain_all(blk)

    def p_ffn(self):
        s = self
        nc = s.nc
        with nc.Block() as blk, contextlib.ExitStack() as ar:
            A = lambda nm, sh, dt=F32: ar.enter_context(nc.sbuf_tensor(nm, sh, dt))
            P_ = lambda nm, sh, dt=F32: ar.enter_context(nc.psum_tensor(nm, sh, dt))
            nm0 = "f%d" % s.S_PE.n
            xbf = A(nm0 + "xbf", [128, 2 * TH], BF16)
            x2bf = A(nm0 + "x2bf", [128, 2 * TH], BF16)
            xct = A(nm0 + "xct", [128, 2 * TH])
            xint = A(nm0 + "xint", [128, DC * TH], BF16)
            runmax = A(nm0 + "runmax", [128, TH])
            runmin = A(nm0 + "runmin", [128, TH])
            hmem = A(nm0 + "hmem", [128, HC * TH], BF16)
            tmpq = A(nm0 + "tmpq", [128, 2 * TH])
            t2b = A(nm0 + "t2b", [128, 2 * TH])
            colsb = A(nm0 + "colsb", [128, 4])
            combsb = A(nm0 + "combsb", [128, TH])
            invsqsb = A(nm0 + "invsqsb", [128, TH])
            shsb = A(nm0 + "shsb", [128, TH])
            wslab = A(nm0 + "wslab", [128, 2 * 2 * 4096], BF16)
            pb = [P_(nm0 + "pb%d" % i, [128, 512]) for i in range(6)]

            QW = Dma([s.SEM(nm0 + "qw%d" % i, 16) for i in range(3)])

            for th in range(2):
                T0 = th * TH
                mk = {}

                for dc in range(DC):
                    pp = dc % 2

                    @blk.scalar
                    def _(sc, dc=dc, pp=pp):
                        if ("pe_st", dc - 2) in mk:
                            sc.wait_ge(s.S_PE.h, mk[("pe_st", dc - 2)])
                        s.S_AC.inc(sc.activation(x2bf[:, pp * TH:pp * TH + TH],
                                                 s.resid[:, dc * TOK + T0:dc * TOK + T0 + TH],
                                                 AF.Square))
                        mk[("ac_sq", dc)] = s.S_AC.n

                    @blk.vector
                    def _(v, dc=dc, pp=pp):
                        if ("pe_st", dc - 2) in mk:
                            v.wait_ge(s.S_PE.h, mk[("pe_st", dc - 2)])
                        v.wait_ge(s.S_DV.h, s.S_DV.n)
                        s.S_DV.inc(v.tensor_copy(xbf[:, pp * TH:pp * TH + TH],
                                                 s.resid[:, dc * TOK + T0:dc * TOK + T0 + TH]))
                        mk[("dv_cp", dc)] = s.S_DV.n

                    @blk.tensor
                    def _(t, dc=dc, pp=pp):
                        if dc == 0:
                            t.wait_ge(s.S_DV.h, s.S_DV.n)
                            t.wait_ge(s.S_GP.h, s.S_GP.n)
                        t.wait_ge(s.S_DV.h, mk[("dv_cp", dc)])
                        t.wait_ge(s.S_AC.h, mk[("ac_sq", dc)])
                        t.matmul(pb[0][:], s.ones_b[:], xbf[:, pp * TH:pp * TH + TH],
                                 start=(dc == 0), stop=(dc == DC - 1))
                        mm = t.matmul(pb[1][:], s.ones_b[:], x2bf[:, pp * TH:pp * TH + TH],
                                      start=(dc == 0), stop=(dc == DC - 1))
                        mk[("pe_st", dc)] = s.S_PE.inc(mm)

                @blk.vector
                def _(v):
                    v.wait_ge(s.S_PE.h, mk[("pe_st", DC - 1)])
                    v.wait_ge(s.S_DV.h, s.S_DV.n)
                    s.S_DV.inc(v.tensor_scalar_mul(s.row(0), pb[0][0:1, :], 1.0 / D))
                    s.S_DV.inc(v.tensor_scalar_mul(s.row(1), pb[1][0:1, :], 1.0 / D))
                    mk["r01"] = s.S_DV.n

                @blk.scalar
                def _(sc):
                    sc.wait_ge(s.S_DV.h, mk["r01"])
                    s.S_AC.inc(sc.activation(s.row(2), s.row(0), AF.Square))
                    mk["r2"] = s.S_AC.n

                @blk.vector
                def _(v):
                    v.wait_ge(s.S_AC.h, mk["r2"])
                    v.wait_ge(s.S_DV.h, s.S_DV.n)
                    s.S_DV.inc(v.tensor_tensor(s.row(3), s.row(1), s.row(2), OP.subtract))
                    mk["r3"] = s.S_DV.n

                @blk.scalar
                def _(sc):
                    sc.wait_ge(s.S_DV.h, mk["r3"])
                    s.S_AC.inc(sc.activation(s.row(4), s.row(3), AF.Sqrt, bias=s.epscol[0:1, 0:1]))
                    mk["r4"] = s.S_AC.n

                @blk.vector
                def _(v):
                    v.wait_ge(s.S_AC.h, mk["r4"])
                    v.wait_ge(s.S_DV.h, s.S_DV.n)
                    s.S_DV.inc(v.reciprocal(s.row(5), s.row(4)))
                    mk["r5"] = s.S_DV.n

                if K_FFN_PARTS() < 2:
                    continue
                for dc in range(DC):
                    pp = dc % 2

                    @blk.vector
                    def _(v, dc=dc, pp=pp):
                        if dc == 0:
                            v.wait_ge(s.S_DV.h, mk["r5"])
                        if ("gp_fold", dc - 2) in mk:
                            v.wait_ge(s.S_DV.h, mk[("gp_fold", dc - 2)])
                        v.wait_ge(s.S_DV.h, s.S_DV.n)
                        s.S_DV.inc(v.scalar_tensor_tensor(
                            xct[:, pp * TH:pp * TH + TH], pb[0][:], -1.0 / D,
                            s.resid[:, dc * TOK + T0:dc * TOK + T0 + TH], OP.mult, OP.add))
                        mk[("dv_xc", dc)] = s.S_DV.n

                    @blk.vector
                    def _(v, dc=dc, pp=pp):
                        v.wait_ge(s.S_DV.h, s.S_DV.n)
                        src = xct[:, pp * TH:pp * TH + TH]
                        if dc == 0:
                            s.S_DV.inc(v.tensor_copy(runmax[:], src))
                            s.S_DV.inc(v.tensor_copy(runmin[:], src))
                        else:
                            s.S_DV.inc(v.tensor_tensor(runmax[:], runmax[:], src, OP.max))
                            s.S_DV.inc(v.tensor_tensor(runmin[:], runmin[:], src, OP.min))
                        mk[("gp_fold", dc)] = s.S_DV.n

                @blk.vector
                def _(v):
                    v.wait_ge(s.S_DV.h, s.S_DV.n)
                    s.S_DV.inc(v.scalar_tensor_tensor(runmax[:], runmin[:], -1.0,
                                                      runmax[:], OP.mult, OP.max))
                    mk["xboth"] = s.S_DV.n

                @blk.tensor
                def _(t):
                    t.wait_ge(s.S_DV.h, mk["xboth"])
                    for i in range(4):
                        mm = t.transpose(pb[4][:, i * 128:i * 128 + 128],
                                         runmax[:, i * 128:i * 128 + 128], s.ident_f[:])
                    mk["peTx"] = s.S_PE.inc(mm)

                @blk.vector
                def _(v):
                    v.wait_ge(s.S_PE.h, mk["peTx"])
                    for i in range(4):
                        v.wait_ge(s.S_DV.h, s.S_DV.n)
                        s.S_DV.inc(v.tensor_reduce(colsb[:, i:i + 1],
                                                   pb[4][:, i * 128:i * 128 + 128],
                                                   AX.X, OP.max))
                    mk["dvRx"] = s.S_DV.n

                @blk.tensor
                def _(t):
                    t.wait_ge(s.S_DV.h, mk["dvRx"])
                    for i in range(4):
                        mm = t.transpose(pb[5][0:1, i * 128:i * 128 + 128],
                                         colsb[:, i:i + 1], s.ident_f[:])
                    mk["peRx"] = s.S_PE.inc(mm)

                @blk.vector
                def _(v):
                    v.wait_ge(s.S_PE.h, mk["peRx"])
                    v.wait_ge(s.S_DV.h, s.S_DV.n)
                    s.S_DV.inc(v.tensor_tensor(s.row(7), pb[5][0:1, 0:TH], s.row(5), OP.mult))
                    v.wait_ge(s.S_DV.h, s.S_DV.n)
                    s.S_DV.inc(v.tensor_scalar_max(s.row(8), s.row(7), EPS))
                    v.wait_ge(s.S_DV.h, s.S_DV.n)
                    s.S_DV.inc(v.reciprocal(s.row(9), s.row(8)))
                    v.wait_ge(s.S_DV.h, s.S_DV.n)
                    s.S_DV.inc(v.tensor_scalar_mul(s.row(9), s.row(9), 127.0))
                    v.wait_ge(s.S_DV.h, s.S_DV.n)
                    s.S_DV.inc(v.tensor_tensor(s.row(10), s.row(5), s.row(9), OP.mult))
                    s.S_DV.inc(v.tensor_scalar_mul(s.row(11), s.row(8), 1.0 / 127.0))
                    mk["rq"] = s.S_DV.n

                @blk.tensor
                def _(t):
                    t.wait_ge(s.S_DV.h, mk["rq"])
                    mm = t.matmul(pb[1][:], s.ones1[:], s.row(10), start=True, stop=True)
                    mk["pe_comb"] = s.S_PE.inc(mm)

                @blk.vector
                def _(v):
                    v.wait_ge(s.S_PE.h, mk["pe_comb"])
                    v.wait_ge(s.S_DV.h, s.S_DV.n)
                    s.S_DV.inc(v.tensor_copy(combsb[:], pb[1][:]))
                    mk["dv_comb"] = s.S_DV.n

                @blk.tensor
                def _(t):
                    t.wait_ge(s.S_DV.h, mk["dv_comb"])
                    mm = t.matmul(pb[1][:], s.ones1[:], s.row(11), start=True, stop=True)
                    mk["pe_invsq"] = s.S_PE.inc(mm)

                @blk.vector
                def _(v):
                    v.wait_ge(s.S_PE.h, mk["pe_invsq"])
                    v.wait_ge(s.S_DV.h, s.S_DV.n)
                    s.S_DV.inc(v.tensor_copy(invsqsb[:], pb[1][:]))
                    mk["dv_invsq"] = s.S_DV.n

                if K_FFN_PARTS() < 3:
                    continue
                for dc in range(DC):
                    pp = dc % 2

                    @blk.vector
                    def _(v, dc=dc, pp=pp):
                        if dc == 0:
                            v.wait_ge(s.S_DV.h, mk["dv_comb"])
                        sl = xct[:, pp * TH:pp * TH + TH]
                        v.wait_ge(s.S_DV.h, s.S_DV.n)
                        s.S_DV.inc(v.scalar_tensor_tensor(
                            sl, pb[0][:], -1.0 / D,
                            s.resid[:, dc * TOK + T0:dc * TOK + T0 + TH], OP.mult, OP.add))
                        v.wait_ge(s.S_DV.h, s.S_DV.n)
                        s.S_DV.inc(v.tensor_tensor(sl, sl, combsb[:], OP.mult))
                        v.wait_ge(s.S_DV.h, s.S_DV.n)
                        s.S_DV.inc(v.tensor_scalar(xint[:, dc * TH:dc * TH + TH], sl,
                                                   MAGIC, MAGIC, OP.add, OP.subtract))
                        mk[("dv_xi", dc)] = s.S_DV.n

                if K_FFN_PARTS() < 4:
                    continue
                def slab_src(mat, hp):
                    t_ = s.q1_out if mat == "w1" else s.q3_out
                    ga, hpl = divmod(hp, HP // 2)
                    base = ga * NCORES * HWH + hpl * 2 * BLK
                    return AP(t_, base, [[512, 128], [HWH, 8], [1, 512]])

                for hp in range(HP):
                    sp = hp % 2

                    @blk.sync
                    def _(sy, hp=hp, sp=sp, pen=None):
                        if ("pe_b", 2 * (hp - 2) + 1) in mk:
                            sy.wait_ge(s.S_PE.h, mk[("pe_b", 2 * (hp - 2) + 1)])
                        if hp <= 1:
                            sy.wait_ge(s.S_PE.h, s.S_PE.n)
                        if hp == 0:
                            sy.wait_ge(s.S_CC.h, s.cc_q3a)
                        if hp == HP // 2:
                            sy.wait_ge(s.S_CC.h, s.cc_q3b)
                        marks = []
                        for mi, mat in enumerate(("w1", "w3")):
                            off = sp * 8192 + mi * 4096
                            dst = AP(wslab, off, [[16384, 128], [512, 8], [1, 512]])
                            _, n = QW.issue(sy, dst, slab_src(mat, hp))
                            marks.append((QW.sems[(QW.i - 1) % 3], n))
                        mk[("ld_w", hp)] = marks

                    for sub in range(2):
                        hc = 2 * hp + sub
                        bq = 2 + 2 * (hc % 2)

                        @blk.tensor
                        def _(t, hp=hp, sp=sp, sub=sub, hc=hc, bq=bq):
                            for sem, n in mk[("ld_w", hp)]:
                                t.wait_ge(sem.h, n)
                            if hc == 0:
                                t.wait_ge(s.S_DV.h, mk[("dv_xi", DC - 1)])
                            if ("ac_t2", hc - 2) in mk:
                                t.wait_ge(s.S_AC.h, mk[("ac_t2", hc - 2)])
                                t.wait_ge(s.S_DV.h, mk[("dv_h", hc - 2)])
                            for dc in range(DC):
                                t.matmul(pb[bq][:],
                                         wslab[:, sp * 8192 + dc * 256 + sub * 128:
                                               sp * 8192 + dc * 256 + sub * 128 + 128],
                                         xint[:, dc * TH:dc * TH + TH],
                                         start=(dc == 0), stop=(dc == DC - 1))
                            for dc in range(DC):
                                mm = t.matmul(pb[bq + 1][:],
                                              wslab[:, sp * 8192 + 4096 + dc * 256 + sub * 128:
                                                    sp * 8192 + 4096 + dc * 256 + sub * 128 + 128],
                                              xint[:, dc * TH:dc * TH + TH],
                                              start=(dc == 0), stop=(dc == DC - 1))
                            mk[("pe_b", hc)] = s.S_PE.inc(mm)

                        @blk.vector
                        def _(v, hc=hc, bq=bq):
                            v.wait_ge(s.S_PE.h, mk[("pe_b", hc)])
                            v.wait_ge(s.S_DV.h, s.S_DV.n)
                            s.S_DV.inc(v.scalar_tensor_tensor(
                                pb[bq][:], pb[bq][:], s.meansn[:, 0:1], invsqsb[:],
                                OP.mult, OP.mult))
                            mk[("dv_b1", hc)] = s.S_DV.n

                        @blk.scalar
                        def _(sc, hc=hc, bq=bq):
                            sc.wait_ge(s.S_DV.h, mk[("dv_b1", hc)])
                            pp = hc % 2
                            s.S_AC.inc(sc.activation(t2b[:, pp * TH:pp * TH + TH],
                                                     pb[bq][:], AF.Silu))
                            mk[("ac_t2", hc)] = s.S_AC.n

                        @blk.vector
                        def _(v, hc=hc, bq=bq):
                            v.wait_ge(s.S_AC.h, mk[("ac_t2", hc)])
                            pp = hc % 2
                            v.wait_ge(s.S_DV.h, s.S_DV.n)
                            s.S_DV.inc(v.tensor_tensor(hmem[:, hc * TH:hc * TH + TH],
                                                       t2b[:, pp * TH:pp * TH + TH],
                                                       pb[bq + 1][:], OP.mult))
                            mk[("dv_h", hc)] = s.S_DV.n

                        @blk.vector
                        def _(v, hc=hc):
                            v.wait_ge(s.S_DV.h, s.S_DV.n)
                            hsl = hmem[:, hc * TH:hc * TH + TH]
                            if hc == 0:
                                s.S_DV.inc(v.tensor_copy(runmax[:], hsl))
                                s.S_DV.inc(v.tensor_copy(runmin[:], hsl))
                            else:
                                s.S_DV.inc(v.tensor_tensor(runmax[:], runmax[:], hsl, OP.max))
                                s.S_DV.inc(v.tensor_tensor(runmin[:], runmin[:], hsl, OP.min))
                            mk[("gp_hf", hc)] = s.S_DV.n

                if K_FFN_PARTS() < 5:
                    continue
                @blk.vector
                def _(v):
                    v.wait_ge(s.S_DV.h, s.S_DV.n)
                    s.S_DV.inc(v.scalar_tensor_tensor(runmax[:], runmin[:], -1.0,
                                                      runmax[:], OP.mult, OP.max))
                    mk["hboth"] = s.S_DV.n

                @blk.tensor
                def _(t):
                    t.wait_ge(s.S_AC.h, mk[("ac_t2", HC - 1)])
                    t.wait_ge(s.S_DV.h, mk["hboth"])
                    for i in range(4):
                        mm = t.transpose(pb[4][:, i * 128:i * 128 + 128],
                                         runmax[:, i * 128:i * 128 + 128], s.ident_f[:])
                    mk["peTh"] = s.S_PE.inc(mm)

                @blk.vector
                def _(v):
                    v.wait_ge(s.S_PE.h, mk["peTh"])
                    for i in range(4):
                        v.wait_ge(s.S_DV.h, s.S_DV.n)
                        s.S_DV.inc(v.tensor_reduce(colsb[:, i:i + 1],
                                                   pb[4][:, i * 128:i * 128 + 128],
                                                   AX.X, OP.max))
                    mk["dvRh"] = s.S_DV.n

                @blk.tensor
                def _(t):
                    t.wait_ge(s.S_DV.h, mk["dvRh"])
                    for i in range(4):
                        mm = t.transpose(pb[5][0:1, i * 128:i * 128 + 128],
                                         colsb[:, i:i + 1], s.ident_f[:])
                    mk["peRh"] = s.S_PE.inc(mm)

                @blk.vector
                def _(v):
                    v.wait_ge(s.S_PE.h, mk["peRh"])
                    v.wait_ge(s.S_DV.h, s.S_DV.n)
                    s.S_DV.inc(v.tensor_scalar_mul(s.row(5), s.row(11), s.qcols[0:1, 1:2]))
                    v.wait_ge(s.S_DV.h, s.S_DV.n)
                    s.S_DV.inc(v.tensor_tensor(s.row(13), pb[5][0:1, 0:TH], s.row(5), OP.mult))
                    v.wait_ge(s.S_DV.h, s.S_DV.n)
                    s.S_DV.inc(v.tensor_scalar_max(s.row(14), s.row(13), EPS))
                    v.wait_ge(s.S_DV.h, s.S_DV.n)
                    s.S_DV.inc(v.reciprocal(s.row(15), s.row(14)))
                    v.wait_ge(s.S_DV.h, s.S_DV.n)
                    s.S_DV.inc(v.tensor_scalar_mul(s.row(15), s.row(15), 127.0))
                    v.wait_ge(s.S_DV.h, s.S_DV.n)
                    s.S_DV.inc(v.tensor_tensor(s.row(16), s.row(15), s.row(5), OP.mult))
                    s.S_DV.inc(v.tensor_scalar_mul(s.row(17), s.row(14), 1.0 / 127.0))
                    mk["rh"] = s.S_DV.n

                @blk.tensor
                def _(t):
                    t.wait_ge(s.S_DV.h, mk["rh"])
                    mm = t.matmul(pb[1][:], s.ones1[:], s.row(16), start=True, stop=True)
                    mk["pe_sh"] = s.S_PE.inc(mm)

                @blk.vector
                def _(v):
                    v.wait_ge(s.S_PE.h, mk["pe_sh"])
                    v.wait_ge(s.S_DV.h, s.S_DV.n)
                    s.S_DV.inc(v.tensor_copy(shsb[:], pb[1][:]))
                    mk["dv_sh"] = s.S_DV.n

                @blk.tensor
                def _(t):
                    t.wait_ge(s.S_DV.h, mk["dv_sh"])
                    mm = t.matmul(pb[1][:], s.ones1[:], s.row(17), start=True, stop=True)
                    mk["pe_ish"] = s.S_PE.inc(mm)

                @blk.vector
                def _(v):
                    v.wait_ge(s.S_PE.h, mk["pe_ish"])
                    v.wait_ge(s.S_DV.h, s.S_DV.n)
                    s.S_DV.inc(v.tensor_copy(invsqsb[:], pb[1][:]))
                    mk["dv_ish"] = s.S_DV.n

                if K_FFN_PARTS() < 6:
                    continue
                for hq in range(HC // 2):
                    @blk.vector
                    def _(v, hq=hq):
                        if hq == 0:
                            v.wait_ge(s.S_DV.h, mk["dv_sh"])
                        v.wait_ge(s.S_DV.h, s.S_DV.n)
                        shrep = shsb.ap().unsqueeze(1).broadcast_to([128, 2, TH])
                        s.S_DV.inc(v.tensor_tensor(
                            tmpq.ap().rearrange("p (a b) -> p a b", a=2),
                            hmem.ap()[:, hq * 2 * TH:hq * 2 * TH + 2 * TH]
                            .rearrange("p (a b) -> p a b", a=2), shrep, OP.mult))
                        v.wait_ge(s.S_DV.h, s.S_DV.n)
                        s.S_DV.inc(v.tensor_scalar(hmem[:, hq * 2 * TH:hq * 2 * TH + 2 * TH],
                                                   tmpq[:],
                                                   MAGIC, MAGIC, OP.add, OP.subtract))
                        mk[("dv_hi", 2 * hq)] = s.S_DV.n
                        mk[("dv_hi", 2 * hq + 1)] = s.S_DV.n

                if K_FFN_PARTS() < 7:
                    continue
                for dq in range(NCORES):
                    bk = 2 + 2 * (dq % 2)
                    for h2 in range(2):

                        @blk.sync
                        def _(sy, dq=dq, h2=h2):
                            if dq == 0:
                                sy.wait_ge(s.S_PE.h, mk[("pe_b", HC - 1)])
                                sy.wait_ge(s.S_CC.h, s.cc_w2)
                            if ("pe_w2", dq - 1, h2) in mk:
                                sy.wait_ge(s.S_PE.h, mk[("pe_w2", dq - 1, h2)])
                            src = AP(s.q2_out, dq * W2_SHARD + h2 * HP * BLK,
                                     [[512, 128], [2 * BLK, HP // 2], [1, 512]])
                            dst = AP(wslab, h2 * HP * 256,
                                     [[16384, 128], [512, HP // 2], [1, 512]])
                            _, n = QW.issue(sy, dst, src)
                            mk[("ld_w2", dq, h2)] = (QW.sems[(QW.i - 1) % 3], n)

                        @blk.tensor
                        def _(t, dq=dq, h2=h2, bk=bk):
                            sem, n = mk[("ld_w2", dq, h2)]
                            t.wait_ge(sem.h, n)
                            if dq == 0:
                                t.wait_ge(s.S_DV.h, mk[("dv_hi", HP - 1 if h2 == 0 else HC - 1)])
                            if h2 == 0 and ("dv_y2", 2 * (dq - 2) + 1) in mk:
                                t.wait_ge(s.S_DV.h, mk[("dv_y2", 2 * (dq - 2) + 1)])
                            for d2 in range(2):
                                for hl in range(HP):
                                    hcc = h2 * HP + hl
                                    mm = t.matmul(pb[bk + d2][:],
                                                  wslab[:, h2 * HP * 256 + hl * 256 + d2 * 128:
                                                        h2 * HP * 256 + hl * 256 + d2 * 128 + 128],
                                                  hmem[:, hcc * TH:hcc * TH + TH],
                                                  start=(hcc == 0), stop=(hcc == HC - 1))
                            mk[("pe_w2", dq, h2)] = s.S_PE.inc(mm)

                    for d2 in range(2):
                        dco = dq * 2 + d2

                        @blk.vector
                        def _(v, dq=dq, d2=d2, dco=dco, bk=bk):
                            v.wait_ge(s.S_PE.h, mk[("pe_w2", dq, 1)])
                            v.wait_ge(s.S_DV.h, s.S_DV.n)
                            s.S_DV.inc(v.scalar_tensor_tensor(
                                pb[bk + d2][:], pb[bk + d2][:], s.meansn[:, 3:4], invsqsb[:],
                                OP.mult, OP.mult))
                            v.wait_ge(s.S_DV.h, s.S_DV.n)
                            s.S_DV.inc(v.tensor_tensor(
                                s.resid[:, dco * TOK + T0:dco * TOK + T0 + TH],
                                s.resid[:, dco * TOK + T0:dco * TOK + T0 + TH],
                                pb[bk + d2][:], OP.add))
                            mk[("dv_y2", dco)] = s.S_DV.n

            self.drain_all(blk, extra=(QW,))

    def p_proj_conv(self):
        s = self
        nc = s.nc
        with nc.Block() as blk, contextlib.ExitStack() as ar:
            A = lambda nm, sh, dt=F32: ar.enter_context(nc.sbuf_tensor(nm, sh, dt))
            P_ = lambda nm, sh, dt=F32: ar.enter_context(nc.psum_tensor(nm, sh, dt))
            xln = A("xln", [128, DC * TOK], BF16)
            xbf = A("pxbf", [128, TH], BF16)
            x2bf = xbf
            xct = A("pxct", [128, 2 * TH])
            zt = A("pzt", [128, TOK], BF16)
            hstage = A("hstage", [128, CCH * 3], BF16)
            ipslab = A("ipslab", [128, 2 * DC * 256], BF16)
            acc2 = P_("pacc2", [128, 1024])
            accs = (xct, acc2)
            pj0 = P_("pj0", [128, 512])
            pj1 = P_("pj1", [128, 512])
            pot = [P_("pot%d" % i, [128, TOK]) for i in range(2)]
            QIP = Dma([s.SEM("qip%d" % i, 16) for i in range(2)])
            mk = {}

            for tt in range(2):
                T0 = tt * TH
                for dc in range(DC):
                    pp = dc % 2

                    @blk.vector
                    def _(v, dc=dc, T0=T0, tt=tt):
                        if ("pe_st2", tt, dc - 1) in mk:
                            v.wait_ge(s.S_PE.h, mk[("pe_st2", tt, dc - 1)])
                        v.wait_ge(s.S_DV.h, s.S_DV.n)
                        s.S_DV.inc(v.tensor_copy(xbf[:],
                                                 s.resid[:, dc * TOK + T0:dc * TOK + T0 + TH]))
                        mk[("dv_cp", tt, dc)] = s.S_DV.n

                    @blk.tensor
                    def _(t, dc=dc, tt=tt):
                        if dc == 0:
                            t.wait_ge(s.S_DV.h, s.S_DV.n)
                        t.wait_ge(s.S_DV.h, mk[("dv_cp", tt, dc)])
                        mm = t.matmul(pj0[:], s.ones_b[:], xbf[:],
                                      start=(dc == 0), stop=(dc == DC - 1))
                        mk[("pe_st1", tt, dc)] = s.S_PE.inc(mm)

                    @blk.scalar
                    def _(sc, dc=dc, T0=T0, tt=tt):
                        sc.wait_ge(s.S_PE.h, mk[("pe_st1", tt, dc)])
                        s.S_AC.inc(sc.activation(x2bf[:],
                                                 s.resid[:, dc * TOK + T0:dc * TOK + T0 + TH],
                                                 AF.Square))
                        mk[("ac_sq", tt, dc)] = s.S_AC.n

                    @blk.tensor
                    def _(t, dc=dc, tt=tt):
                        t.wait_ge(s.S_AC.h, mk[("ac_sq", tt, dc)])
                        mm = t.matmul(pj1[:], s.ones_b[:], x2bf[:],
                                      start=(dc == 0), stop=(dc == DC - 1))
                        mk[("pe_st2", tt, dc)] = s.S_PE.inc(mm)
                        mk[("pe_st", tt, dc)] = s.S_PE.n

                @blk.vector
                def _(v, tt=tt):
                    v.wait_ge(s.S_PE.h, mk[("pe_st", tt, DC - 1)])
                    v.wait_ge(s.S_DV.h, s.S_DV.n)
                    s.S_DV.inc(v.tensor_scalar_mul(s.row(0), pj0[0:1, :], 1.0 / D))
                    s.S_DV.inc(v.tensor_scalar_mul(s.row(1), pj1[0:1, :], 1.0 / D))
                    mk["r01"] = s.S_DV.n

                @blk.scalar
                def _(sc, tt=tt):
                    sc.wait_ge(s.S_DV.h, mk["r01"])
                    s.S_AC.inc(sc.activation(s.row(2), s.row(0), AF.Square))
                    mk["r2"] = s.S_AC.n

                @blk.vector
                def _(v, tt=tt):
                    v.wait_ge(s.S_AC.h, mk["r2"])
                    v.wait_ge(s.S_DV.h, s.S_DV.n)
                    s.S_DV.inc(v.tensor_tensor(s.row(3), s.row(1), s.row(2), OP.subtract))
                    mk["r3"] = s.S_DV.n

                @blk.scalar
                def _(sc, tt=tt):
                    sc.wait_ge(s.S_DV.h, mk["r3"])
                    s.S_AC.inc(sc.activation(s.row(4), s.row(3), AF.Sqrt, bias=s.epscol[0:1, 0:1]))
                    mk["r4"] = s.S_AC.n

                @blk.vector
                def _(v, tt=tt):
                    v.wait_ge(s.S_AC.h, mk["r4"])
                    v.wait_ge(s.S_DV.h, s.S_DV.n)
                    s.S_DV.inc(v.reciprocal(s.row(5), s.row(4)))
                    mk["r5"] = s.S_DV.n

                @blk.tensor
                def _(t, tt=tt):
                    t.wait_ge(s.S_DV.h, mk["r5"])
                    mm = t.matmul(pj1[:], s.ones1[:], s.row(5), start=True, stop=True)
                    mk[("pe_rs", tt)] = s.S_PE.inc(mm)

                for dc in range(DC):
                    pp = dc % 2

                    @blk.vector
                    def _(v, dc=dc, pp=pp, T0=T0, tt=tt):
                        if dc == 0:
                            v.wait_ge(s.S_PE.h, mk[("pe_rs", tt)])
                        v.wait_ge(s.S_DV.h, s.S_DV.n)
                        sl = xct[:, pp * TH:pp * TH + TH]
                        s.S_DV.inc(v.scalar_tensor_tensor(
                            sl, pj0[:], -1.0 / D,
                            s.resid[:, dc * TOK + T0:dc * TOK + T0 + TH], OP.mult, OP.add))
                        v.wait_ge(s.S_DV.h, s.S_DV.n)
                        s.S_DV.inc(v.tensor_tensor(xln[:, dc * TOK + T0:dc * TOK + T0 + TH],
                                                   sl, pj1[:], OP.mult))
                        mk[("dv_xl", tt, dc)] = s.S_DV.n

            for f in range(67):
                wf = 128 if f < 66 else 64
                pp = f % 2
                fp, sub = f // 2, f % 2
                wslab = 256 if fp < 33 else 64
                sp = fp % 2

                if sub == 0:
                    @blk.sync
                    def _(sy, f=f, fp=fp, wslab=wslab, sp=sp):
                        if f == 0:
                            sy.wait_ge(s.S_DV.h, mk[("dv_xl", 1, DC - 1)])
                            sy.wait_ge(s.S_CC.h, s.cc_ipw)
                        if ("pe_ip", 2 * (fp - 2) + 1) in mk:
                            sy.wait_ge(s.S_PE.h, mk[("pe_ip", 2 * (fp - 2) + 1)])
                        elif ("pe_ip", 2 * (fp - 2)) in mk:
                            sy.wait_ge(s.S_PE.h, mk[("pe_ip", 2 * (fp - 2))])
                        src = AP(s.ipw_bf, fp * 256,
                                 [[PROJ, 128], [PROJ * 128, DC], [1, wslab]])
                        dst = AP(ipslab, sp * DC * 256,
                                 [[2 * DC * 256, 128], [wslab, DC], [1, wslab]])
                        _, n = QIP.issue(sy, dst, src)
                        mk[("ld_ip", fp)] = (QIP.sems[(QIP.i - 1) % 2], n)

                @blk.tensor
                def _(t, f=f, wf=wf, pp=pp, fp=fp, sub=sub, wslab=wslab, sp=sp):
                    sem, n = mk[("ld_ip", fp)]
                    t.wait_ge(sem.h, n)
                    if ("ev_ip", f - 2) in mk:
                        eng, v = mk[("ev_ip", f - 2)]
                        t.wait_ge(eng, v)
                    for tt in range(2):
                        for dc in range(DC):
                            mm = t.matmul(pot[pp][0:wf, tt * TH:tt * TH + TH],
                                          ipslab[:, sp * DC * 256 + dc * wslab + sub * 128:
                                                sp * DC * 256 + dc * wslab + sub * 128 + wf],
                                          xln[:, dc * TOK + tt * TH:dc * TOK + tt * TH + TH],
                                          start=(dc == 0), stop=(dc == DC - 1))
                    mk[("pe_ip", f)] = s.S_PE.inc(mm)

                if f < IC:
                    @blk.scalar
                    def _(sc, f=f, pp=pp):
                        sc.wait_ge(s.S_PE.h, mk[("pe_ip", f)])
                        if ("st_z", f - 2) in mk:
                            sem, n = mk[("st_z", f - 2)]
                            sc.wait_ge(sem.h, n)
                        if ("st_z", f - 1) in mk:
                            sem, n = mk[("st_z", f - 1)]
                            sc.wait_ge(sem.h, n)
                        s.S_AC.inc(sc.activation(zt[:], pot[pp][:], AF.Silu, scale=0.25))
                        mk[("ev_ip", f)] = (s.S_AC.h, s.S_AC.n)

                    @blk.scalar
                    def _(sc, f=f, pp=pp):
                        eng, v = mk[("ev_ip", f)]
                        sc.wait_ge(eng, v)
                        dst = AP(s.z_spill, f * 128 * TOK, [[TOK, 128], [1, TOK]])
                        _, n = s.Q_ST.issue(sc, dst, zt[:])
                        mk[("st_z", f)] = (s.Q_ST.sems[(s.Q_ST.i - 1) % 2], n)
                elif f < 66:
                    @blk.vector
                    def _(v, f=f, pp=pp):
                        v.wait_ge(s.S_PE.h, mk[("pe_ip", f)])
                        v.wait_ge(s.S_DV.h, s.S_DV.n)
                        c = f - IC
                        s.S_DV.inc(v.tensor_scalar_mul(
                            s.xbc[:, c * XW + 3:c * XW + 3 + TOK], pot[pp][:], 0.5))
                        v.wait_ge(s.S_DV.h, s.S_DV.n)
                        s.S_DV.inc(v.tensor_copy(hstage[:, c * 3:c * 3 + 3],
                                                 s.xbc[:, c * XW + 1024:c * XW + 1027]))
                        mk[("ev_ip", f)] = (s.S_DV.h, s.S_DV.n)

                    @blk.vector
                    def _(v, f=f):
                        c = f - IC
                        base = c * XW
                        acc = accs[c % 2]
                        if ("ac_cvm", c - 2) in mk:
                            v.wait_ge(s.S_AC.h, mk[("ac_cvm", c - 2)])
                        v.wait_ge(s.S_DV.h, s.S_DV.n)
                        s.S_DV.inc(v.tensor_scalar_mul(acc[:, 0:1021],
                                                       s.xbc[:, base + 3:base + 3 + 1021],
                                                       s.convw_s[:, 4 * c:4 * c + 1]))
                        for k in range(1, 4):
                            v.wait_ge(s.S_DV.h, s.S_DV.n)
                            s.S_DV.inc(v.scalar_tensor_tensor(
                                acc[:, 0:1021],
                                s.xbc[:, base + 3 + k:base + 3 + k + 1021],
                                s.convw_s[:, 4 * c + k:4 * c + k + 1], acc[:, 0:1021],
                                OP.mult, OP.add))
                        mk[("dv_cvm", c)] = s.S_DV.n

                    @blk.scalar
                    def _(sc, f=f):
                        c = f - IC
                        acc = accs[c % 2]
                        sc.wait_ge(s.S_DV.h, mk[("dv_cvm", c)])
                        s.S_AC.inc(sc.activation(s.xbc[:, c * XW + 6:c * XW + 6 + 1021],
                                                 acc[:, 0:1021], AF.Silu,
                                                 bias=s.convb_s[:, c:c + 1]))
                        mk[("ac_cvm", c)] = s.S_AC.n
                else:
                    @blk.scalar
                    def _(sc, f=f, pp=pp):
                        sc.wait_ge(s.S_PE.h, mk[("pe_ip", f)])
                        s.S_AC.inc(sc.activation(s.dtsp[:, :], pot[pp][0:HEADS, :],
                                                 AF.Sigmoid, scale=-1.0,
                                                 bias=s.dtbn_s[:]))
                        sc.wait_ge(s.S_AC.h, s.S_AC.n)
                        s.S_AC.inc(sc.activation(s.dtsp[:, :], s.dtsp[:, :], AF.Ln))
                        mk[("ev_ip", f)] = (s.S_AC.h, s.S_AC.n)

            @blk.gpsimd
            def _(g):
                for f in range(IC, 67):
                    eng, v = mk[("ev_ip", f)]
                    g.wait_ge(eng, v)
                xv = s.xbc.ap().rearrange("p (c t) -> p c t", c=CCH)
                s.Q_GP.issue(g, AP(s.hl_in, 0, [[3, 128], [3 * 128, CCH], [1, 3]]),
                             hstage.ap().rearrange("p (c t) -> p c t", c=CCH))
                s.Q_GP.drain(g)
                s.S_CC.inc(g.collective_compute(
                    "AllGather", OP.bypass,
                    replica_groups=[[0, 1], [2, 3], [4, 5], [6, 7]],
                    ins=[s.hl_in.ap().opt()], outs=[s.hl_out.ap().opt()]))
                g.wait_ge(s.S_CC.h, s.S_CC.n)
                s.Q_GP.issue(g, xv[:, :, 0:3],
                             AP(s.hl_out, 0, [[3, 128], [3 * 128, CCH], [1, 3]]))
                s.Q_GP.drain(g)
                s.S_GP.inc(g.engine_nop())
            mk["halo"] = s.S_GP.n

            @blk.vector
            def _(v):
                v.wait_ge(s.S_GP.h, mk["halo"])
                v.wait_ge(s.S_DV.h, s.S_DV.n)
                xv = s.xbc.ap().rearrange("p (c t) -> p c t", c=CCH)
                s.S_DV.inc(v.tensor_scalar_mul(xv[:, :, 0:3], xv[:, :, 0:3], s.flag_s[:]))
                mk["halo_m"] = s.S_DV.n

            for c in range(CCH):
                @blk.vector
                def _(v, c=c):
                    if c == 0:
                        v.wait_ge(s.S_DV.h, mk["halo_m"])
                        v.wait_ge(s.S_AC.h, mk[("ac_cvm", CCH - 1)])
                    if ("ac_cv", c - 1) in mk:
                        v.wait_ge(s.S_AC.h, mk[("ac_cv", c - 1)])
                    v.wait_ge(s.S_DV.h, s.S_DV.n)
                    base = c * XW
                    acc = accs[0]
                    s.S_DV.inc(v.tensor_scalar_mul(acc[:, 0:3], s.xbc[:, base:base + 3],
                                                   s.convw_s[:, 4 * c:4 * c + 1]))
                    for k in range(1, 4):
                        v.wait_ge(s.S_DV.h, s.S_DV.n)
                        s.S_DV.inc(v.scalar_tensor_tensor(
                            acc[:, 0:3], s.xbc[:, base + k:base + k + 3],
                            s.convw_s[:, 4 * c + k:4 * c + k + 1], acc[:, 0:3],
                            OP.mult, OP.add))
                    mk[("dv_cv", c)] = s.S_DV.n

                @blk.scalar
                def _(sc, c=c):
                    acc = accs[0]
                    sc.wait_ge(s.S_DV.h, mk[("dv_cv", c)])
                    s.S_AC.inc(sc.activation(s.xbc[:, c * XW + 3:c * XW + 6],
                                             acc[:, 0:3], AF.Silu,
                                             bias=s.convb_s[:, c:c + 1]))
                    mk[("ac_cv", c)] = s.S_AC.n

            self.drain_all(blk, extra=(QIP,))

    def p_scan(self):
        s = self
        nc = s.nc
        with nc.Block() as blk, contextlib.ExitStack() as ar:
            A = lambda nm, sh, dt=F32: ar.enter_context(nc.sbuf_tensor(nm, sh, dt))
            P_ = lambda nm, sh, dt=F32: ar.enter_context(nc.psum_tensor(nm, sh, dt))
            dA = A("dA", [HEADS, TOK])
            Pall = A("Pall", [HEADS, TOK])
            Pp = A("Pp", [HEADS, TOK])
            tdmat = A("tdmat", [HEADS, 8])
            Imat = A("Imat", [HEADS, 8])
            dtT = A("dtT", [128, NJ * HEADS])
            dxo = A("dxo", [128, 2 * 512], BF16)
            dxw = A("dxw", [128, 512], BF16)
            Ebuf = A("Ebuf", [128, 1024])
            Sbuf = A("Sbuf", [128, 1024], BF16)
            CPb = A("CPb", [128, 1024], BF16)
            BT = A("BT", [128, 128], BF16)
            dAz = A("dAz", [HEADS, TOK])
            identrep = A("identrep", [128, 1024], BF16)
            fA = A("fA", [1, 1024])
            fP = A("fP", [1, 1024])
            bAW = P_("bAW", [128, 1024])
            bA = [bAW.ap()[:, 0:512], bAW.ap()[:, 512:1024]]
            bCB = P_("bCB", [128, 512])
            bTR = P_("bTR", [128, 1024], BF16)
            bT2 = P_("bT2", [128, 1024], BF16)
            bY = [P_("bY%d" % i, [128, 512]) for i in range(2)]
            bG = P_("bG", [128, 512])
            QF = Dma([s.SEM("qf%d" % i, 16) for i in range(2)])
            mk = {}
            xv = lambda c, a, w: s.xbc[:, c * XW + a:c * XW + a + w]

            @blk.gpsimd
            def _(g):
                s.S_CC.inc(g.collective_compute(
                    "AllGather", OP.bypass, replica_groups=[list(range(NCORES))],
                    ins=[AP(s.pc_in, 256 * PROJ, [[1, 512 * D]])],
                    outs=[AP(s.opw_bf, 0, [[1, INTER * D]]).opt()]))
                s.cc_opw = s.S_CC.n
                s.S_GP.inc(g.memset(identrep[:], 1.0))
                g.wait_ge(s.S_GP.h, s.S_GP.n)
                s.S_GP.inc(g.affine_select(identrep[:], identrep[:], [[0, 8], [1, 128]],
                                           OP.is_equal, 0.0, base=0, channel_multiplier=-1))
                mk["irep"] = s.S_GP.n

            @blk.vector
            def _(v):
                v.wait_ge(s.S_DV.h, s.S_DV.n)
                s.S_DV.inc(v.tensor_scalar_mul(s.dtsp[:], s.dtsp[:], -1.0))
                v.wait_ge(s.S_DV.h, s.S_DV.n)
                s.S_DV.inc(v.tensor_scalar_mul(dA[:], s.dtsp[:], s.acol[:]))
                mk["dtA"] = s.S_DV.n

            @blk.scalar
            def _(sc):
                sc.wait_ge(s.S_DV.h, mk["dtA"])
                s.S_AC.inc(sc.activation(dA[:], dA[:], AF.Exp))
                mk["dA"] = s.S_AC.n

            @blk.vector
            def _(v):
                v.wait_ge(s.S_AC.h, mk["dA"])
                for j in range(NJ):
                    v.wait_ge(s.S_DV.h, s.S_DV.n)
                    s.S_DV.inc(v.tensor_tensor_scan(
                        Pall[:, j * 128:(j + 1) * 128], dA[:, j * 128:(j + 1) * 128],
                        s.zeros64[:, 0:128], 1.0, OP.mult, OP.add))
                v.wait_ge(s.S_DV.h, s.S_DV.n)
                tdsrc = AP(Pall, 127, [[TOK, HEADS], [128, 8]])
                s.S_DV.inc(v.tensor_copy(tdmat[:], tdsrc))
                v.wait_ge(s.S_DV.h, s.S_DV.n)
                s.S_DV.inc(v.tensor_tensor_scan(Imat[:], tdmat[:], s.zeros64[:, 0:8],
                                                1.0, OP.mult, OP.add))
                v.wait_ge(s.S_DV.h, s.S_DV.n)
                s.S_DV.inc(v.tensor_copy(Pp[:, 0:128], Pall[:, 0:128]))
                for j in range(1, NJ):
                    s.S_DV.inc(v.tensor_scalar_mul(Pp[:, j * 128:(j + 1) * 128],
                                                   Pall[:, j * 128:(j + 1) * 128],
                                                   Imat[:, j - 1:j]))
                mk["pp"] = s.S_DV.n

            for j in range(NJ):
                @blk.tensor
                def _(t, j=j):
                    if j == 0:
                        t.wait_ge(s.S_DV.h, s.S_DV.n)
                    if ("dv_dtT", j - 2) in mk:
                        t.wait_ge(s.S_DV.h, mk[("dv_dtT", j - 2)])
                    mm = t.transpose(bAW[:, (j % 2) * 512:(j % 2) * 512 + 64],
                                     s.dtsp[:, j * 128:(j + 1) * 128],
                                     s.ident_f[0:64, 0:64])
                    mk[("pe_dtT", j)] = s.S_PE.inc(mm)

                @blk.vector
                def _(v, j=j):
                    v.wait_ge(s.S_PE.h, mk[("pe_dtT", j)])
                    v.wait_ge(s.S_DV.h, s.S_DV.n)
                    s.S_DV.inc(v.tensor_copy(dtT[:, j * HEADS:(j + 1) * HEADS],
                                             bAW[:, (j % 2) * 512:(j % 2) * 512 + 64]))
                    mk[("dv_dtT", j)] = s.S_DV.n

            @blk.vector
            def _(v):
                v.wait_ge(s.S_AC.h, mk["dA"])
                v.wait_ge(s.S_DV.h, s.S_DV.n)
                s.S_DV.inc(v.tensor_copy(dAz[:], dA[:]))
                mk["dAz"] = s.S_DV.n

            @blk.gpsimd
            def _(g):
                g.wait_ge(s.S_DV.h, mk["dAz"])
                for j in range(NJ):
                    s.S_GP.inc(g.memset(dAz[:, j * 128:j * 128 + 1], 0.0))
                mk["dAz0"] = s.S_GP.n

            @blk.gpsimd
            def _(g):
                g.wait_ge(s.S_GP.h, mk["dAz0"])
                g.wait_ge(s.S_DV.h, mk["pp"])
                s.Q_GP.issue(g, AP(s.rowb, 0, [[TOK, HEADS], [1, TOK]]), dAz[:])
                s.Q_GP.issue(g, AP(s.rowb, 64 * TOK, [[TOK, HEADS], [1, TOK]]), Pall[:])
                s.Q_GP.issue(g, AP(s.rowb, 128 * TOK, [[TOK, HEADS], [1, TOK]]), Pp[:])
                s.Q_GP.drain(g)
                s.S_GP.inc(g.engine_nop())
            mk["rowb"] = s.S_GP.n

            @blk.gpsimd
            def _(g):
                g.wait_ge(s.S_DV.h, s.S_DV.n)
                s.S_GP.inc(g.memset(s.state[:], 0.0))
            mk["st0"] = s.S_GP.n

            for j in range(NJ):
                jc = 3 + j * 128

                @blk.tensor
                def _(t, j=j, jc=jc):
                    if j == 0:
                        t.wait_ge(s.S_DV.h, mk[("dv_dtT", NJ - 1)])
                        t.wait_ge(s.S_GP.h, mk["st0"])
                    if ("dv_BT", j - 1) in mk:
                        t.wait_ge(s.S_DV.h, mk[("dv_BT", j - 1)])
                    mm = t.transpose(bTR[:, 0:128], xv(32, jc, 128), s.ident_b[:])
                    mk[("pe_BT", j)] = s.S_PE.inc(mm)
                    if ("dv_S", j - 1, 7, 7) in mk:
                        t.wait_ge(s.S_DV.h, mk[("dv_S", j - 1, 7, 7)])
                    mm = t.matmul(bCB[:, 0:128], xv(32, jc, 128), xv(33, jc, 128),
                                  start=True, stop=True)
                    mk[("pe_CB", j)] = s.S_PE.inc(mm)

                @blk.vector
                def _(v, j=j):
                    v.wait_ge(s.S_PE.h, mk[("pe_BT", j)])
                    v.wait_ge(s.S_DV.h, s.S_DV.n)
                    s.S_DV.inc(v.tensor_copy(BT[:], bTR[:, 0:128]))
                    mk[("dv_BT", j)] = s.S_DV.n

                for o in range(8):
                    op = o % 2

                    @blk.gpsimd
                    def _(g, j=j, o=o, op=op):
                        if o == 0 and j == 0:
                            g.wait_ge(s.S_GP.h, mk["rowb"])
                        gk = ("pe_pbc", j, o - 1) if o else ("pe_pbc", j - 1, 7)
                        if gk in mk:
                            g.wait_ge(s.S_PE.h, mk[gk])
                        _, n = QF.issue(g, fA[0:1, :],
                                        AP(s.rowb, (8 * o) * TOK + j * 128,
                                           [[1, 1], [TOK, 8], [1, 128]]))
                        mk[("ld_fA", j, o)] = (QF.sems[(QF.i - 1) % 2], n)
                        _, n = QF.issue(g, fP[0:1, :],
                                        AP(s.rowb, (64 + 8 * o) * TOK + j * 128,
                                           [[1, 1], [TOK, 8], [1, 128]]))
                        mk[("ld_fP", j, o)] = (QF.sems[(QF.i - 1) % 2], n)

                    @blk.tensor
                    def _(t, j=j, o=o, op=op, jc=jc):
                        if o == 0:
                            t.wait_ge(s.S_DV.h, mk[("dv_BT", j)])
                        if ("dv_dx", j, o - 1) in mk:
                            t.wait_ge(s.S_DV.h, mk[("dv_dx", j, o - 1)])
                        tb = bTR if op == 0 else bT2
                        for q in range(4):
                            ic = o * 4 + q
                            mm = t.transpose(tb[:, 128 + q * 128:128 + q * 128 + 128]
                                             if tb is bTR else tb[:, q * 128:q * 128 + 128],
                                             xv(ic, jc, 128), s.ident_b[:])
                        mk[("pe_tr", j, o)] = s.S_PE.inc(mm)

                    @blk.vector
                    def _(v, j=j, o=o, op=op):
                        v.wait_ge(s.S_PE.h, mk[("pe_tr", j, o)])
                        tb = bTR if op == 0 else bT2
                        base = 128 if op == 0 else 0
                        v.wait_ge(s.S_DV.h, s.S_DV.n)
                        dtrep = dtT.ap()[:, j * HEADS + 8 * o:j * HEADS + 8 * o + 8] \
                            .unsqueeze(2).broadcast_to([128, 8, 64])
                        s.S_DV.inc(v.tensor_tensor(
                            dxo.ap()[:, op * 512:op * 512 + 512]
                            .rearrange("p (a b) -> p a b", a=8),
                            tb.ap()[:, base:base + 512].rearrange("p (a b) -> p a b", a=8),
                            dtrep, OP.mult))
                        mk[("dv_dx", j, o)] = s.S_DV.n

                    @blk.tensor
                    def _(t, j=j, o=o, op=op):
                        sem, n = mk[("ld_fA", j, o)]
                        t.wait_ge(sem.h, n)
                        for key in ((("dv_ch", j, o - 1, 7) if o else ("dv_ch", j - 1, 7, 7)),
                                    (("dv_CP", j, o - 1, 7) if o else ("dv_CP", j - 1, 7, 7))):
                            if key in mk:
                                t.wait_ge(s.S_DV.h, mk[key])
                        for q in range(2):
                            mm = t.matmul(bA[q][:], s.ones1[:],
                                          fA[0:1, q * 512:q * 512 + 512],
                                          start=True, stop=True)
                        mk[("pe_abc", j, o)] = s.S_PE.inc(mm)

                    @blk.vector
                    def _(v, j=j, o=o, op=op):
                        v.wait_ge(s.S_PE.h, mk[("pe_abc", j, o)])
                        v.wait_ge(s.S_PE.h, mk[("pe_CB", j)])
                        v.wait_ge(s.S_DV.h, mk[("dv_dx", j, o)])
                        gk = ("pe_G", j, o - 1) if o else ("pe_G", j - 1, 7)
                        if gk in mk:
                            v.wait_ge(s.S_PE.h, mk[gk])
                        v.wait_ge(s.S_DV.h, s.S_DV.n)
                        if j == 0 and o == 0:
                            v.wait_ge(s.S_GP.h, mk["irep"])
                        s.S_DV.inc(v.tensor_tensor_scan(
                            Ebuf[:], bAW[:], identrep[:], 0.0, OP.mult, OP.add))
                        v.wait_ge(s.S_DV.h, s.S_DV.n)
                        cbrep = bCB.ap()[:, 0:128].unsqueeze(1).broadcast_to([128, 8, 128])
                        s.S_DV.inc(v.tensor_tensor(
                            Sbuf.ap().rearrange("p (a b) -> p a b", a=8),
                            Ebuf.ap().rearrange("p (a b) -> p a b", a=8), cbrep, OP.mult))
                        v.wait_ge(s.S_DV.h, s.S_DV.n)
                        wrep = AP(Ebuf, 127, [[1024, 128], [128, 8]]) \
                            .unsqueeze(2).broadcast_to([128, 8, 64])
                        s.S_DV.inc(v.tensor_tensor(
                            dxw.ap().rearrange("p (a b) -> p a b", a=8),
                            dxo.ap()[:, op * 512:op * 512 + 512]
                            .rearrange("p (a b) -> p a b", a=8), wrep, OP.mult))
                        n = s.S_DV.n
                        for m in range(8):
                            mk[("dv_E", j, o, m)] = n
                            mk[("dv_S", j, o, m)] = n

                    @blk.tensor
                    def _(t, j=j, o=o, op=op):
                        sem, n = mk[("ld_fP", j, o)]
                        t.wait_ge(sem.h, n)
                        t.wait_ge(s.S_DV.h, mk[("dv_E", j, o, 7)])
                        for q in range(2):
                            mm = t.matmul(bA[q][:], s.ones1[:],
                                          fP[0:1, q * 512:q * 512 + 512],
                                          start=True, stop=True)
                        mk[("pe_pbc", j, o)] = s.S_PE.inc(mm)

                    @blk.vector
                    def _(v, j=j, o=o, op=op, jc=jc):
                        v.wait_ge(s.S_PE.h, mk[("pe_pbc", j, o)])
                        v.wait_ge(s.S_DV.h, s.S_DV.n)
                        crep = s.xbc.ap()[:, 33 * XW + jc:33 * XW + jc + 128] \
                            .unsqueeze(1).broadcast_to([128, 8, 128])
                        s.S_DV.inc(v.tensor_tensor(
                            CPb.ap().rearrange("p (a b) -> p a b", a=8), crep,
                            bAW.ap().rearrange("p (a b) -> p a b", a=8), OP.mult))
                        n = s.S_DV.n
                        for m in range(8):
                            mk[("dv_CP", j, o, m)] = n

                    for m in range(8):
                        h = 8 * o + m

                        @blk.tensor
                        def _(t, j=j, o=o, op=op, m=m, h=h):
                            if m == 0:
                                t.wait_ge(s.S_DV.h, mk[("dv_S", j, o, 7)])
                                t.wait_ge(s.S_DV.h, mk[("dv_CP", j, o, 7)])
                            yb = bY[op]
                            reg = yb[(m % 2) * 64:(m % 2) * 64 + 64,
                                     (m // 2) * 128:(m // 2) * 128 + 128]
                            tp = (0, 64) if m % 2 else None
                            mm = t.matmul(reg, dxo[:, op * 512 + m * 64:op * 512 + m * 64 + 64],
                                          Sbuf[:, m * 128:m * 128 + 128],
                                          start=True, stop=(j == 0), tile_position=tp)
                            if j > 0:
                                mm = t.matmul(reg, s.state[:, h * 64:h * 64 + 64],
                                              CPb[:, m * 128:m * 128 + 128],
                                              start=False, stop=True, tile_position=tp)
                            mk[("pe_Y", j, o, m)] = s.S_PE.inc(mm)

                    @blk.tensor
                    def _(t, j=j, o=o, op=op):
                        t.wait_ge(s.S_DV.h, mk[("dv_E", j, o, 7)])
                        if ("dv_ch", j, o - 1, 7) in mk:
                            t.wait_ge(s.S_DV.h, mk[("dv_ch", j, o - 1, 7)])
                        mm = t.matmul(bG[:], BT[:], dxw[:],
                                      start=True, stop=True)
                        mk[("pe_G", j, o)] = s.S_PE.inc(mm)

                    @blk.vector
                    def _(v, j=j, o=o, op=op):
                        v.wait_ge(s.S_PE.h, mk[("pe_G", j, o)])
                        v.wait_ge(s.S_DV.h, s.S_DV.n)
                        tdrep = AP(bAW, 127, [[1024, 128], [128, 8]]) \
                            .unsqueeze(2).broadcast_to([128, 8, 64])
                        stv = s.state.ap()[:, o * 512:o * 512 + 512] \
                            .rearrange("p (a b) -> p a b", a=8)
                        s.S_DV.inc(v.tensor_tensor(stv, stv, tdrep, OP.mult))
                        v.wait_ge(s.S_DV.h, s.S_DV.n)
                        s.S_DV.inc(v.tensor_tensor(
                            s.state[:, o * 512:o * 512 + 512],
                            s.state[:, o * 512:o * 512 + 512], bG[:], OP.add))
                        n = s.S_DV.n
                        for m in range(8):
                            mk[("dv_ch", j, o, m)] = n

                    for pr in range(4):
                        ic = o * 4 + pr

                        @blk.vector
                        def _(v, j=j, o=o, op=op, pr=pr, ic=ic, jc=jc):
                            v.wait_ge(s.S_PE.h, mk[("pe_Y", j, o, 7)])
                            v.wait_ge(s.S_DV.h, s.S_DV.n)
                            s.S_DV.inc(v.scalar_tensor_tensor(
                                xv(ic, jc, 128), xv(ic, jc, 128),
                                s.drep_s[:, ic:ic + 1],
                                bY[op][:, pr * 128:pr * 128 + 128], OP.mult, OP.add))
                            mk[("dv_Y", j, o, pr)] = s.S_DV.n

            @blk.gpsimd
            def _(g):
                g.wait_ge(s.S_DV.h, s.S_DV.n)
                s.Q_GP.issue(g, AP(s.s_in, 0, [[INTER, GTS], [1, INTER]]), s.state[:])
                s.Q_GP.drain(g)
                s.S_CC.inc(g.collective_compute(
                    "AllGather", OP.bypass,
                    replica_groups=[[0, 1], [2, 3], [4, 5], [6, 7]],
                    ins=[s.s_in.ap().opt()], outs=[s.s_out.ap().opt()]))
                g.wait_ge(s.S_CC.h, s.S_CC.n)
                s.Q_GP.issue(g, s.state[:], AP(s.s_out, 0, [[INTER, GTS], [1, INTER]]))
                s.Q_GP.drain(g)
                s.S_GP.inc(g.engine_nop())
            self.mk_ag2 = s.S_GP.n

            @blk.vector
            def _(v):
                v.wait_ge(s.S_GP.h, self.mk_ag2)
                v.wait_ge(s.S_DV.h, s.S_DV.n)
                s.S_DV.inc(v.tensor_scalar_mul(s.state[:], s.state[:], s.flag_s[:]))
            self.mk_smask = s.S_DV.n
            self.drain_all(blk, extra=(QF,))

    def p_carry2(self):
        s = self
        nc = s.nc
        with nc.Block() as blk, contextlib.ExitStack() as ar:
            A = lambda nm, sh, dt=F32: ar.enter_context(nc.sbuf_tensor(nm, sh, dt))
            P_ = lambda nm, sh, dt=F32: ar.enter_context(nc.psum_tensor(nm, sh, dt))
            fP = A("c_fP", [1, 1024])
            CPb4 = A("c_CPb4", [128, 8 * 512], BF16)
            bAW = P_("c_bAW", [128, 1024])
            bA = [bAW.ap()[:, 0:512], bAW.ap()[:, 512:1024]]
            bY = [P_("c_bY%d" % i, [128, 512]) for i in range(4)]
            QF = Dma([s.SEM("cqf%d" % i, 16) for i in range(2)])
            mk = {}
            xv = lambda c, a, w: s.xbc[:, c * XW + a:c * XW + a + w]

            for jh in range(2):
                for o in range(8):
                    for jl in range(4):
                        j = jh * 4 + jl
                        jc = 3 + j * 128

                        @blk.gpsimd
                        def _(g, j=j, o=o, jl=jl):
                            gk = (("pe_pbc", o, jl - 1) if jl else
                                  (("pe_pbc", o - 1, 3) if o else ("pe_pbc_l", jh - 1)))
                            if gk in mk:
                                g.wait_ge(s.S_PE.h, mk[gk])
                            _, n = QF.issue(g, fP[0:1, :],
                                            AP(s.rowb, (128 + 8 * o) * TOK + j * 128,
                                               [[1, 1], [TOK, 8], [1, 128]]))
                            mk[("ld_fP", o, jl)] = (QF.sems[(QF.i - 1) % 2], n)

                        @blk.tensor
                        def _(t, j=j, o=o, jl=jl):
                            sem, n = mk[("ld_fP", o, jl)]
                            t.wait_ge(sem.h, n)
                            gk = (("dv_CP", o, jl - 1) if jl else
                                  (("dv_CP", o - 1, 3) if o else ("dv_CP_l", jh - 1)))
                            if gk in mk:
                                t.wait_ge(s.S_DV.h, mk[gk])
                            for q in range(2):
                                mm = t.matmul(bA[q][:], s.ones1[:],
                                              fP[0:1, q * 512:q * 512 + 512],
                                              start=True, stop=True)
                            mk[("pe_pbc", o, jl)] = s.S_PE.inc(mm)
                            if jl == 3:
                                mk[("pe_pbc_l", jh)] = s.S_PE.n

                        @blk.vector
                        def _(v, j=j, o=o, jl=jl, jc=jc):
                            v.wait_ge(s.S_PE.h, mk[("pe_pbc", o, jl)])
                            if jl == 0:
                                gk = ("pe_Y", o - 1, 7) if o else ("pe_Y_l", jh - 1)
                                if gk in mk:
                                    v.wait_ge(s.S_PE.h, mk[gk])
                            v.wait_ge(s.S_DV.h, s.S_DV.n)
                            crep = s.xbc.ap()[:, 33 * XW + jc:33 * XW + jc + 128] \
                                .unsqueeze(1).broadcast_to([128, 8, 128])
                            dst = AP(CPb4, jl * 128, [[8 * 512, 128], [512, 8], [1, 128]])
                            s.S_DV.inc(v.tensor_tensor(
                                dst, crep,
                                bAW.ap().rearrange("p (a b) -> p a b", a=8), OP.mult))
                            mk[("dv_CP", o, jl)] = s.S_DV.n

                    for m in range(8):
                        h = 8 * o + m

                        @blk.tensor
                        def _(t, o=o, m=m, h=h, jh=jh):
                            if m == 0:
                                if o == 0 and jh == 0:
                                    t.wait_ge(s.S_DV.h, s.mk_smask)
                                t.wait_ge(s.S_DV.h, mk[("dv_CP", o, 3)])
                                gk = ("dv_Y", o - 1, 3) if o else ("dv_Y_l", jh - 1)
                                if gk in mk:
                                    t.wait_ge(s.S_DV.h, mk[gk])
                            reg = bY[m // 2][(m % 2) * 64:(m % 2) * 64 + 64, :]
                            tp = (0, 64) if m % 2 else None
                            mm = t.matmul(reg, s.state[:, h * 64:h * 64 + 64],
                                          CPb4[:, m * 512:m * 512 + 512],
                                          start=True, stop=True, tile_position=tp)
                            mk[("pe_Y", o, m)] = s.S_PE.inc(mm)
                            if m == 7:
                                mk[("pe_Y_l", jh)] = s.S_PE.n

                    for pr in range(4):
                        ic = o * 4 + pr

                        @blk.vector
                        def _(v, o=o, pr=pr, ic=ic, jh=jh):
                            v.wait_ge(s.S_PE.h, mk[("pe_Y", o, 7)])
                            v.wait_ge(s.S_DV.h, s.S_DV.n)
                            s.S_DV.inc(v.tensor_tensor(
                                xv(ic, 3 + jh * 512, 512), xv(ic, 3 + jh * 512, 512),
                                bY[pr][:], OP.add))
                            mk[("dv_Y", o, pr)] = s.S_DV.n
                            if pr == 3:
                                mk[("dv_Y_l", jh)] = s.S_DV.n

            self.drain_all(blk, extra=(QF,))

    def p_gate_out(self):
        s = self
        nc = s.nc
        with nc.Block() as blk, contextlib.ExitStack() as ar:
            A = lambda nm, sh, dt=F32: ar.enter_context(nc.sbuf_tensor(nm, sh, dt))
            P_ = lambda nm, sh, dt=F32: ar.enter_context(nc.psum_tensor(nm, sh, dt))
            gated = A("gated", [128, IC * TH], BF16)
            y2b = A("y2b", [128, 2 * TH], BF16)
            zt = A("g_zt", [128, 2 * TH], BF16)
            tq = A("g_tq", [128, TH])
            opslab = A("opslab", [128, 2 * IC * 128], BF16)
            g0 = P_("g0", [128, 512])
            g1 = P_("g1", [128, 512])
            g1b = P_("g1b", [128, 512])
            go = [P_("go%d" % i, [128, 512]) for i in range(2)]
            QOP = Dma([s.SEM("qop%d" % i, 16) for i in range(2)])
            QZ = Dma([s.SEM("qz%d" % i, 16) for i in range(2)])
            mk = {}
            xv = lambda c, a, w: s.xbc[:, c * XW + a:c * XW + a + w]

            def emit_stats(tt, g1t):
                T0 = 3 + tt * TH
                for ic in range(IC):
                    pp = ic % 2

                    @blk.scalar
                    def _(sc, ic=ic, pp=pp, T0=T0, tt=tt):
                        if tt == 0 and ic == 0:
                            sc.wait_ge(s.S_DV.h, s.S_DV.n)
                        if ("pe_rms", tt, ic - 2) in mk:
                            sc.wait_ge(s.S_PE.h, mk[("pe_rms", tt, ic - 2)])
                        s.S_AC.inc(sc.activation(y2b[:, pp * TH:pp * TH + TH],
                                                 xv(ic, T0, TH), AF.Square))
                        mk[("ac_y2", tt, ic)] = s.S_AC.n

                    @blk.tensor
                    def _(t, ic=ic, pp=pp, tt=tt):
                        t.wait_ge(s.S_AC.h, mk[("ac_y2", tt, ic)])
                        if tt == 1 and ic == 0:
                            t.wait_ge(s.S_DV.h, s.S_DV.n)
                        mm = t.matmul(g0[:], s.ones_b[:], y2b[:, pp * TH:pp * TH + TH],
                                      start=(ic == 0), stop=(ic == IC - 1))
                        mk[("pe_rms", tt, ic)] = s.S_PE.inc(mm)

                @blk.vector
                def _(v, tt=tt):
                    v.wait_ge(s.S_PE.h, mk[("pe_rms", tt, IC - 1)])
                    v.wait_ge(s.S_DV.h, s.S_DV.n)
                    s.S_DV.inc(v.tensor_scalar_mul(s.row(0), g0[0:1, :], 1.0 / INTER))
                    mk["g_r0"] = s.S_DV.n

                @blk.scalar
                def _(sc, tt=tt):
                    sc.wait_ge(s.S_DV.h, mk["g_r0"])
                    s.S_AC.inc(sc.activation(s.row(1), s.row(0), AF.Sqrt,
                                             bias=s.epscol[0:1, 0:1]))
                    mk["g_r1"] = s.S_AC.n

                @blk.vector
                def _(v, tt=tt):
                    v.wait_ge(s.S_AC.h, mk["g_r1"])
                    v.wait_ge(s.S_DV.h, s.S_DV.n)
                    s.S_DV.inc(v.reciprocal(s.row(2), s.row(1)))
                    mk["g_r2"] = s.S_DV.n

                @blk.tensor
                def _(t, tt=tt, g1t=g1t):
                    t.wait_ge(s.S_DV.h, mk["g_r2"])
                    mm = t.matmul(g1t[:], s.ones1[:], s.row(2), start=True, stop=True)
                    mk[("pe_g1", tt)] = s.S_PE.inc(mm)

            def emit_gate(tt, g1t):
                T0 = 3 + tt * TH
                for ic in range(IC):
                    pp = ic % 2

                    @blk.sync
                    def _(sy, ic=ic, pp=pp, tt=tt):
                        gk = ("dv_gate", tt, ic - 2) if ic >= 2 else ("dv_gate", tt - 1, IC - 2 + ic)
                        if gk in mk:
                            sy.wait_ge(s.S_DV.h, mk[gk])
                        src = AP(s.z_spill, ic * 128 * TOK + tt * TH, [[TOK, 128], [1, TH]])
                        _, n = QZ.issue(sy, zt[:, pp * TH:pp * TH + TH], src)
                        mk[("ld_z", tt, ic)] = (QZ.sems[(QZ.i - 1) % 2], n)

                    @blk.vector
                    def _(v, ic=ic, pp=pp, T0=T0, tt=tt, g1t=g1t):
                        if ic == 0:
                            v.wait_ge(s.S_PE.h, mk[("pe_g1", tt)])
                            if tt == 1:
                                v.wait_ge(s.S_PE.h, mk[("pe_op", 0, DC - 1)])
                        sem, n = mk[("ld_z", tt, ic)]
                        v.wait_ge(sem.h, n)
                        v.wait_ge(s.S_DV.h, s.S_DV.n)
                        s.S_DV.inc(v.scalar_tensor_tensor(
                            tq[:], xv(ic, T0, TH),
                            s.nw_s[:, ic:ic + 1], g1t[:], OP.mult, OP.mult))
                        v.wait_ge(s.S_DV.h, s.S_DV.n)
                        s.S_DV.inc(v.tensor_tensor(gated[:, ic * TH:ic * TH + TH],
                                                   tq[:],
                                                   zt[:, pp * TH:pp * TH + TH], OP.mult))
                        mk[("dv_gate", tt, ic)] = s.S_DV.n

            def emit_op(tt):
                for do in range(DC):
                    pp = do % 2

                    @blk.sync
                    def _(sy, do=do, pp=pp, tt=tt):
                        if tt == 0 and do == 0:
                            sy.wait_ge(s.S_CC.h, s.cc_opw)
                        if ("pe_op", tt, do - 2) in mk:
                            sy.wait_ge(s.S_PE.h, mk[("pe_op", tt, do - 2)])
                        elif tt == 1 and ("pe_op", 0, DC - 2 + do) in mk:
                            sy.wait_ge(s.S_PE.h, mk[("pe_op", 0, DC - 2 + do)])
                        src = AP(s.opw_bf, do * 128,
                                 [[D, 128], [D * 128, IC], [1, 128]])
                        dst = AP(opslab, pp * IC * 128, [[2 * IC * 128, 128], [128, IC], [1, 128]])
                        _, n = QOP.issue(sy, dst, src)
                        mk[("ld_op", tt, do)] = (QOP.sems[(QOP.i - 1) % 2], n)

                    @blk.tensor
                    def _(t, do=do, pp=pp, tt=tt):
                        sem, n = mk[("ld_op", tt, do)]
                        t.wait_ge(sem.h, n)
                        if do == 0:
                            t.wait_ge(s.S_DV.h, mk[("dv_gate", tt, IC - 1)])
                        if ("dv_op", tt, do - 2) in mk:
                            t.wait_ge(s.S_DV.h, mk[("dv_op", tt, do - 2)])
                        for ic in range(IC):
                            mm = t.matmul(go[pp][:],
                                          opslab[:, pp * IC * 128 + ic * 128:
                                                pp * IC * 128 + ic * 128 + 128],
                                          gated[:, ic * TH:ic * TH + TH],
                                          start=(ic == 0), stop=(ic == IC - 1))
                        mk[("pe_op", tt, do)] = s.S_PE.inc(mm)

                    @blk.vector
                    def _(v, do=do, pp=pp, tt=tt):
                        v.wait_ge(s.S_PE.h, mk[("pe_op", tt, do)])
                        v.wait_ge(s.S_DV.h, s.S_DV.n)
                        T0r = tt * TH
                        s.S_DV.inc(v.tensor_tensor(
                            s.resid[:, do * TOK + T0r:do * TOK + T0r + TH],
                            s.resid[:, do * TOK + T0r:do * TOK + T0r + TH],
                            go[pp][:], OP.add))
                        mk[("dv_op", tt, do)] = s.S_DV.n

            emit_stats(0, g1)
            emit_gate(0, g1)
            emit_stats(1, g1b)
            emit_op(0)
            emit_gate(1, g1b)
            emit_op(1)

            self.drain_all(blk, extra=(QOP, QZ))

    def p_store(self):
        s = self
        with s.nc.Block() as blk:
            @blk.scalar
            def _(sc):
                dst = AP(s.out_ap.tensor, 0, [[TOK, 128], [TOK * 128, DC], [1, TOK]])
                s.Q_ST.issue(sc, dst, s.resid.ap().rearrange("p (c t) -> p c t", c=DC))
                s.Q_ST.drain(sc)


def _prep_inputs(inputs, c):
    b, hf = c // 2, c % 2
    x = np.asarray(inputs["L0_attn_hidden_states"])
    w1 = np.asarray(inputs["L0_ffn_w1"])
    w2 = np.asarray(inputs["L0_ffn_w2"])
    w3 = np.asarray(inputs["L0_ffn_w3"])
    return dict(
        x_fm=np.ascontiguousarray(x[b, hf * TOK:(hf + 1) * TOK, :].T),
        w1s=np.ascontiguousarray(w1[:, c * 256:(c + 1) * 256]),
        w3s=np.ascontiguousarray(w3[:, c * 256:(c + 1) * 256]),
        w2s=np.ascontiguousarray(w2[c * 256:(c + 1) * 256, :]),
        ipw_sh=np.ascontiguousarray(
            np.asarray(inputs["L0_attn_in_proj_w"])[c * 256:(c + 1) * 256, :]),
        opw_sh=np.ascontiguousarray(
            np.asarray(inputs["L0_attn_out_proj_w"])[c * 512:(c + 1) * 512, :]),
        convw=np.ascontiguousarray(inputs["L0_attn_conv_w"]),
        convb=np.ascontiguousarray(np.asarray(inputs["L0_attn_conv_b"]).reshape(CONV, 1)),
        alog=np.ascontiguousarray(np.asarray(inputs["L0_attn_A_log"]).reshape(HEADS, 1)),
        dtb=np.ascontiguousarray(np.asarray(inputs["L0_attn_dt_bias"]).reshape(HEADS, 1)),
        drep=np.ascontiguousarray(np.repeat(np.asarray(inputs["L0_attn_D"]), PD).reshape(INTER, 1)),
        nw=np.ascontiguousarray(np.asarray(inputs["L0_attn_norm_w"]).reshape(INTER, 1)),
        flag=np.full((128, 1), float(hf), np.float32),
    )


def kernel(**inputs):
    mk = MK()
    nc = mk.build()
    in_maps = [_prep_inputs(inputs, c) for c in range(NCORES)]
    res = run_bass_kernel_spmd(nc, in_maps, core_ids=list(range(NCORES)))
    out = np.empty((4, 2048, 2048), np.float32)
    for c in range(NCORES):
        b, hf = c // 2, c % 2
        out[b, hf * TOK:(hf + 1) * TOK, :] = np.asarray(res.results[c]["out_fm"]).T
    return out


# version 15 (final)
# speedup vs baseline: 1.0147x; 1.0147x over previous
import contextlib
import numpy as np
import concourse.bass as bass
import concourse.mybir as mybir
from concourse.ap import AP
from concourse.bass_utils import run_bass_kernel_spmd

F32 = mybir.dt.float32
BF16 = mybir.dt.bfloat16
AF = mybir.ActivationFunctionType
OP = mybir.AluOpType
AX = mybir.AxisListType

MAGIC = float(np.float32(3 * 2**22))
EPS = 1e-5
NCORES = 8
TOK = 1024
TH = 512
D = 2048
DC = 16
HID = 5632
HC = 44
HP = 22
INTER = 4096
IC = 32
GTS = 128
CONV = 4352
CCH = 34
XW = 1027
PROJ = 8512
HEADS = 64
PD = 64
NJ = 8
BLK = 128 * 256
NW_ELEMS = float(HID * D)

W1_SHARD = 2 * HP * BLK
HWH = W1_SHARD // 2
W2_SHARD = HC * BLK
RANK_SH = 2 * W1_SHARD + W2_SHARD
W1_OFF, W3_OFF, W2_OFF = 0, W1_SHARD, 2 * W1_SHARD

STAGE = 4
FFN_PARTS = 9


def K_FFN_PARTS():
    return FFN_PARTS


class Sem:
    def __init__(self, handle, step=1):
        self.h = handle
        self.step = step
        self.n = 0

    def inc(self, instr):
        self.n += self.step
        instr.then_inc(self.h, self.step)
        return self.n

    def wait(self, eng, n=None):
        n = self.n if n is None else n
        if n > 0:
            eng.wait_ge(self.h, n)


class Dma:
    def __init__(self, sems):
        self.sems = sems
        self.i = 0

    def issue(self, eng, out, in_, **kw):
        s = self.sems[self.i % len(self.sems)]
        if s.n > 0:
            eng.wait_ge(s.h, s.n)
        s.inc(eng.dma_start(out=out, in_=in_, **kw))
        self.i += 1
        return s, s.n

    def drain(self, eng):
        for s in self.sems:
            s.wait(eng)


class MK:
    def __init__(self):
        nc = self.nc = bass.Bass("TRN2", target_bir_lowering=False)
        DT = nc.dram_tensor
        self.x_in = DT("x_fm", [D, TOK], F32, kind="ExternalInput").ap()
        self.w1s_in = DT("w1s", [HID, 256], F32, kind="ExternalInput").ap()
        self.w3s_in = DT("w3s", [HID, 256], F32, kind="ExternalInput").ap()
        self.w2s_in = DT("w2s", [256, HID], F32, kind="ExternalInput").ap()
        self.ipw_in = DT("ipw_sh", [256, PROJ], F32, kind="ExternalInput").ap()
        self.opw_in = DT("opw_sh", [512, D], F32, kind="ExternalInput").ap()
        self.convw_in = DT("convw", [CONV, 4], F32, kind="ExternalInput").ap()
        self.convb_in = DT("convb", [CONV, 1], F32, kind="ExternalInput").ap()
        self.alog_in = DT("alog", [HEADS, 1], F32, kind="ExternalInput").ap()
        self.dtb_in = DT("dtb", [HEADS, 1], F32, kind="ExternalInput").ap()
        self.drep_in = DT("drep", [INTER, 1], F32, kind="ExternalInput").ap()
        self.nw_in = DT("nw", [INTER, 1], F32, kind="ExternalInput").ap()
        self.flag_in = DT("flag", [128, 1], F32, kind="ExternalInput").ap()
        self.out_ap = DT("out_fm", [D, TOK], F32, kind="ExternalOutput").ap()

        self.q_in = DT("q_in", [RANK_SH], BF16, kind="Internal")
        self.q1_out = DT("q1_out", [NCORES * W1_SHARD], BF16, kind="Internal")
        self.q3_out = DT("q3_out", [NCORES * W1_SHARD], BF16, kind="Internal")
        self.q2_out = DT("q2_out", [NCORES * W2_SHARD], BF16, kind="Internal")
        self.m_in = DT("m_in", [1, 4], F32, kind="Internal")
        self.m_out = DT("m_out", [1, 4], F32, kind="Internal")
        self.hl_in = DT("hl_in", [CONV, 3], BF16, kind="Internal")
        self.hl_out = DT("hl_out", [2 * CONV, 3], BF16, kind="Internal")
        self.s_in = DT("s_in", [GTS, INTER], F32, kind="Internal")
        self.s_out = DT("s_out", [2 * GTS, INTER], F32, kind="Internal")
        self.z_spill = DT("z_spill", [INTER, TOK], BF16, kind="Internal")
        self.rowb = DT("rowb", [192, TOK], F32, kind="Internal")
        self.ipw_bf = DT("ipw_bf", [D, PROJ], BF16, kind="Internal")
        self.opw_bf = DT("opw_bf", [INTER, D], BF16, kind="Internal")
        self.pc_in = DT("pc_in", [256 * PROJ + 512 * D], BF16, kind="Internal")

    def SB(self, nm, sh, dt=F32):
        return self.ctx.enter_context(self.nc.sbuf_tensor(nm, sh, dt))

    def SEM(self, nm, step=1):
        return Sem(self.ctx.enter_context(self.nc.semaphore(nm)), step)

    def build(self):
        self.ctx = contextlib.ExitStack()
        with self.ctx:
            self.alloc_persistent()
            self.p_consts()
            self.p_quant()
            self.p_ffn()
            if STAGE >= 2:
                with contextlib.ExitStack() as mar:
                    A = lambda nm, sh, dt=F32: mar.enter_context(self.nc.sbuf_tensor(nm, sh, dt))
                    self.xbc = A("xbc", [128, CCH * XW], BF16)
                    self.dtsp = A("dtsp", [HEADS, TOK])
                    self.p_proj_conv()
                    if STAGE >= 3:
                        with contextlib.ExitStack() as sar:
                            self.state = sar.enter_context(
                                self.nc.sbuf_tensor("state", [128, INTER], BF16))
                            self.p_scan()
                            self.p_carry2()
                        self.p_gate_out()
            if STAGE >= 4:
                self.p_ffn()
            self.p_store()
        return self.nc

    def alloc_persistent(self):
        SB, SEM = self.SB, self.SEM
        self.resid = SB("resid", [128, DC * TOK])
        self.ident_b = SB("ident_b", [128, 128], BF16)
        self.ident_f = SB("ident_f", [128, 128], F32)
        self.ones_b = SB("ones_b", [128, 128], BF16)
        self.ones1 = SB("ones1", [1, 128], F32)
        self.ones_f = SB("ones_f", [128, 128], F32)
        self.epscol = SB("epscol", [128, 1], F32)
        self.convw_s = SB("convw_s", [128, CCH * 4])
        self.convb_s = SB("convb_s", [128, CCH])
        self.acol = SB("acol", [HEADS, 1])
        self.dtb_s = SB("dtb_s", [HEADS, 1])
        self.dtbn_s = SB("dtbn_s", [HEADS, 1])
        self.drep_s = SB("drep_s", [128, IC])
        self.nw_s = SB("nw_s", [128, IC])
        self.flag_s = SB("flag_s", [128, 1])
        self.qcols = SB("qcols", [128, 12])
        self.meansn = SB("meansn", [128, 4])
        self.rows = SB("rows", [1, 6 * TH])
        self.rows2 = SB("rows2", [1, 16])
        self.zeros64 = SB("zeros64", [HEADS, 128])

        self.S_PE = SEM("S_PE")
        self.S_DV = SEM("S_DV")
        self.S_AC = SEM("S_AC")
        self.S_GP = SEM("S_GP")
        self.S_CC = SEM("S_CC")
        self.Q_SY = Dma([SEM("sy%d" % i, 16) for i in range(3)])
        self.Q_SC = Dma([SEM("sc%d" % i, 16) for i in range(2)])
        self.Q_GP = Dma([SEM("gp%d" % i, 16) for i in range(2)])
        self.Q_ST = Dma([SEM("qst%d" % i, 16) for i in range(2)])

    ROWMAP = {0: 0, 1: 1, 2: 2, 3: 3, 4: 0, 5: 5, 6: 1, 7: 2, 8: 3, 9: 0,
              10: 1, 11: 4, 12: 0, 13: 2, 14: 3, 15: 0, 16: 2, 17: 1}

    def row(self, k, w=TH):
        k = self.ROWMAP[k]
        return self.rows[0:1, k * TH:k * TH + w]

    def drain_all(self, blk, extra=()):
        s = self

        @blk.gpsimd
        def _(g):
            for q in (s.Q_SY, s.Q_SC, s.Q_GP, s.Q_ST) + tuple(extra):
                q.drain(g)

    def p_consts(self):
        s = self
        with s.nc.Block() as blk:
            @blk.sync
            def _(sy):
                src = AP(s.x_in.tensor, 0, [[TOK, 128], [TOK * 128, DC], [1, TOK]])
                s.Q_SY.issue(sy, s.resid.ap().rearrange("p (c t) -> p c t", c=DC), src)
                s.Q_SY.issue(sy, s.convw_s.ap().rearrange("p (c k) -> p c k", c=CCH),
                             AP(s.convw_in.tensor, 0, [[4, 128], [4 * 128, CCH], [1, 4]]))
                s.Q_SY.issue(sy, s.convb_s.ap().unsqueeze(2),
                             AP(s.convb_in.tensor, 0, [[1, 128], [128, CCH], [1, 1]]), allow_slow_non_contiguous=True)
                s.Q_SY.issue(sy, s.dtb_s[:], s.dtb_in)
                s.Q_SY.issue(sy, s.drep_s.ap().unsqueeze(2),
                             AP(s.drep_in.tensor, 0, [[1, 128], [128, IC], [1, 1]]), allow_slow_non_contiguous=True)
                s.Q_SY.issue(sy, s.nw_s.ap().unsqueeze(2),
                             AP(s.nw_in.tensor, 0, [[1, 128], [128, IC], [1, 1]]), allow_slow_non_contiguous=True)
                s.Q_SY.issue(sy, s.flag_s[:], s.flag_in)

            @blk.scalar
            def _(sc):
                s.Q_SC.issue(sc, s.acol[:], s.alog_in)
                s.Q_SC.drain(sc)
                s.S_AC.inc(sc.activation(s.acol[:], s.acol[:], AF.Exp))

            @blk.gpsimd
            def _(g):
                g.memset(s.ones_b[:], 1.0)
                g.memset(s.ones1[:], 1.0)
                g.memset(s.ones_f[:], 1.0)
                g.memset(s.epscol[:], EPS)
                g.memset(s.zeros64[:], 0.0)
                s.S_GP.inc(g.memset(s.ident_f[:], 1.0))
                g.wait_ge(s.S_GP.h, s.S_GP.n)
                s.S_GP.inc(g.affine_select(s.ident_b[:], s.ones_b[:], [[1, 128]],
                                           OP.is_equal, 0.0, base=0, channel_multiplier=-1))
                s.S_GP.inc(g.affine_select(s.ident_f[:], s.ident_f[:], [[1, 128]],
                                           OP.is_equal, 0.0, base=0, channel_multiplier=-1))

            @blk.vector
            def _(v):
                s.S_AC.wait(v)
                s.S_DV.inc(v.tensor_scalar_mul(s.acol[:], s.acol[:], -1.0))
                s.Q_SY.drain(v)
                v.wait_ge(s.S_DV.h, s.S_DV.n)
                s.S_DV.inc(v.tensor_scalar_mul(s.dtbn_s[:], s.dtb_s[:], -1.0))
        self.gp_idents = s.S_GP.n

    def p_quant(self):
        s = self
        nc = s.nc
        w13 = ([("w1", i) for i in range(HC // 2)] + [("w3", i) for i in range(HC // 2)] +
               [("w1", i) for i in range(HC // 2, HC)] + [("w3", i) for i in range(HC // 2, HC)])

        def w13_src(mat, i):
            t = s.w1s_in.tensor if mat == "w1" else s.w3s_in.tensor
            return AP(t, i * 128 * 256, [[256, 128], [1, 256]])

        def w2_src(r, hhalf):
            return AP(s.w2s_in.tensor, r * 128 * HID + hhalf * (HID // 2),
                      [[HID, 128], [1, HID // 2]])

        with nc.Block() as blk, contextlib.ExitStack() as ar:
            A = lambda nm, sh, dt=F32: ar.enter_context(nc.sbuf_tensor(nm, sh, dt))
            qp = A("qp1b", [128, 2 * 5632])
            qsum = A("qsum", [128, 4])
            qred = A("qred", [128, 4])
            ps_m = ar.enter_context(nc.psum_tensor("ps_m", [128, 512], F32))
            loads = [(m, h, AP(t, h * 5632, [[11264, 128], [1, 5632]]), 5632)
                     for m, t in (("w1", s.w1s_in.tensor), ("w3", s.w3s_in.tensor),
                                  ("w2", s.w2s_in.tensor))
                     for h in range(2)]
            col = {"w1": 0, "w3": 1, "w2": 2}
            ldm, dvm = {}, {}

            @blk.gpsimd
            def _(g):
                s.S_GP.inc(g.memset(qsum[:], 0.0))
            gp_zero = s.S_GP.n

            for k, (mat, i, src, w) in enumerate(loads):
                @blk.sync
                def _(sy, k=k, src=src, w=w):
                    if k - 2 in dvm:
                        sy.wait_ge(s.S_DV.h, dvm[k - 2])
                    _, n = s.Q_SY.issue(sy, qp[:, (k % 2) * 5632:(k % 2) * 5632 + w], src)
                    ldm[k] = (s.Q_SY.sems[(s.Q_SY.i - 1) % 3], n)

                @blk.vector
                def _(v, k=k, mat=mat, w=w):
                    sem, n = ldm[k]
                    v.wait_ge(sem.h, n)
                    if k == 0:
                        v.wait_ge(s.S_GP.h, gp_zero)
                    m = col[mat]
                    v.wait_ge(s.S_DV.h, s.S_DV.n)
                    s.S_DV.inc(v.tensor_reduce(qred[:, m:m + 1],
                                               qp[:, (k % 2) * 5632:(k % 2) * 5632 + w],
                                               AX.X, OP.add, apply_absolute_value=True))
                    v.wait_ge(s.S_DV.h, s.S_DV.n)
                    s.S_DV.inc(v.tensor_tensor(qsum[:, m:m + 1], qsum[:, m:m + 1],
                                               qred[:, m:m + 1], OP.add))
                    dvm[k] = s.S_DV.n

            @blk.tensor
            def _(t):
                t.wait_ge(s.S_DV.h, s.S_DV.n)
                s.S_PE.inc(t.matmul(ps_m[:, 0:4], s.ones_f[:], qsum[:], start=True, stop=True))

            @blk.vector
            def _(v):
                s.S_PE.wait(v)
                s.S_DV.inc(v.tensor_copy(qred[0:1, 0:4], ps_m[0:1, 0:4]))

            @blk.gpsimd
            def _(g):
                g.wait_ge(s.S_DV.h, s.S_DV.n)
                s.Q_GP.issue(g, AP(s.m_in, 0, [[4, 1], [1, 4]]), qred[0:1, 0:4])
                s.Q_GP.drain(g)
                s.S_CC.inc(g.collective_compute(
                    "AllReduce", OP.add, replica_groups=[list(range(NCORES))],
                    ins=[s.m_in.ap().opt()], outs=[s.m_out.ap().opt()]))
                g.wait_ge(s.S_CC.h, s.S_CC.n)
                s.Q_GP.issue(g, s.rows2[0:1, 0:4], AP(s.m_out, 0, [[4, 1], [1, 4]]))
                s.Q_GP.drain(g)
                s.S_GP.inc(g.engine_nop())

            @blk.vector
            def _(v):
                s.S_GP.wait(v)
                s.S_DV.inc(v.tensor_scalar(s.rows2[0:1, 4:8], s.rows2[0:1, 0:4],
                                           1.0 / NW_ELEMS, EPS, OP.mult, OP.max))
                v.wait_ge(s.S_DV.h, s.S_DV.n)
                s.S_DV.inc(v.tensor_scalar_mul(s.rows2[0:1, 8:12], s.rows2[0:1, 4:8], 0.5))
                s.S_DV.inc(v.tensor_scalar_mul(s.rows2[0:1, 12:16], s.rows2[0:1, 4:8], -0.5))

            @blk.tensor
            def _(t):
                t.wait_ge(s.S_DV.h, s.S_DV.n)
                s.S_PE.inc(t.matmul(ps_m[:, 0:12], s.ones1[:], s.rows2[0:1, 4:16],
                                    start=True, stop=True))

            @blk.vector
            def _(v):
                s.S_PE.wait(v)
                s.S_DV.inc(v.tensor_copy(s.qcols[:], ps_m[:, 0:12]))
                v.wait_ge(s.S_DV.h, s.S_DV.n)
                s.S_DV.inc(v.tensor_scalar_mul(s.meansn[:, 0:3], s.qcols[:, 0:3], -1.0))
                s.S_DV.inc(v.tensor_scalar_mul(s.meansn[:, 3:4], s.qcols[:, 2:3], 0.5))

            pcf = ar.enter_context(nc.sbuf_tensor("pcf", [128, 2 * 4256], F32))
            pcb = ar.enter_context(nc.sbuf_tensor("pcb", [128, 2 * 4256], BF16))
            QPC = Dma([s.SEM("qpc%d" % i, 16) for i in range(2)])
            pieces_pc = []
            for dc in range(2):
                for pc in range(2):
                    off = dc * 128 * PROJ + pc * 4256
                    pieces_pc.append((s.ipw_in.tensor, s.pc_in, off, off, PROJ, 4256))
            for k2 in range(2):
                off = k2 * 128 * 4096
                pieces_pc.append((s.opw_in.tensor, s.pc_in, off, 256 * PROJ + off,
                                  4096, 4096))
            pcm = {}
            for k, (srcT, dstT, off, doff, rs, w) in enumerate(pieces_pc):
                pp = k % 2

                @blk.sync
                def _(sy, k=k, srcT=srcT, off=off, rs=rs, w=w, pp=pp):
                    if ("pc_dv", k - 2) in pcm:
                        sy.wait_ge(s.S_DV.h, pcm[("pc_dv", k - 2)])
                    _, n = QPC.issue(sy, pcf.ap()[:, pp * 4256:pp * 4256 + w],
                                     AP(srcT, off, [[rs, 128], [1, w]]))
                    pcm[("pc_ld", k)] = (QPC.sems[(QPC.i - 1) % 2], n)

                @blk.vector
                def _(v, k=k, w=w, pp=pp):
                    sem, n = pcm[("pc_ld", k)]
                    v.wait_ge(sem.h, n)
                    if ("pc_st", k - 2) in pcm:
                        sem2, n2 = pcm[("pc_st", k - 2)]
                        v.wait_ge(sem2.h, n2)
                    v.wait_ge(s.S_DV.h, s.S_DV.n)
                    s.S_DV.inc(v.tensor_copy(pcb[:, pp * 4256:pp * 4256 + w],
                                             pcf[:, pp * 4256:pp * 4256 + w]))
                    pcm[("pc_dv", k)] = s.S_DV.n

                @blk.scalar
                def _(sc, k=k, dstT=dstT, doff=doff, rs=rs, w=w, pp=pp):
                    sc.wait_ge(s.S_DV.h, pcm[("pc_dv", k)])
                    _, n = s.Q_ST.issue(sc, AP(dstT, doff, [[rs, 128], [1, w]]),
                                        pcb.ap()[:, pp * 4256:pp * 4256 + w])
                    pcm[("pc_st", k)] = (s.Q_ST.sems[(s.Q_ST.i - 1) % 2], n)

            self.drain_all(blk, extra=(QPC,))

        with nc.Block() as blk, contextlib.ExitStack() as ar:
            A = lambda nm, sh, dt=F32: ar.enter_context(nc.sbuf_tensor(nm, sh, dt))
            qp = A("qp2", [128, 2 * 256])
            qt = A("qt2", [128, 2 * 256], BF16)
            qa = A("qa2", [128, 256], BF16)
            qasm = A("qasm", [128, 2 * 512], BF16)
            ps_q = ar.enter_context(nc.psum_tensor("ps_q", [128, 1024], BF16))
            ldm, dvm, pem, evm, stm = {}, {}, {}, {}, {}

            for k, (mat, i) in enumerate(w13):
                pp = k % 2

                @blk.sync
                def _(sy, k=k, mat=mat, i=i, pp=pp):
                    if k - 2 in dvm:
                        sy.wait_ge(s.S_DV.h, dvm[k - 2])
                    _, n = s.Q_SY.issue(sy, qp[:, pp * 256:pp * 256 + 256], w13_src(mat, i))
                    ldm[k] = (s.Q_SY.sems[(s.Q_SY.i - 1) % 3], n)

                @blk.vector
                def _(v, k=k, mat=mat, pp=pp):
                    sem, n = ldm[k]
                    v.wait_ge(sem.h, n)
                    m = {"w1": 0, "w3": 1}[mat]
                    if k - 1 in pem:
                        v.wait_ge(s.S_PE.h, pem[k - 1])
                    v.wait_ge(s.S_DV.h, s.S_DV.n)
                    s.S_DV.inc(v.tensor_single_scalar(qa[:], qp[:, pp * 256:pp * 256 + 256],
                                                      s.qcols[:, 4 + m:5 + m], OP.is_gt))
                    v.wait_ge(s.S_DV.h, s.S_DV.n)
                    s.S_DV.inc(v.scalar_tensor_tensor(qt[:, pp * 256:pp * 256 + 256],
                                                      qp[:, pp * 256:pp * 256 + 256],
                                                      s.qcols[:, 8 + m:9 + m], qa[:],
                                                      OP.is_lt, OP.subtract))
                    dvm[k] = s.S_DV.n

                @blk.tensor
                def _(t, k=k, i=i, pp=pp):
                    t.wait_ge(s.S_DV.h, dvm[k])
                    if k == 0:
                        t.wait_ge(s.S_GP.h, self.gp_idents)
                    if i % 2 == 0 and k - 1 in evm:
                        t.wait_ge(s.S_DV.h, evm[k - 1])
                    for dcl in range(2):
                        mm = t.transpose(ps_q[:, dcl * 256 + (i % 2) * 128:
                                              dcl * 256 + (i % 2) * 128 + 128],
                                         qt[:, pp * 256 + dcl * 128:pp * 256 + dcl * 128 + 128],
                                         s.ident_b[:])
                    pem[k] = s.S_PE.inc(mm)

                if i % 2 == 1:
                    @blk.vector
                    def _(v, k=k, i=i):
                        v.wait_ge(s.S_PE.h, pem[k])
                        hp = i // 2
                        half = hp % 2
                        if ("st", half) in stm:
                            sem, n = stm[("st", half)]
                            v.wait_ge(sem.h, n)
                        v.wait_ge(s.S_DV.h, s.S_DV.n)
                        s.S_DV.inc(v.tensor_copy(qasm[:, half * 512:half * 512 + 512], ps_q[:, 0:512]))
                        evm[k] = s.S_DV.n

                    @blk.scalar
                    def _(sc, k=k, mat=mat, i=i):
                        sc.wait_ge(s.S_DV.h, evm[k])
                        hp = i // 2
                        half = hp % 2
                        base = (W1_OFF if mat == "w1" else W3_OFF) + hp * 2 * BLK
                        dst = AP(s.q_in, base, [[512, 128], [256, 2], [1, 256]])
                        _, n = s.Q_ST.issue(sc, dst,
                                            qasm.ap()[:, half * 512:half * 512 + 512]
                                            .rearrange("p (a b) -> p a b", a=2))
                        stm[("st", half)] = (s.Q_ST.sems[(s.Q_ST.i - 1) % 2], n)

                seg = {("w1", HC // 2 - 1): ("q1", 0), ("w3", HC // 2 - 1): ("q3", 0),
                       ("w1", HC - 1): ("q1", 1), ("w3", HC - 1): ("q3", 1)}
                if (mat, i) in seg:
                    which, gb = seg[(mat, i)]

                    @blk.gpsimd
                    def _(g, which=which, gb=gb, mat=mat):
                        for q in s.Q_ST.sems:
                            q.wait(g)
                        src_off = (W1_OFF if mat == "w1" else W3_OFF) + gb * HWH
                        outt = s.q1_out if which == "q1" else s.q3_out
                        s.S_CC.inc(g.collective_compute(
                            "AllGather", OP.bypass, replica_groups=[list(range(NCORES))],
                            ins=[AP(s.q_in, src_off, [[1, HWH]])],
                            outs=[AP(outt, gb * NCORES * HWH, [[1, NCORES * HWH]]).opt()]))
                        setattr(s, "cc_%s%s" % (which, "ab"[gb]), s.S_CC.n)

            self.drain_all(blk)

        with nc.Block() as blk, contextlib.ExitStack() as ar:
            A = lambda nm, sh, dt=F32: ar.enter_context(nc.sbuf_tensor(nm, sh, dt))
            qp = A("qp3", [128, 2816])
            qt0 = A("qt30", [128, 2816], BF16)
            qt1 = A("qt31", [128, 2816], BF16)
            qa = A("qa3", [128, 2816], BF16)
            qasm = A("qasm3", [128, 2 * 256], BF16)
            ps_q = ar.enter_context(nc.psum_tensor("ps_q3", [128, 2048], BF16))
            ldm, dvm, pem, evm, stm = {}, {}, {}, {}, {}
            for hhalf in range(2):
                for r in range(2):
                    @blk.sync
                    def _(sy, r=r, hhalf=hhalf, dvn=None):
                        sy.wait_ge(s.S_DV.h, s.S_DV.n) if s.S_DV.n else None
                        _, n = s.Q_SY.issue(sy, qp[:], w2_src(r, hhalf))
                        ldm[(hhalf, r)] = (s.Q_SY.sems[(s.Q_SY.i - 1) % 3], n)

                    @blk.vector
                    def _(v, r=r, hhalf=hhalf, pen=None):
                        sem, n = ldm[(hhalf, r)]
                        v.wait_ge(sem.h, n)
                        if s.S_PE.n:
                            v.wait_ge(s.S_PE.h, s.S_PE.n)
                        v.wait_ge(s.S_DV.h, s.S_DV.n)
                        s.S_DV.inc(v.tensor_single_scalar(qa[:], qp[:],
                                                          s.qcols[:, 6:7], OP.is_gt))
                        v.wait_ge(s.S_DV.h, s.S_DV.n)
                        qt = qt0 if r == 0 else qt1
                        s.S_DV.inc(v.scalar_tensor_tensor(qt[:], qp[:], s.qcols[:, 10:11],
                                                          qa[:], OP.is_lt, OP.subtract))
                        dvm[(hhalf, r)] = s.S_DV.n

                for sl in range(HP):
                    pp = sl % 2

                    @blk.tensor
                    def _(t, sl=sl, pp=pp, hhalf=hhalf):
                        t.wait_ge(s.S_DV.h, dvm[(hhalf, 1)])
                        if (hhalf, sl - 2) in evm:
                            t.wait_ge(s.S_DV.h, evm[(hhalf, sl - 2)])
                        elif hhalf and (hhalf - 1, HP - 2 + sl) in evm:
                            t.wait_ge(s.S_DV.h, evm[(hhalf - 1, HP - 2 + sl)])
                        mm = t.transpose(ps_q[:, pp * 1024:pp * 1024 + 128],
                                         qt0[:, sl * 128:sl * 128 + 128], s.ident_b[:])
                        mm = t.transpose(ps_q[:, pp * 1024 + 128:pp * 1024 + 256],
                                         qt1[:, sl * 128:sl * 128 + 128], s.ident_b[:])
                        pem[(hhalf, sl)] = s.S_PE.inc(mm)

                    @blk.vector
                    def _(v, sl=sl, pp=pp, hhalf=hhalf):
                        v.wait_ge(s.S_PE.h, pem[(hhalf, sl)])
                        if ("st", pp) in stm:
                            sem, n = stm[("st", pp)]
                            v.wait_ge(sem.h, n)
                        v.wait_ge(s.S_DV.h, s.S_DV.n)
                        s.S_DV.inc(v.tensor_copy(qasm[:, pp * 256:pp * 256 + 256],
                                                 ps_q[:, pp * 1024:pp * 1024 + 256]))
                        evm[(hhalf, sl)] = s.S_DV.n

                    @blk.scalar
                    def _(sc, sl=sl, pp=pp, hhalf=hhalf):
                        sc.wait_ge(s.S_DV.h, evm[(hhalf, sl)])
                        hc = hhalf * HP + sl
                        dst = AP(s.q_in, W2_OFF + (hc // 2) * 2 * BLK + (hc % 2) * 256,
                                 [[512, 128], [1, 256]])
                        _, n = s.Q_ST.issue(sc, dst, qasm[:, pp * 256:pp * 256 + 256])
                        stm[("st", pp)] = (s.Q_ST.sems[(s.Q_ST.i - 1) % 2], n)

            @blk.gpsimd
            def _(g):
                for q in s.Q_ST.sems:
                    q.wait(g)
                s.S_CC.inc(g.collective_compute(
                    "AllGather", OP.bypass, replica_groups=[list(range(NCORES))],
                    ins=[AP(s.q_in, W2_OFF, [[1, W2_SHARD]])],
                    outs=[s.q2_out.ap().opt()]))
                s.cc_w2 = s.S_CC.n
                s.S_CC.inc(g.collective_compute(
                    "AllGather", OP.bypass, replica_groups=[list(range(NCORES))],
                    ins=[AP(s.pc_in, 0, [[1, 256 * PROJ]])],
                    outs=[AP(s.ipw_bf, 0, [[1, D * PROJ]]).opt()]))
                s.cc_ipw = s.S_CC.n

            self.drain_all(blk)

    def p_ffn(self):
        s = self
        nc = s.nc
        with nc.Block() as blk, contextlib.ExitStack() as ar:
            A = lambda nm, sh, dt=F32: ar.enter_context(nc.sbuf_tensor(nm, sh, dt))
            P_ = lambda nm, sh, dt=F32: ar.enter_context(nc.psum_tensor(nm, sh, dt))
            nm0 = "f%d" % s.S_PE.n
            xbf = A(nm0 + "xbf", [128, 2 * TH], BF16)
            x2bf = A(nm0 + "x2bf", [128, 2 * TH], BF16)
            xct = A(nm0 + "xct", [128, 2 * TH])
            xint = A(nm0 + "xint", [128, DC * TH], BF16)
            runmax = A(nm0 + "runmax", [128, TH])
            runmin = A(nm0 + "runmin", [128, TH])
            hmem = A(nm0 + "hmem", [128, HC * TH], BF16)
            tmpq = A(nm0 + "tmpq", [128, 2 * TH])
            t2b = A(nm0 + "t2b", [128, 2 * TH])
            colsb = A(nm0 + "colsb", [128, 8])
            combsb = A(nm0 + "combsb", [128, TH])
            invsqsb = A(nm0 + "invsqsb", [128, TH])
            shsb = A(nm0 + "shsb", [128, TH])
            wslab = A(nm0 + "wslab", [128, 2 * 2 * 4096], BF16)
            pb = [P_(nm0 + "pb%d" % i, [128, 512]) for i in range(6)]

            QW = Dma([s.SEM(nm0 + "qw%d" % i, 16) for i in range(3)])

            prev_peTh = [0]
            for th in range(2):
                T0 = th * TH
                mk = {}

                for dc in range(DC):
                    pp = dc % 2

                    @blk.scalar
                    def _(sc, dc=dc, pp=pp):
                        if ("pe_st", dc - 2) in mk:
                            sc.wait_ge(s.S_PE.h, mk[("pe_st", dc - 2)])
                        s.S_AC.inc(sc.activation(x2bf[:, pp * TH:pp * TH + TH],
                                                 s.resid[:, dc * TOK + T0:dc * TOK + T0 + TH],
                                                 AF.Square))
                        mk[("ac_sq", dc)] = s.S_AC.n

                    @blk.vector
                    def _(v, dc=dc, pp=pp):
                        if ("pe_st", dc - 2) in mk:
                            v.wait_ge(s.S_PE.h, mk[("pe_st", dc - 2)])
                        v.wait_ge(s.S_DV.h, s.S_DV.n)
                        s.S_DV.inc(v.tensor_copy(xbf[:, pp * TH:pp * TH + TH],
                                                 s.resid[:, dc * TOK + T0:dc * TOK + T0 + TH]))
                        mk[("dv_cp", dc)] = s.S_DV.n

                    @blk.vector
                    def _(v, dc=dc):
                        src = s.resid[:, dc * TOK + T0:dc * TOK + T0 + TH]
                        if dc == 0 and prev_peTh[0]:
                            v.wait_ge(s.S_PE.h, prev_peTh[0])
                        v.wait_ge(s.S_DV.h, s.S_DV.n)
                        if dc == 0:
                            s.S_DV.inc(v.tensor_copy(runmax[:], src))
                            s.S_DV.inc(v.tensor_copy(runmin[:], src))
                        else:
                            s.S_DV.inc(v.tensor_tensor(runmax[:], runmax[:], src, OP.max))
                            s.S_DV.inc(v.tensor_tensor(runmin[:], runmin[:], src, OP.min))

                    @blk.tensor
                    def _(t, dc=dc, pp=pp):
                        if dc == 0:
                            t.wait_ge(s.S_DV.h, s.S_DV.n)
                            t.wait_ge(s.S_GP.h, s.S_GP.n)
                        t.wait_ge(s.S_DV.h, mk[("dv_cp", dc)])
                        t.wait_ge(s.S_AC.h, mk[("ac_sq", dc)])
                        t.matmul(pb[0][:], s.ones_b[:], xbf[:, pp * TH:pp * TH + TH],
                                 start=(dc == 0), stop=(dc == DC - 1))
                        mm = t.matmul(pb[1][:], s.ones_b[:], x2bf[:, pp * TH:pp * TH + TH],
                                      start=(dc == 0), stop=(dc == DC - 1))
                        mk[("pe_st", dc)] = s.S_PE.inc(mm)

                @blk.vector
                def _(v):
                    v.wait_ge(s.S_PE.h, mk[("pe_st", DC - 1)])
                    v.wait_ge(s.S_DV.h, s.S_DV.n)
                    s.S_DV.inc(v.tensor_scalar_mul(s.row(0), pb[0][0:1, :], 1.0 / D))
                    s.S_DV.inc(v.tensor_scalar_mul(s.row(1), pb[1][0:1, :], 1.0 / D))
                    s.S_DV.inc(v.tensor_scalar_mul(s.row(11), pb[0][0:1, :], 1.0 / D))
                    mk["r01"] = s.S_DV.n

                @blk.scalar
                def _(sc):
                    sc.wait_ge(s.S_DV.h, mk["r01"])
                    s.S_AC.inc(sc.activation(s.row(2), s.row(0), AF.Square))
                    mk["r2"] = s.S_AC.n

                @blk.vector
                def _(v):
                    v.wait_ge(s.S_AC.h, mk["r2"])
                    v.wait_ge(s.S_DV.h, s.S_DV.n)
                    s.S_DV.inc(v.tensor_tensor(s.row(3), s.row(1), s.row(2), OP.subtract))
                    mk["r3"] = s.S_DV.n

                @blk.scalar
                def _(sc):
                    sc.wait_ge(s.S_DV.h, mk["r3"])
                    s.S_AC.inc(sc.activation(s.row(4), s.row(3), AF.Sqrt, bias=s.epscol[0:1, 0:1]))
                    mk["r4"] = s.S_AC.n

                @blk.vector
                def _(v):
                    v.wait_ge(s.S_AC.h, mk["r4"])
                    v.wait_ge(s.S_DV.h, s.S_DV.n)
                    s.S_DV.inc(v.reciprocal(s.row(5), s.row(4)))
                    mk["r5"] = s.S_DV.n

                if K_FFN_PARTS() < 2:
                    continue
                @blk.vector
                def _(v):
                    v.wait_ge(s.S_DV.h, s.S_DV.n)
                    s.S_DV.inc(v.tensor_scalar_mul(runmin[:], runmin[:], -1.0))
                    mk["xneg"] = s.S_DV.n

                @blk.tensor
                def _(t):
                    t.wait_ge(s.S_DV.h, mk["xneg"])
                    for i in range(4):
                        t.transpose(pb[4][:, i * 128:i * 128 + 128],
                                    runmax[:, i * 128:i * 128 + 128], s.ident_f[:])
                    for i in range(4):
                        mm = t.transpose(pb[2][:, i * 128:i * 128 + 128],
                                         runmin[:, i * 128:i * 128 + 128], s.ident_f[:])
                    mk["peTx"] = s.S_PE.inc(mm)

                @blk.vector
                def _(v):
                    v.wait_ge(s.S_PE.h, mk["peTx"])
                    for i in range(4):
                        v.wait_ge(s.S_DV.h, s.S_DV.n)
                        s.S_DV.inc(v.tensor_reduce(colsb[:, i:i + 1],
                                                   pb[4][:, i * 128:i * 128 + 128],
                                                   AX.X, OP.max))
                    for i in range(4):
                        v.wait_ge(s.S_DV.h, s.S_DV.n)
                        s.S_DV.inc(v.tensor_reduce(colsb[:, 4 + i:5 + i],
                                                   pb[2][:, i * 128:i * 128 + 128],
                                                   AX.X, OP.max))
                    mk["dvRx"] = s.S_DV.n

                @blk.tensor
                def _(t):
                    t.wait_ge(s.S_DV.h, mk["dvRx"])
                    for i in range(4):
                        t.transpose(pb[5][0:1, i * 128:i * 128 + 128],
                                    colsb[:, i:i + 1], s.ident_f[:])
                    for i in range(4):
                        mm = t.transpose(pb[3][0:1, i * 128:i * 128 + 128],
                                         colsb[:, 4 + i:5 + i], s.ident_f[:])
                    mk["peRx"] = s.S_PE.inc(mm)

                @blk.vector
                def _(v):
                    v.wait_ge(s.S_PE.h, mk["peRx"])
                    v.wait_ge(s.S_DV.h, s.S_DV.n)
                    s.S_DV.inc(v.tensor_tensor(s.row(6), pb[5][0:1, 0:TH], s.row(11),
                                               OP.subtract))
                    v.wait_ge(s.S_DV.h, s.S_DV.n)
                    s.S_DV.inc(v.tensor_tensor(s.row(2), pb[3][0:1, 0:TH], s.row(11), OP.add))
                    v.wait_ge(s.S_DV.h, s.S_DV.n)
                    s.S_DV.inc(v.tensor_tensor(s.row(6), s.row(6), s.row(2), OP.max))
                    v.wait_ge(s.S_DV.h, s.S_DV.n)
                    s.S_DV.inc(v.tensor_tensor(s.row(7), s.row(6), s.row(5), OP.mult))
                    v.wait_ge(s.S_DV.h, s.S_DV.n)
                    s.S_DV.inc(v.tensor_scalar_max(s.row(8), s.row(7), EPS))
                    v.wait_ge(s.S_DV.h, s.S_DV.n)
                    s.S_DV.inc(v.reciprocal(s.row(9), s.row(8)))
                    v.wait_ge(s.S_DV.h, s.S_DV.n)
                    s.S_DV.inc(v.tensor_scalar_mul(s.row(9), s.row(9), 127.0))
                    v.wait_ge(s.S_DV.h, s.S_DV.n)
                    s.S_DV.inc(v.tensor_tensor(s.row(10), s.row(5), s.row(9), OP.mult))
                    s.S_DV.inc(v.tensor_scalar_mul(s.row(11), s.row(8), 1.0 / 127.0))
                    mk["rq"] = s.S_DV.n

                @blk.tensor
                def _(t):
                    t.wait_ge(s.S_DV.h, mk["rq"])
                    mm = t.matmul(pb[1][:], s.ones1[:], s.row(10), start=True, stop=True)
                    mk["pe_comb"] = s.S_PE.inc(mm)

                @blk.vector
                def _(v):
                    v.wait_ge(s.S_PE.h, mk["pe_comb"])
                    v.wait_ge(s.S_DV.h, s.S_DV.n)
                    s.S_DV.inc(v.tensor_copy(combsb[:], pb[1][:]))
                    mk["dv_comb"] = s.S_DV.n

                @blk.tensor
                def _(t):
                    t.wait_ge(s.S_DV.h, mk["dv_comb"])
                    mm = t.matmul(pb[1][:], s.ones1[:], s.row(11), start=True, stop=True)
                    mk["pe_invsq"] = s.S_PE.inc(mm)

                @blk.vector
                def _(v):
                    v.wait_ge(s.S_PE.h, mk["pe_invsq"])
                    v.wait_ge(s.S_DV.h, s.S_DV.n)
                    s.S_DV.inc(v.tensor_copy(invsqsb[:], pb[1][:]))
                    mk["dv_invsq"] = s.S_DV.n

                if K_FFN_PARTS() < 3:
                    continue
                for dc in range(DC):
                    pp = dc % 2

                    @blk.vector
                    def _(v, dc=dc, pp=pp):
                        if dc == 0:
                            v.wait_ge(s.S_DV.h, mk["dv_comb"])
                        sl = xct[:, pp * TH:pp * TH + TH]
                        v.wait_ge(s.S_DV.h, s.S_DV.n)
                        s.S_DV.inc(v.scalar_tensor_tensor(
                            sl, pb[0][:], -1.0 / D,
                            s.resid[:, dc * TOK + T0:dc * TOK + T0 + TH], OP.mult, OP.add))
                        v.wait_ge(s.S_DV.h, s.S_DV.n)
                        s.S_DV.inc(v.tensor_tensor(sl, sl, combsb[:], OP.mult))
                        v.wait_ge(s.S_DV.h, s.S_DV.n)
                        s.S_DV.inc(v.tensor_scalar(xint[:, dc * TH:dc * TH + TH], sl,
                                                   MAGIC, MAGIC, OP.add, OP.subtract))
                        mk[("dv_xi", dc)] = s.S_DV.n

                if K_FFN_PARTS() < 4:
                    continue
                def slab_src(mat, hp):
                    t_ = s.q1_out if mat == "w1" else s.q3_out
                    ga, hpl = divmod(hp, HP // 2)
                    base = ga * NCORES * HWH + hpl * 2 * BLK
                    return AP(t_, base, [[512, 128], [HWH, 8], [1, 512]])

                for hp in range(HP):
                    sp = hp % 2

                    @blk.sync
                    def _(sy, hp=hp, sp=sp, pen=None):
                        if ("pe_b", 2 * (hp - 2) + 1) in mk:
                            sy.wait_ge(s.S_PE.h, mk[("pe_b", 2 * (hp - 2) + 1)])
                        if hp <= 1:
                            sy.wait_ge(s.S_PE.h, s.S_PE.n)
                        if hp == 0:
                            sy.wait_ge(s.S_CC.h, s.cc_q3a)
                        if hp == HP // 2:
                            sy.wait_ge(s.S_CC.h, s.cc_q3b)
                        marks = []
                        for mi, mat in enumerate(("w1", "w3")):
                            off = sp * 8192 + mi * 4096
                            dst = AP(wslab, off, [[16384, 128], [512, 8], [1, 512]])
                            _, n = QW.issue(sy, dst, slab_src(mat, hp))
                            marks.append((QW.sems[(QW.i - 1) % 3], n))
                        mk[("ld_w", hp)] = marks

                    for sub in range(2):
                        hc = 2 * hp + sub
                        bq = 2 + 2 * (hc % 2)

                        @blk.tensor
                        def _(t, hp=hp, sp=sp, sub=sub, hc=hc, bq=bq):
                            for sem, n in mk[("ld_w", hp)]:
                                t.wait_ge(sem.h, n)
                            if hc == 0:
                                t.wait_ge(s.S_DV.h, mk[("dv_xi", DC - 1)])
                            if ("ac_t2", hc - 2) in mk:
                                t.wait_ge(s.S_AC.h, mk[("ac_t2", hc - 2)])
                                t.wait_ge(s.S_DV.h, mk[("dv_h", hc - 2)])
                            for dc in range(DC):
                                t.matmul(pb[bq][:],
                                         wslab[:, sp * 8192 + dc * 256 + sub * 128:
                                               sp * 8192 + dc * 256 + sub * 128 + 128],
                                         xint[:, dc * TH:dc * TH + TH],
                                         start=(dc == 0), stop=(dc == DC - 1))
                            for dc in range(DC):
                                mm = t.matmul(pb[bq + 1][:],
                                              wslab[:, sp * 8192 + 4096 + dc * 256 + sub * 128:
                                                    sp * 8192 + 4096 + dc * 256 + sub * 128 + 128],
                                              xint[:, dc * TH:dc * TH + TH],
                                              start=(dc == 0), stop=(dc == DC - 1))
                            mk[("pe_b", hc)] = s.S_PE.inc(mm)

                        @blk.vector
                        def _(v, hc=hc, bq=bq):
                            v.wait_ge(s.S_PE.h, mk[("pe_b", hc)])
                            v.wait_ge(s.S_DV.h, s.S_DV.n)
                            s.S_DV.inc(v.scalar_tensor_tensor(
                                pb[bq][:], pb[bq][:], s.meansn[:, 0:1], invsqsb[:],
                                OP.mult, OP.mult))
                            mk[("dv_b1", hc)] = s.S_DV.n

                        @blk.scalar
                        def _(sc, hc=hc, bq=bq):
                            sc.wait_ge(s.S_DV.h, mk[("dv_b1", hc)])
                            pp = hc % 2
                            s.S_AC.inc(sc.activation(t2b[:, pp * TH:pp * TH + TH],
                                                     pb[bq][:], AF.Silu))
                            mk[("ac_t2", hc)] = s.S_AC.n

                        @blk.vector
                        def _(v, hc=hc, bq=bq):
                            v.wait_ge(s.S_AC.h, mk[("ac_t2", hc)])
                            pp = hc % 2
                            v.wait_ge(s.S_DV.h, s.S_DV.n)
                            s.S_DV.inc(v.tensor_tensor(hmem[:, hc * TH:hc * TH + TH],
                                                       t2b[:, pp * TH:pp * TH + TH],
                                                       pb[bq + 1][:], OP.mult))
                            mk[("dv_h", hc)] = s.S_DV.n

                        @blk.vector
                        def _(v, hc=hc):
                            v.wait_ge(s.S_DV.h, s.S_DV.n)
                            hsl = hmem[:, hc * TH:hc * TH + TH]
                            if hc == 0:
                                s.S_DV.inc(v.tensor_copy(runmax[:], hsl))
                                s.S_DV.inc(v.tensor_copy(runmin[:], hsl))
                            else:
                                s.S_DV.inc(v.tensor_tensor(runmax[:], runmax[:], hsl, OP.max))
                                s.S_DV.inc(v.tensor_tensor(runmin[:], runmin[:], hsl, OP.min))
                            mk[("gp_hf", hc)] = s.S_DV.n

                if K_FFN_PARTS() < 5:
                    continue
                @blk.vector
                def _(v):
                    v.wait_ge(s.S_DV.h, s.S_DV.n)
                    s.S_DV.inc(v.scalar_tensor_tensor(runmax[:], runmin[:], -1.0,
                                                      runmax[:], OP.mult, OP.max))
                    mk["hboth"] = s.S_DV.n

                @blk.tensor
                def _(t):
                    t.wait_ge(s.S_AC.h, mk[("ac_t2", HC - 1)])
                    t.wait_ge(s.S_DV.h, mk["hboth"])
                    for i in range(4):
                        mm = t.transpose(pb[4][:, i * 128:i * 128 + 128],
                                         runmax[:, i * 128:i * 128 + 128], s.ident_f[:])
                    mk["peTh"] = s.S_PE.inc(mm)
                    prev_peTh[0] = mk["peTh"]

                @blk.vector
                def _(v):
                    v.wait_ge(s.S_PE.h, mk["peTh"])
                    for i in range(4):
                        v.wait_ge(s.S_DV.h, s.S_DV.n)
                        s.S_DV.inc(v.tensor_reduce(colsb[:, i:i + 1],
                                                   pb[4][:, i * 128:i * 128 + 128],
                                                   AX.X, OP.max))
                    mk["dvRh"] = s.S_DV.n

                @blk.tensor
                def _(t):
                    t.wait_ge(s.S_DV.h, mk["dvRh"])
                    for i in range(4):
                        mm = t.transpose(pb[5][0:1, i * 128:i * 128 + 128],
                                         colsb[:, i:i + 1], s.ident_f[:])
                    mk["peRh"] = s.S_PE.inc(mm)

                @blk.vector
                def _(v):
                    v.wait_ge(s.S_PE.h, mk["peRh"])
                    v.wait_ge(s.S_DV.h, s.S_DV.n)
                    s.S_DV.inc(v.tensor_scalar_mul(s.row(5), s.row(11), s.qcols[0:1, 1:2]))
                    v.wait_ge(s.S_DV.h, s.S_DV.n)
                    s.S_DV.inc(v.tensor_tensor(s.row(13), pb[5][0:1, 0:TH], s.row(5), OP.mult))
                    v.wait_ge(s.S_DV.h, s.S_DV.n)
                    s.S_DV.inc(v.tensor_scalar_max(s.row(14), s.row(13), EPS))
                    v.wait_ge(s.S_DV.h, s.S_DV.n)
                    s.S_DV.inc(v.reciprocal(s.row(15), s.row(14)))
                    v.wait_ge(s.S_DV.h, s.S_DV.n)
                    s.S_DV.inc(v.tensor_scalar_mul(s.row(15), s.row(15), 127.0))
                    v.wait_ge(s.S_DV.h, s.S_DV.n)
                    s.S_DV.inc(v.tensor_tensor(s.row(16), s.row(15), s.row(5), OP.mult))
                    s.S_DV.inc(v.tensor_scalar_mul(s.row(17), s.row(14), 1.0 / 127.0))
                    mk["rh"] = s.S_DV.n

                @blk.tensor
                def _(t):
                    t.wait_ge(s.S_DV.h, mk["rh"])
                    mm = t.matmul(pb[1][:], s.ones1[:], s.row(16), start=True, stop=True)
                    mk["pe_sh"] = s.S_PE.inc(mm)

                @blk.vector
                def _(v):
                    v.wait_ge(s.S_PE.h, mk["pe_sh"])
                    v.wait_ge(s.S_DV.h, s.S_DV.n)
                    s.S_DV.inc(v.tensor_copy(shsb[:], pb[1][:]))
                    mk["dv_sh"] = s.S_DV.n

                @blk.tensor
                def _(t):
                    t.wait_ge(s.S_DV.h, mk["dv_sh"])
                    mm = t.matmul(pb[1][:], s.ones1[:], s.row(17), start=True, stop=True)
                    mk["pe_ish"] = s.S_PE.inc(mm)

                @blk.vector
                def _(v):
                    v.wait_ge(s.S_PE.h, mk["pe_ish"])
                    v.wait_ge(s.S_DV.h, s.S_DV.n)
                    s.S_DV.inc(v.tensor_copy(invsqsb[:], pb[1][:]))
                    mk["dv_ish"] = s.S_DV.n

                if K_FFN_PARTS() < 6:
                    continue
                for hq in range(HC // 2):
                    @blk.vector
                    def _(v, hq=hq):
                        if hq == 0:
                            v.wait_ge(s.S_DV.h, mk["dv_sh"])
                        v.wait_ge(s.S_DV.h, s.S_DV.n)
                        shrep = shsb.ap().unsqueeze(1).broadcast_to([128, 2, TH])
                        s.S_DV.inc(v.tensor_tensor(
                            tmpq.ap().rearrange("p (a b) -> p a b", a=2),
                            hmem.ap()[:, hq * 2 * TH:hq * 2 * TH + 2 * TH]
                            .rearrange("p (a b) -> p a b", a=2), shrep, OP.mult))
                        v.wait_ge(s.S_DV.h, s.S_DV.n)
                        s.S_DV.inc(v.tensor_scalar(hmem[:, hq * 2 * TH:hq * 2 * TH + 2 * TH],
                                                   tmpq[:],
                                                   MAGIC, MAGIC, OP.add, OP.subtract))
                        mk[("dv_hi", 2 * hq)] = s.S_DV.n
                        mk[("dv_hi", 2 * hq + 1)] = s.S_DV.n

                if K_FFN_PARTS() < 7:
                    continue
                for dq in range(NCORES):
                    bk = 2 + 2 * (dq % 2)
                    for h2 in range(2):

                        @blk.sync
                        def _(sy, dq=dq, h2=h2):
                            if dq == 0:
                                sy.wait_ge(s.S_PE.h, mk[("pe_b", HC - 1)])
                                sy.wait_ge(s.S_CC.h, s.cc_w2)
                            if ("pe_w2", dq - 1, h2) in mk:
                                sy.wait_ge(s.S_PE.h, mk[("pe_w2", dq - 1, h2)])
                            src = AP(s.q2_out, dq * W2_SHARD + h2 * HP * BLK,
                                     [[512, 128], [2 * BLK, HP // 2], [1, 512]])
                            dst = AP(wslab, h2 * HP * 256,
                                     [[16384, 128], [512, HP // 2], [1, 512]])
                            _, n = QW.issue(sy, dst, src)
                            mk[("ld_w2", dq, h2)] = (QW.sems[(QW.i - 1) % 3], n)

                        @blk.tensor
                        def _(t, dq=dq, h2=h2, bk=bk):
                            sem, n = mk[("ld_w2", dq, h2)]
                            t.wait_ge(sem.h, n)
                            if dq == 0:
                                t.wait_ge(s.S_DV.h, mk[("dv_hi", HP - 1 if h2 == 0 else HC - 1)])
                            if h2 == 0 and ("dv_y2", 2 * (dq - 2) + 1) in mk:
                                t.wait_ge(s.S_DV.h, mk[("dv_y2", 2 * (dq - 2) + 1)])
                            for d2 in range(2):
                                for hl in range(HP):
                                    hcc = h2 * HP + hl
                                    mm = t.matmul(pb[bk + d2][:],
                                                  wslab[:, h2 * HP * 256 + hl * 256 + d2 * 128:
                                                        h2 * HP * 256 + hl * 256 + d2 * 128 + 128],
                                                  hmem[:, hcc * TH:hcc * TH + TH],
                                                  start=(hcc == 0), stop=(hcc == HC - 1))
                            mk[("pe_w2", dq, h2)] = s.S_PE.inc(mm)

                    for d2 in range(2):
                        dco = dq * 2 + d2

                        @blk.vector
                        def _(v, dq=dq, d2=d2, dco=dco, bk=bk):
                            v.wait_ge(s.S_PE.h, mk[("pe_w2", dq, 1)])
                            v.wait_ge(s.S_DV.h, s.S_DV.n)
                            s.S_DV.inc(v.scalar_tensor_tensor(
                                pb[bk + d2][:], pb[bk + d2][:], s.meansn[:, 3:4], invsqsb[:],
                                OP.mult, OP.mult))
                            v.wait_ge(s.S_DV.h, s.S_DV.n)
                            s.S_DV.inc(v.tensor_tensor(
                                s.resid[:, dco * TOK + T0:dco * TOK + T0 + TH],
                                s.resid[:, dco * TOK + T0:dco * TOK + T0 + TH],
                                pb[bk + d2][:], OP.add))
                            mk[("dv_y2", dco)] = s.S_DV.n

            self.drain_all(blk, extra=(QW,))

    def p_proj_conv(self):
        s = self
        nc = s.nc
        with nc.Block() as blk, contextlib.ExitStack() as ar:
            A = lambda nm, sh, dt=F32: ar.enter_context(nc.sbuf_tensor(nm, sh, dt))
            P_ = lambda nm, sh, dt=F32: ar.enter_context(nc.psum_tensor(nm, sh, dt))
            xln = A("xln", [128, DC * TOK], BF16)
            xbf = A("pxbf", [128, TH], BF16)
            x2bf = xbf
            xct = A("pxct", [128, 2 * TH])
            zt = A("pzt", [128, TOK], BF16)
            hstage = A("hstage", [128, CCH * 3], BF16)
            ipslab = A("ipslab", [128, 2 * DC * 256], BF16)
            acc = xct
            pj0 = P_("pj0", [128, 512])
            pj1 = P_("pj1", [128, 512])
            pot = [P_("pot%d" % i, [128, TOK]) for i in range(2)]
            QIP = Dma([s.SEM("qip%d" % i, 16) for i in range(2)])
            mk = {}

            for tt in range(2):
                T0 = tt * TH
                for dc in range(DC):
                    pp = dc % 2

                    @blk.vector
                    def _(v, dc=dc, T0=T0, tt=tt):
                        if ("pe_st2", tt, dc - 1) in mk:
                            v.wait_ge(s.S_PE.h, mk[("pe_st2", tt, dc - 1)])
                        v.wait_ge(s.S_DV.h, s.S_DV.n)
                        s.S_DV.inc(v.tensor_copy(xbf[:],
                                                 s.resid[:, dc * TOK + T0:dc * TOK + T0 + TH]))
                        mk[("dv_cp", tt, dc)] = s.S_DV.n

                    @blk.tensor
                    def _(t, dc=dc, tt=tt):
                        if dc == 0:
                            t.wait_ge(s.S_DV.h, s.S_DV.n)
                        t.wait_ge(s.S_DV.h, mk[("dv_cp", tt, dc)])
                        mm = t.matmul(pj0[:], s.ones_b[:], xbf[:],
                                      start=(dc == 0), stop=(dc == DC - 1))
                        mk[("pe_st1", tt, dc)] = s.S_PE.inc(mm)

                    @blk.scalar
                    def _(sc, dc=dc, T0=T0, tt=tt):
                        sc.wait_ge(s.S_PE.h, mk[("pe_st1", tt, dc)])
                        s.S_AC.inc(sc.activation(x2bf[:],
                                                 s.resid[:, dc * TOK + T0:dc * TOK + T0 + TH],
                                                 AF.Square))
                        mk[("ac_sq", tt, dc)] = s.S_AC.n

                    @blk.tensor
                    def _(t, dc=dc, tt=tt):
                        t.wait_ge(s.S_AC.h, mk[("ac_sq", tt, dc)])
                        mm = t.matmul(pj1[:], s.ones_b[:], x2bf[:],
                                      start=(dc == 0), stop=(dc == DC - 1))
                        mk[("pe_st2", tt, dc)] = s.S_PE.inc(mm)
                        mk[("pe_st", tt, dc)] = s.S_PE.n

                @blk.vector
                def _(v, tt=tt):
                    v.wait_ge(s.S_PE.h, mk[("pe_st", tt, DC - 1)])
                    v.wait_ge(s.S_DV.h, s.S_DV.n)
                    s.S_DV.inc(v.tensor_scalar_mul(s.row(0), pj0[0:1, :], 1.0 / D))
                    s.S_DV.inc(v.tensor_scalar_mul(s.row(1), pj1[0:1, :], 1.0 / D))
                    mk["r01"] = s.S_DV.n

                @blk.scalar
                def _(sc, tt=tt):
                    sc.wait_ge(s.S_DV.h, mk["r01"])
                    s.S_AC.inc(sc.activation(s.row(2), s.row(0), AF.Square))
                    mk["r2"] = s.S_AC.n

                @blk.vector
                def _(v, tt=tt):
                    v.wait_ge(s.S_AC.h, mk["r2"])
                    v.wait_ge(s.S_DV.h, s.S_DV.n)
                    s.S_DV.inc(v.tensor_tensor(s.row(3), s.row(1), s.row(2), OP.subtract))
                    mk["r3"] = s.S_DV.n

                @blk.scalar
                def _(sc, tt=tt):
                    sc.wait_ge(s.S_DV.h, mk["r3"])
                    s.S_AC.inc(sc.activation(s.row(4), s.row(3), AF.Sqrt, bias=s.epscol[0:1, 0:1]))
                    mk["r4"] = s.S_AC.n

                @blk.vector
                def _(v, tt=tt):
                    v.wait_ge(s.S_AC.h, mk["r4"])
                    v.wait_ge(s.S_DV.h, s.S_DV.n)
                    s.S_DV.inc(v.reciprocal(s.row(5), s.row(4)))
                    mk["r5"] = s.S_DV.n

                @blk.tensor
                def _(t, tt=tt):
                    t.wait_ge(s.S_DV.h, mk["r5"])
                    mm = t.matmul(pj1[:], s.ones1[:], s.row(5), start=True, stop=True)
                    mk[("pe_rs", tt)] = s.S_PE.inc(mm)

                for dc in range(DC):
                    pp = dc % 2

                    @blk.vector
                    def _(v, dc=dc, pp=pp, T0=T0, tt=tt):
                        if dc == 0:
                            v.wait_ge(s.S_PE.h, mk[("pe_rs", tt)])
                        v.wait_ge(s.S_DV.h, s.S_DV.n)
                        sl = xct[:, pp * TH:pp * TH + TH]
                        s.S_DV.inc(v.scalar_tensor_tensor(
                            sl, pj0[:], -1.0 / D,
                            s.resid[:, dc * TOK + T0:dc * TOK + T0 + TH], OP.mult, OP.add))
                        v.wait_ge(s.S_DV.h, s.S_DV.n)
                        s.S_DV.inc(v.tensor_tensor(xln[:, dc * TOK + T0:dc * TOK + T0 + TH],
                                                   sl, pj1[:], OP.mult))
                        mk[("dv_xl", tt, dc)] = s.S_DV.n

            for f in range(67):
                wf = 128 if f < 66 else 64
                pp = f % 2
                fp, sub = f // 2, f % 2
                wslab = 256 if fp < 33 else 64
                sp = fp % 2

                if sub == 0:
                    @blk.sync
                    def _(sy, f=f, fp=fp, wslab=wslab, sp=sp):
                        if f == 0:
                            sy.wait_ge(s.S_DV.h, mk[("dv_xl", 1, DC - 1)])
                            sy.wait_ge(s.S_CC.h, s.cc_ipw)
                        if ("pe_ip", 2 * (fp - 2) + 1) in mk:
                            sy.wait_ge(s.S_PE.h, mk[("pe_ip", 2 * (fp - 2) + 1)])
                        elif ("pe_ip", 2 * (fp - 2)) in mk:
                            sy.wait_ge(s.S_PE.h, mk[("pe_ip", 2 * (fp - 2))])
                        src = AP(s.ipw_bf, fp * 256,
                                 [[PROJ, 128], [PROJ * 128, DC], [1, wslab]])
                        dst = AP(ipslab, sp * DC * 256,
                                 [[2 * DC * 256, 128], [wslab, DC], [1, wslab]])
                        _, n = QIP.issue(sy, dst, src)
                        mk[("ld_ip", fp)] = (QIP.sems[(QIP.i - 1) % 2], n)

                @blk.tensor
                def _(t, f=f, wf=wf, pp=pp, fp=fp, sub=sub, wslab=wslab, sp=sp):
                    sem, n = mk[("ld_ip", fp)]
                    t.wait_ge(sem.h, n)
                    if ("ev_ip", f - 2) in mk:
                        eng, v = mk[("ev_ip", f - 2)]
                        t.wait_ge(eng, v)
                    for tt in range(2):
                        for dc in range(DC):
                            mm = t.matmul(pot[pp][0:wf, tt * TH:tt * TH + TH],
                                          ipslab[:, sp * DC * 256 + dc * wslab + sub * 128:
                                                sp * DC * 256 + dc * wslab + sub * 128 + wf],
                                          xln[:, dc * TOK + tt * TH:dc * TOK + tt * TH + TH],
                                          start=(dc == 0), stop=(dc == DC - 1))
                    mk[("pe_ip", f)] = s.S_PE.inc(mm)

                if f < IC:
                    @blk.scalar
                    def _(sc, f=f, pp=pp):
                        sc.wait_ge(s.S_PE.h, mk[("pe_ip", f)])
                        if ("st_z", f - 2) in mk:
                            sem, n = mk[("st_z", f - 2)]
                            sc.wait_ge(sem.h, n)
                        if ("st_z", f - 1) in mk:
                            sem, n = mk[("st_z", f - 1)]
                            sc.wait_ge(sem.h, n)
                        s.S_AC.inc(sc.activation(zt[:], pot[pp][:], AF.Silu, scale=0.25))
                        mk[("ev_ip", f)] = (s.S_AC.h, s.S_AC.n)

                    @blk.scalar
                    def _(sc, f=f, pp=pp):
                        eng, v = mk[("ev_ip", f)]
                        sc.wait_ge(eng, v)
                        dst = AP(s.z_spill, f * 128 * TOK, [[TOK, 128], [1, TOK]])
                        _, n = s.Q_ST.issue(sc, dst, zt[:])
                        mk[("st_z", f)] = (s.Q_ST.sems[(s.Q_ST.i - 1) % 2], n)
                elif f < 66:
                    @blk.vector
                    def _(v, f=f, pp=pp):
                        v.wait_ge(s.S_PE.h, mk[("pe_ip", f)])
                        v.wait_ge(s.S_DV.h, s.S_DV.n)
                        c = f - IC
                        s.S_DV.inc(v.tensor_scalar_mul(
                            s.xbc[:, c * XW + 3:c * XW + 3 + TOK], pot[pp][:], 0.5))
                        v.wait_ge(s.S_DV.h, s.S_DV.n)
                        s.S_DV.inc(v.tensor_copy(hstage[:, c * 3:c * 3 + 3],
                                                 s.xbc[:, c * XW + 1024:c * XW + 1027]))
                        mk[("ev_ip", f)] = (s.S_DV.h, s.S_DV.n)

                    @blk.vector
                    def _(v, f=f):
                        c = f - IC
                        base = c * XW
                        if ("ac_cvm", c - 1) in mk:
                            v.wait_ge(s.S_AC.h, mk[("ac_cvm", c - 1)])
                        v.wait_ge(s.S_DV.h, s.S_DV.n)
                        s.S_DV.inc(v.tensor_scalar_mul(acc[:, 0:1021],
                                                       s.xbc[:, base + 3:base + 3 + 1021],
                                                       s.convw_s[:, 4 * c:4 * c + 1]))
                        for k in range(1, 4):
                            v.wait_ge(s.S_DV.h, s.S_DV.n)
                            s.S_DV.inc(v.scalar_tensor_tensor(
                                acc[:, 0:1021], s.xbc[:, base + 3 + k:base + 3 + k + 1021],
                                s.convw_s[:, 4 * c + k:4 * c + k + 1], acc[:, 0:1021],
                                OP.mult, OP.add))
                        mk[("dv_cvm", c)] = s.S_DV.n

                    @blk.scalar
                    def _(sc, f=f):
                        c = f - IC
                        sc.wait_ge(s.S_DV.h, mk[("dv_cvm", c)])
                        s.S_AC.inc(sc.activation(s.xbc[:, c * XW + 6:c * XW + 6 + 1021],
                                                 acc[:, 0:1021], AF.Silu,
                                                 bias=s.convb_s[:, c:c + 1]))
                        mk[("ac_cvm", c)] = s.S_AC.n
                else:
                    @blk.scalar
                    def _(sc, f=f, pp=pp):
                        sc.wait_ge(s.S_PE.h, mk[("pe_ip", f)])
                        s.S_AC.inc(sc.activation(s.dtsp[:, :], pot[pp][0:HEADS, :],
                                                 AF.Sigmoid, scale=-1.0,
                                                 bias=s.dtbn_s[:]))
                        sc.wait_ge(s.S_AC.h, s.S_AC.n)
                        s.S_AC.inc(sc.activation(s.dtsp[:, :], s.dtsp[:, :], AF.Ln))
                        mk[("ev_ip", f)] = (s.S_AC.h, s.S_AC.n)

            @blk.gpsimd
            def _(g):
                for f in range(IC, 67):
                    eng, v = mk[("ev_ip", f)]
                    g.wait_ge(eng, v)
                xv = s.xbc.ap().rearrange("p (c t) -> p c t", c=CCH)
                s.Q_GP.issue(g, AP(s.hl_in, 0, [[3, 128], [3 * 128, CCH], [1, 3]]),
                             hstage.ap().rearrange("p (c t) -> p c t", c=CCH))
                s.Q_GP.drain(g)
                s.S_CC.inc(g.collective_compute(
                    "AllGather", OP.bypass,
                    replica_groups=[[0, 1], [2, 3], [4, 5], [6, 7]],
                    ins=[s.hl_in.ap().opt()], outs=[s.hl_out.ap().opt()]))
                g.wait_ge(s.S_CC.h, s.S_CC.n)
                s.Q_GP.issue(g, xv[:, :, 0:3],
                             AP(s.hl_out, 0, [[3, 128], [3 * 128, CCH], [1, 3]]))
                s.Q_GP.drain(g)
                s.S_GP.inc(g.engine_nop())
            mk["halo"] = s.S_GP.n

            @blk.vector
            def _(v):
                v.wait_ge(s.S_GP.h, mk["halo"])
                v.wait_ge(s.S_DV.h, s.S_DV.n)
                xv = s.xbc.ap().rearrange("p (c t) -> p c t", c=CCH)
                s.S_DV.inc(v.tensor_scalar_mul(xv[:, :, 0:3], xv[:, :, 0:3], s.flag_s[:]))
                mk["halo_m"] = s.S_DV.n

            for c in range(CCH):
                @blk.vector
                def _(v, c=c):
                    if c == 0:
                        v.wait_ge(s.S_DV.h, mk["halo_m"])
                        v.wait_ge(s.S_AC.h, mk[("ac_cvm", CCH - 1)])
                    if ("ac_cv", c - 1) in mk:
                        v.wait_ge(s.S_AC.h, mk[("ac_cv", c - 1)])
                    v.wait_ge(s.S_DV.h, s.S_DV.n)
                    base = c * XW
                    s.S_DV.inc(v.tensor_scalar_mul(acc[:, 0:3], s.xbc[:, base:base + 3],
                                                   s.convw_s[:, 4 * c:4 * c + 1]))
                    for k in range(1, 4):
                        v.wait_ge(s.S_DV.h, s.S_DV.n)
                        s.S_DV.inc(v.scalar_tensor_tensor(
                            acc[:, 0:3], s.xbc[:, base + k:base + k + 3],
                            s.convw_s[:, 4 * c + k:4 * c + k + 1], acc[:, 0:3],
                            OP.mult, OP.add))
                    mk[("dv_cv", c)] = s.S_DV.n

                @blk.scalar
                def _(sc, c=c):
                    sc.wait_ge(s.S_DV.h, mk[("dv_cv", c)])
                    s.S_AC.inc(sc.activation(s.xbc[:, c * XW + 3:c * XW + 6],
                                             acc[:, 0:3], AF.Silu,
                                             bias=s.convb_s[:, c:c + 1]))
                    mk[("ac_cv", c)] = s.S_AC.n

            self.drain_all(blk, extra=(QIP,))

    def p_scan(self):
        s = self
        nc = s.nc
        with nc.Block() as blk, contextlib.ExitStack() as ar:
            A = lambda nm, sh, dt=F32: ar.enter_context(nc.sbuf_tensor(nm, sh, dt))
            P_ = lambda nm, sh, dt=F32: ar.enter_context(nc.psum_tensor(nm, sh, dt))
            dA = A("dA", [HEADS, TOK])
            Pall = A("Pall", [HEADS, TOK])
            Pp = A("Pp", [HEADS, TOK])
            tdmat = A("tdmat", [HEADS, 8])
            Imat = A("Imat", [HEADS, 8])
            dtT = A("dtT", [128, NJ * HEADS])
            dxo = A("dxo", [128, 2 * 512], BF16)
            dxw = A("dxw", [128, 512], BF16)
            Ebuf = A("Ebuf", [128, 1024])
            Sbuf = A("Sbuf", [128, 1024], BF16)
            CPb = A("CPb", [128, 1024], BF16)
            BT = A("BT", [128, 128], BF16)
            dAz = A("dAz", [HEADS, TOK])
            identrep = A("identrep", [128, 1024], BF16)
            fA = A("fA", [1, 1024])
            fP = A("fP", [1, 1024])
            bAW = P_("bAW", [128, 1024])
            bA = [bAW.ap()[:, 0:512], bAW.ap()[:, 512:1024]]
            bCB = P_("bCB", [128, 512])
            bTR = P_("bTR", [128, 1024], BF16)
            bT2 = P_("bT2", [128, 1024], BF16)
            bY = [P_("bY%d" % i, [128, 512]) for i in range(2)]
            bG = P_("bG", [128, 512])
            QF = Dma([s.SEM("qf%d" % i, 16) for i in range(2)])
            mk = {}
            xv = lambda c, a, w: s.xbc[:, c * XW + a:c * XW + a + w]

            @blk.gpsimd
            def _(g):
                s.S_CC.inc(g.collective_compute(
                    "AllGather", OP.bypass, replica_groups=[list(range(NCORES))],
                    ins=[AP(s.pc_in, 256 * PROJ, [[1, 512 * D]])],
                    outs=[AP(s.opw_bf, 0, [[1, INTER * D]]).opt()]))
                s.cc_opw = s.S_CC.n
                s.S_GP.inc(g.memset(identrep[:], 1.0))
                g.wait_ge(s.S_GP.h, s.S_GP.n)
                s.S_GP.inc(g.affine_select(identrep[:], identrep[:], [[0, 8], [1, 128]],
                                           OP.is_equal, 0.0, base=0, channel_multiplier=-1))
                mk["irep"] = s.S_GP.n

            @blk.vector
            def _(v):
                v.wait_ge(s.S_DV.h, s.S_DV.n)
                s.S_DV.inc(v.tensor_scalar_mul(s.dtsp[:], s.dtsp[:], -1.0))
                v.wait_ge(s.S_DV.h, s.S_DV.n)
                s.S_DV.inc(v.tensor_scalar_mul(dA[:], s.dtsp[:], s.acol[:]))
                mk["dtA"] = s.S_DV.n

            @blk.scalar
            def _(sc):
                sc.wait_ge(s.S_DV.h, mk["dtA"])
                s.S_AC.inc(sc.activation(dA[:], dA[:], AF.Exp))
                mk["dA"] = s.S_AC.n

            @blk.vector
            def _(v):
                v.wait_ge(s.S_AC.h, mk["dA"])
                for j in range(NJ):
                    v.wait_ge(s.S_DV.h, s.S_DV.n)
                    s.S_DV.inc(v.tensor_tensor_scan(
                        Pall[:, j * 128:(j + 1) * 128], dA[:, j * 128:(j + 1) * 128],
                        s.zeros64[:, 0:128], 1.0, OP.mult, OP.add))
                v.wait_ge(s.S_DV.h, s.S_DV.n)
                tdsrc = AP(Pall, 127, [[TOK, HEADS], [128, 8]])
                s.S_DV.inc(v.tensor_copy(tdmat[:], tdsrc))
                v.wait_ge(s.S_DV.h, s.S_DV.n)
                s.S_DV.inc(v.tensor_tensor_scan(Imat[:], tdmat[:], s.zeros64[:, 0:8],
                                                1.0, OP.mult, OP.add))
                v.wait_ge(s.S_DV.h, s.S_DV.n)
                s.S_DV.inc(v.tensor_copy(Pp[:, 0:128], Pall[:, 0:128]))
                for j in range(1, NJ):
                    s.S_DV.inc(v.tensor_scalar_mul(Pp[:, j * 128:(j + 1) * 128],
                                                   Pall[:, j * 128:(j + 1) * 128],
                                                   Imat[:, j - 1:j]))
                mk["pp"] = s.S_DV.n

            for j in range(NJ):
                @blk.tensor
                def _(t, j=j):
                    if j == 0:
                        t.wait_ge(s.S_DV.h, s.S_DV.n)
                    if ("dv_dtT", j - 2) in mk:
                        t.wait_ge(s.S_DV.h, mk[("dv_dtT", j - 2)])
                    mm = t.transpose(bAW[:, (j % 2) * 512:(j % 2) * 512 + 64],
                                     s.dtsp[:, j * 128:(j + 1) * 128],
                                     s.ident_f[0:64, 0:64])
                    mk[("pe_dtT", j)] = s.S_PE.inc(mm)

                @blk.vector
                def _(v, j=j):
                    v.wait_ge(s.S_PE.h, mk[("pe_dtT", j)])
                    v.wait_ge(s.S_DV.h, s.S_DV.n)
                    s.S_DV.inc(v.tensor_copy(dtT[:, j * HEADS:(j + 1) * HEADS],
                                             bAW[:, (j % 2) * 512:(j % 2) * 512 + 64]))
                    mk[("dv_dtT", j)] = s.S_DV.n

            @blk.vector
            def _(v):
                v.wait_ge(s.S_AC.h, mk["dA"])
                v.wait_ge(s.S_DV.h, s.S_DV.n)
                s.S_DV.inc(v.tensor_copy(dAz[:], dA[:]))
                mk["dAz"] = s.S_DV.n

            @blk.gpsimd
            def _(g):
                g.wait_ge(s.S_DV.h, mk["dAz"])
                for j in range(NJ):
                    s.S_GP.inc(g.memset(dAz[:, j * 128:j * 128 + 1], 0.0))
                mk["dAz0"] = s.S_GP.n

            @blk.gpsimd
            def _(g):
                g.wait_ge(s.S_GP.h, mk["dAz0"])
                g.wait_ge(s.S_DV.h, mk["pp"])
                s.Q_GP.issue(g, AP(s.rowb, 0, [[TOK, HEADS], [1, TOK]]), dAz[:])
                s.Q_GP.issue(g, AP(s.rowb, 64 * TOK, [[TOK, HEADS], [1, TOK]]), Pall[:])
                s.Q_GP.issue(g, AP(s.rowb, 128 * TOK, [[TOK, HEADS], [1, TOK]]), Pp[:])
                s.Q_GP.drain(g)
                s.S_GP.inc(g.engine_nop())
            mk["rowb"] = s.S_GP.n

            @blk.gpsimd
            def _(g):
                g.wait_ge(s.S_DV.h, s.S_DV.n)
                s.S_GP.inc(g.memset(s.state[:], 0.0))
            mk["st0"] = s.S_GP.n

            for j in range(NJ):
                jc = 3 + j * 128

                @blk.tensor
                def _(t, j=j, jc=jc):
                    if j == 0:
                        t.wait_ge(s.S_DV.h, mk[("dv_dtT", NJ - 1)])
                        t.wait_ge(s.S_GP.h, mk["st0"])
                    if ("dv_BT", j - 1) in mk:
                        t.wait_ge(s.S_DV.h, mk[("dv_BT", j - 1)])
                    mm = t.transpose(bTR[:, 0:128], xv(32, jc, 128), s.ident_b[:])
                    mk[("pe_BT", j)] = s.S_PE.inc(mm)
                    if ("dv_S", j - 1, 7, 7) in mk:
                        t.wait_ge(s.S_DV.h, mk[("dv_S", j - 1, 7, 7)])
                    mm = t.matmul(bCB[:, 0:128], xv(32, jc, 128), xv(33, jc, 128),
                                  start=True, stop=True)
                    mk[("pe_CB", j)] = s.S_PE.inc(mm)

                @blk.vector
                def _(v, j=j):
                    v.wait_ge(s.S_PE.h, mk[("pe_BT", j)])
                    v.wait_ge(s.S_DV.h, s.S_DV.n)
                    s.S_DV.inc(v.tensor_copy(BT[:], bTR[:, 0:128]))
                    mk[("dv_BT", j)] = s.S_DV.n

                for o in range(8):
                    op = o % 2

                    @blk.gpsimd
                    def _(g, j=j, o=o, op=op):
                        if o == 0 and j == 0:
                            g.wait_ge(s.S_GP.h, mk["rowb"])
                        gk = ("pe_pbc", j, o - 1) if o else ("pe_pbc", j - 1, 7)
                        if gk in mk:
                            g.wait_ge(s.S_PE.h, mk[gk])
                        _, n = QF.issue(g, fA[0:1, :],
                                        AP(s.rowb, (8 * o) * TOK + j * 128,
                                           [[1, 1], [TOK, 8], [1, 128]]))
                        mk[("ld_fA", j, o)] = (QF.sems[(QF.i - 1) % 2], n)
                        _, n = QF.issue(g, fP[0:1, :],
                                        AP(s.rowb, (64 + 8 * o) * TOK + j * 128,
                                           [[1, 1], [TOK, 8], [1, 128]]))
                        mk[("ld_fP", j, o)] = (QF.sems[(QF.i - 1) % 2], n)

                    @blk.tensor
                    def _(t, j=j, o=o, op=op, jc=jc):
                        if o == 0:
                            t.wait_ge(s.S_DV.h, mk[("dv_BT", j)])
                        if ("dv_dx", j, o - 1) in mk:
                            t.wait_ge(s.S_DV.h, mk[("dv_dx", j, o - 1)])
                        tb = bTR if op == 0 else bT2
                        for q in range(4):
                            ic = o * 4 + q
                            mm = t.transpose(tb[:, 128 + q * 128:128 + q * 128 + 128]
                                             if tb is bTR else tb[:, q * 128:q * 128 + 128],
                                             xv(ic, jc, 128), s.ident_b[:])
                        mk[("pe_tr", j, o)] = s.S_PE.inc(mm)

                    @blk.vector
                    def _(v, j=j, o=o, op=op):
                        v.wait_ge(s.S_PE.h, mk[("pe_tr", j, o)])
                        tb = bTR if op == 0 else bT2
                        base = 128 if op == 0 else 0
                        v.wait_ge(s.S_DV.h, s.S_DV.n)
                        dtrep = dtT.ap()[:, j * HEADS + 8 * o:j * HEADS + 8 * o + 8] \
                            .unsqueeze(2).broadcast_to([128, 8, 64])
                        s.S_DV.inc(v.tensor_tensor(
                            dxo.ap()[:, op * 512:op * 512 + 512]
                            .rearrange("p (a b) -> p a b", a=8),
                            tb.ap()[:, base:base + 512].rearrange("p (a b) -> p a b", a=8),
                            dtrep, OP.mult))
                        mk[("dv_dx", j, o)] = s.S_DV.n

                    @blk.tensor
                    def _(t, j=j, o=o, op=op):
                        sem, n = mk[("ld_fA", j, o)]
                        t.wait_ge(sem.h, n)
                        for key in ((("dv_ch", j, o - 1, 7) if o else ("dv_ch", j - 1, 7, 7)),
                                    (("dv_CP", j, o - 1, 7) if o else ("dv_CP", j - 1, 7, 7))):
                            if key in mk:
                                t.wait_ge(s.S_DV.h, mk[key])
                        for q in range(2):
                            mm = t.matmul(bA[q][:], s.ones1[:],
                                          fA[0:1, q * 512:q * 512 + 512],
                                          start=True, stop=True)
                        mk[("pe_abc", j, o)] = s.S_PE.inc(mm)

                    @blk.vector
                    def _(v, j=j, o=o, op=op):
                        v.wait_ge(s.S_PE.h, mk[("pe_abc", j, o)])
                        v.wait_ge(s.S_PE.h, mk[("pe_CB", j)])
                        v.wait_ge(s.S_DV.h, mk[("dv_dx", j, o)])
                        gk = ("pe_G", j, o - 1) if o else ("pe_G", j - 1, 7)
                        if gk in mk:
                            v.wait_ge(s.S_PE.h, mk[gk])
                        v.wait_ge(s.S_DV.h, s.S_DV.n)
                        if j == 0 and o == 0:
                            v.wait_ge(s.S_GP.h, mk["irep"])
                        s.S_DV.inc(v.tensor_tensor_scan(
                            Ebuf[:], bAW[:], identrep[:], 0.0, OP.mult, OP.add))
                        v.wait_ge(s.S_DV.h, s.S_DV.n)
                        cbrep = bCB.ap()[:, 0:128].unsqueeze(1).broadcast_to([128, 8, 128])
                        s.S_DV.inc(v.tensor_tensor(
                            Sbuf.ap().rearrange("p (a b) -> p a b", a=8),
                            Ebuf.ap().rearrange("p (a b) -> p a b", a=8), cbrep, OP.mult))
                        v.wait_ge(s.S_DV.h, s.S_DV.n)
                        wrep = AP(Ebuf, 127, [[1024, 128], [128, 8]]) \
                            .unsqueeze(2).broadcast_to([128, 8, 64])
                        s.S_DV.inc(v.tensor_tensor(
                            dxw.ap().rearrange("p (a b) -> p a b", a=8),
                            dxo.ap()[:, op * 512:op * 512 + 512]
                            .rearrange("p (a b) -> p a b", a=8), wrep, OP.mult))
                        n = s.S_DV.n
                        for m in range(8):
                            mk[("dv_E", j, o, m)] = n
                            mk[("dv_S", j, o, m)] = n

                    @blk.tensor
                    def _(t, j=j, o=o, op=op):
                        sem, n = mk[("ld_fP", j, o)]
                        t.wait_ge(sem.h, n)
                        t.wait_ge(s.S_DV.h, mk[("dv_E", j, o, 7)])
                        for q in range(2):
                            mm = t.matmul(bA[q][:], s.ones1[:],
                                          fP[0:1, q * 512:q * 512 + 512],
                                          start=True, stop=True)
                        mk[("pe_pbc", j, o)] = s.S_PE.inc(mm)

                    @blk.vector
                    def _(v, j=j, o=o, op=op, jc=jc):
                        v.wait_ge(s.S_PE.h, mk[("pe_pbc", j, o)])
                        v.wait_ge(s.S_DV.h, s.S_DV.n)
                        crep = s.xbc.ap()[:, 33 * XW + jc:33 * XW + jc + 128] \
                            .unsqueeze(1).broadcast_to([128, 8, 128])
                        s.S_DV.inc(v.tensor_tensor(
                            CPb.ap().rearrange("p (a b) -> p a b", a=8), crep,
                            bAW.ap().rearrange("p (a b) -> p a b", a=8), OP.mult))
                        n = s.S_DV.n
                        for m in range(8):
                            mk[("dv_CP", j, o, m)] = n

                    for m in range(8):
                        h = 8 * o + m

                        @blk.tensor
                        def _(t, j=j, o=o, op=op, m=m, h=h):
                            if m == 0:
                                t.wait_ge(s.S_DV.h, mk[("dv_S", j, o, 7)])
                                t.wait_ge(s.S_DV.h, mk[("dv_CP", j, o, 7)])
                            yb = bY[op]
                            reg = yb[(m % 2) * 64:(m % 2) * 64 + 64,
                                     (m // 2) * 128:(m // 2) * 128 + 128]
                            tp = (0, 64) if m % 2 else None
                            mm = t.matmul(reg, dxo[:, op * 512 + m * 64:op * 512 + m * 64 + 64],
                                          Sbuf[:, m * 128:m * 128 + 128],
                                          start=True, stop=(j == 0), tile_position=tp)
                            if j > 0:
                                mm = t.matmul(reg, s.state[:, h * 64:h * 64 + 64],
                                              CPb[:, m * 128:m * 128 + 128],
                                              start=False, stop=True, tile_position=tp)
                            mk[("pe_Y", j, o, m)] = s.S_PE.inc(mm)

                    @blk.tensor
                    def _(t, j=j, o=o, op=op):
                        t.wait_ge(s.S_DV.h, mk[("dv_E", j, o, 7)])
                        if ("dv_ch", j, o - 1, 7) in mk:
                            t.wait_ge(s.S_DV.h, mk[("dv_ch", j, o - 1, 7)])
                        mm = t.matmul(bG[:], BT[:], dxw[:],
                                      start=True, stop=True)
                        mk[("pe_G", j, o)] = s.S_PE.inc(mm)

                    @blk.vector
                    def _(v, j=j, o=o, op=op):
                        v.wait_ge(s.S_PE.h, mk[("pe_G", j, o)])
                        v.wait_ge(s.S_DV.h, s.S_DV.n)
                        tdrep = AP(bAW, 127, [[1024, 128], [128, 8]]) \
                            .unsqueeze(2).broadcast_to([128, 8, 64])
                        stv = s.state.ap()[:, o * 512:o * 512 + 512] \
                            .rearrange("p (a b) -> p a b", a=8)
                        s.S_DV.inc(v.tensor_tensor(stv, stv, tdrep, OP.mult))
                        v.wait_ge(s.S_DV.h, s.S_DV.n)
                        s.S_DV.inc(v.tensor_tensor(
                            s.state[:, o * 512:o * 512 + 512],
                            s.state[:, o * 512:o * 512 + 512], bG[:], OP.add))
                        n = s.S_DV.n
                        for m in range(8):
                            mk[("dv_ch", j, o, m)] = n

                    for pr in range(4):
                        ic = o * 4 + pr

                        @blk.vector
                        def _(v, j=j, o=o, op=op, pr=pr, ic=ic, jc=jc):
                            v.wait_ge(s.S_PE.h, mk[("pe_Y", j, o, 7)])
                            v.wait_ge(s.S_DV.h, s.S_DV.n)
                            s.S_DV.inc(v.scalar_tensor_tensor(
                                xv(ic, jc, 128), xv(ic, jc, 128),
                                s.drep_s[:, ic:ic + 1],
                                bY[op][:, pr * 128:pr * 128 + 128], OP.mult, OP.add))
                            mk[("dv_Y", j, o, pr)] = s.S_DV.n

            @blk.gpsimd
            def _(g):
                g.wait_ge(s.S_DV.h, s.S_DV.n)
                s.Q_GP.issue(g, AP(s.s_in, 0, [[INTER, GTS], [1, INTER]]), s.state[:])
                s.Q_GP.drain(g)
                s.S_CC.inc(g.collective_compute(
                    "AllGather", OP.bypass,
                    replica_groups=[[0, 1], [2, 3], [4, 5], [6, 7]],
                    ins=[s.s_in.ap().opt()], outs=[s.s_out.ap().opt()]))
                g.wait_ge(s.S_CC.h, s.S_CC.n)
                s.Q_GP.issue(g, s.state[:], AP(s.s_out, 0, [[INTER, GTS], [1, INTER]]))
                s.Q_GP.drain(g)
                s.S_GP.inc(g.engine_nop())
            self.mk_ag2 = s.S_GP.n

            @blk.vector
            def _(v):
                v.wait_ge(s.S_GP.h, self.mk_ag2)
                v.wait_ge(s.S_DV.h, s.S_DV.n)
                s.S_DV.inc(v.tensor_scalar_mul(s.state[:], s.state[:], s.flag_s[:]))
            self.mk_smask = s.S_DV.n
            self.drain_all(blk, extra=(QF,))

    def p_carry2(self):
        s = self
        nc = s.nc
        with nc.Block() as blk, contextlib.ExitStack() as ar:
            A = lambda nm, sh, dt=F32: ar.enter_context(nc.sbuf_tensor(nm, sh, dt))
            P_ = lambda nm, sh, dt=F32: ar.enter_context(nc.psum_tensor(nm, sh, dt))
            fP = A("c_fP", [1, 1024])
            CPb4 = A("c_CPb4", [128, 8 * 512], BF16)
            bAW = P_("c_bAW", [128, 1024])
            bA = [bAW.ap()[:, 0:512], bAW.ap()[:, 512:1024]]
            bY = [P_("c_bY%d" % i, [128, 512]) for i in range(4)]
            QF = Dma([s.SEM("cqf%d" % i, 16) for i in range(2)])
            mk = {}
            xv = lambda c, a, w: s.xbc[:, c * XW + a:c * XW + a + w]

            for jh in range(2):
                for o in range(8):
                    for jl in range(4):
                        j = jh * 4 + jl
                        jc = 3 + j * 128

                        @blk.gpsimd
                        def _(g, j=j, o=o, jl=jl):
                            gk = (("pe_pbc", o, jl - 1) if jl else
                                  (("pe_pbc", o - 1, 3) if o else ("pe_pbc_l", jh - 1)))
                            if gk in mk:
                                g.wait_ge(s.S_PE.h, mk[gk])
                            _, n = QF.issue(g, fP[0:1, :],
                                            AP(s.rowb, (128 + 8 * o) * TOK + j * 128,
                                               [[1, 1], [TOK, 8], [1, 128]]))
                            mk[("ld_fP", o, jl)] = (QF.sems[(QF.i - 1) % 2], n)

                        @blk.tensor
                        def _(t, j=j, o=o, jl=jl):
                            sem, n = mk[("ld_fP", o, jl)]
                            t.wait_ge(sem.h, n)
                            if j == 0 and o == 0 and jh == 0:
                                t.wait_ge(s.S_DV.h, s.mk_smask)
                            gk = (("dv_CP", o, jl - 1) if jl else
                                  (("dv_CP", o - 1, 3) if o else ("dv_CP_l", jh - 1)))
                            if gk in mk:
                                t.wait_ge(s.S_DV.h, mk[gk])
                            for q in range(2):
                                mm = t.matmul(bA[q][:], s.ones1[:],
                                              fP[0:1, q * 512:q * 512 + 512],
                                              start=True, stop=True)
                            mk[("pe_pbc", o, jl)] = s.S_PE.inc(mm)
                            if jl == 3:
                                mk[("pe_pbc_l", jh)] = s.S_PE.n

                        @blk.vector
                        def _(v, j=j, o=o, jl=jl, jc=jc):
                            v.wait_ge(s.S_PE.h, mk[("pe_pbc", o, jl)])
                            if jl == 0:
                                gk = ("pe_Y", o - 1, 7) if o else ("pe_Y_l", jh - 1)
                                if gk in mk:
                                    v.wait_ge(s.S_PE.h, mk[gk])
                            v.wait_ge(s.S_DV.h, s.S_DV.n)
                            crep = s.xbc.ap()[:, 33 * XW + jc:33 * XW + jc + 128] \
                                .unsqueeze(1).broadcast_to([128, 8, 128])
                            dst = AP(CPb4, jl * 128, [[8 * 512, 128], [512, 8], [1, 128]])
                            s.S_DV.inc(v.tensor_tensor(
                                dst, crep,
                                bAW.ap().rearrange("p (a b) -> p a b", a=8), OP.mult))
                            mk[("dv_CP", o, jl)] = s.S_DV.n

                    for m in range(8):
                        h = 8 * o + m

                        @blk.tensor
                        def _(t, o=o, m=m, h=h, jh=jh):
                            if m == 0:
                                t.wait_ge(s.S_DV.h, mk[("dv_CP", o, 3)])
                                gk = ("dv_Y", o - 1, 3) if o else ("dv_Y_l", jh - 1)
                                if gk in mk:
                                    t.wait_ge(s.S_DV.h, mk[gk])
                            reg = bY[m // 2][(m % 2) * 64:(m % 2) * 64 + 64, :]
                            tp = (0, 64) if m % 2 else None
                            mm = t.matmul(reg, s.state[:, h * 64:h * 64 + 64],
                                          CPb4[:, m * 512:m * 512 + 512],
                                          start=True, stop=True, tile_position=tp)
                            mk[("pe_Y", o, m)] = s.S_PE.inc(mm)
                            if m == 7:
                                mk[("pe_Y_l", jh)] = s.S_PE.n

                    for pr in range(4):
                        ic = o * 4 + pr

                        @blk.vector
                        def _(v, o=o, pr=pr, ic=ic, jh=jh):
                            v.wait_ge(s.S_PE.h, mk[("pe_Y", o, 7)])
                            v.wait_ge(s.S_DV.h, s.S_DV.n)
                            s.S_DV.inc(v.tensor_tensor(
                                xv(ic, 3 + jh * 512, 512), xv(ic, 3 + jh * 512, 512),
                                bY[pr][:], OP.add))
                            mk[("dv_Y", o, pr)] = s.S_DV.n
                            if pr == 3:
                                mk[("dv_Y_l", jh)] = s.S_DV.n

            self.drain_all(blk, extra=(QF,))

    def p_gate_out(self):
        s = self
        nc = s.nc
        with nc.Block() as blk, contextlib.ExitStack() as ar:
            A = lambda nm, sh, dt=F32: ar.enter_context(nc.sbuf_tensor(nm, sh, dt))
            P_ = lambda nm, sh, dt=F32: ar.enter_context(nc.psum_tensor(nm, sh, dt))
            gated = A("gated", [128, IC * TH], BF16)
            y2b = A("y2b", [128, 2 * TH], BF16)
            zt = A("g_zt", [128, 2 * TH], BF16)
            tq = A("g_tq", [128, TH])
            opslab = A("opslab", [128, 2 * IC * 128], BF16)
            g0 = P_("g0", [128, 512])
            g1 = P_("g1", [128, 512])
            go = [P_("go%d" % i, [128, 512]) for i in range(2)]
            QOP = Dma([s.SEM("qop%d" % i, 16) for i in range(2)])
            QZ = Dma([s.SEM("qz%d" % i, 16) for i in range(2)])
            mk = {}
            xv = lambda c, a, w: s.xbc[:, c * XW + a:c * XW + a + w]

            for tt in range(2):
                T0 = 3 + tt * TH
                for ic in range(IC):
                    pp = ic % 2

                    @blk.scalar
                    def _(sc, ic=ic, pp=pp, T0=T0, tt=tt):
                        if tt == 0 and ic == 0:
                            sc.wait_ge(s.S_DV.h, s.S_DV.n)
                        if ("pe_rms", tt, ic - 2) in mk:
                            sc.wait_ge(s.S_PE.h, mk[("pe_rms", tt, ic - 2)])
                        s.S_AC.inc(sc.activation(y2b[:, pp * TH:pp * TH + TH],
                                                 xv(ic, T0, TH), AF.Square))
                        mk[("ac_y2", tt, ic)] = s.S_AC.n

                    @blk.tensor
                    def _(t, ic=ic, pp=pp, tt=tt):
                        t.wait_ge(s.S_AC.h, mk[("ac_y2", tt, ic)])
                        if tt == 1 and ic == 0:
                            t.wait_ge(s.S_DV.h, s.S_DV.n)
                        mm = t.matmul(g0[:], s.ones_b[:], y2b[:, pp * TH:pp * TH + TH],
                                      start=(ic == 0), stop=(ic == IC - 1))
                        mk[("pe_rms", tt, ic)] = s.S_PE.inc(mm)

                @blk.vector
                def _(v, tt=tt):
                    v.wait_ge(s.S_PE.h, mk[("pe_rms", tt, IC - 1)])
                    v.wait_ge(s.S_DV.h, s.S_DV.n)
                    s.S_DV.inc(v.tensor_scalar_mul(s.row(0), g0[0:1, :], 1.0 / INTER))
                    mk["g_r0"] = s.S_DV.n

                @blk.scalar
                def _(sc, tt=tt):
                    sc.wait_ge(s.S_DV.h, mk["g_r0"])
                    s.S_AC.inc(sc.activation(s.row(1), s.row(0), AF.Sqrt,
                                             bias=s.epscol[0:1, 0:1]))
                    mk["g_r1"] = s.S_AC.n

                @blk.vector
                def _(v, tt=tt):
                    v.wait_ge(s.S_AC.h, mk["g_r1"])
                    v.wait_ge(s.S_DV.h, s.S_DV.n)
                    s.S_DV.inc(v.reciprocal(s.row(2), s.row(1)))
                    mk["g_r2"] = s.S_DV.n

                @blk.tensor
                def _(t, tt=tt):
                    t.wait_ge(s.S_DV.h, mk["g_r2"])
                    mm = t.matmul(g1[:], s.ones1[:], s.row(2), start=True, stop=True)
                    mk[("pe_g1", tt)] = s.S_PE.inc(mm)

                for ic in range(IC):
                    pp = ic % 2

                    @blk.sync
                    def _(sy, ic=ic, pp=pp, tt=tt):
                        gk = ("dv_gate", tt, ic - 2) if ic >= 2 else ("dv_gate", tt - 1, IC - 2 + ic)
                        if gk in mk:
                            sy.wait_ge(s.S_DV.h, mk[gk])
                        src = AP(s.z_spill, ic * 128 * TOK + tt * TH, [[TOK, 128], [1, TH]])
                        _, n = QZ.issue(sy, zt[:, pp * TH:pp * TH + TH], src)
                        mk[("ld_z", tt, ic)] = (QZ.sems[(QZ.i - 1) % 2], n)

                    @blk.vector
                    def _(v, ic=ic, pp=pp, T0=T0, tt=tt):
                        if ic == 0:
                            v.wait_ge(s.S_PE.h, mk[("pe_g1", tt)])
                        sem, n = mk[("ld_z", tt, ic)]
                        v.wait_ge(sem.h, n)
                        v.wait_ge(s.S_DV.h, s.S_DV.n)
                        s.S_DV.inc(v.scalar_tensor_tensor(
                            tq[:], xv(ic, T0, TH),
                            s.nw_s[:, ic:ic + 1], g1[:], OP.mult, OP.mult))
                        v.wait_ge(s.S_DV.h, s.S_DV.n)
                        s.S_DV.inc(v.tensor_tensor(gated[:, ic * TH:ic * TH + TH],
                                                   tq[:],
                                                   zt[:, pp * TH:pp * TH + TH], OP.mult))
                        mk[("dv_gate", tt, ic)] = s.S_DV.n

                for do in range(DC):
                    pp = do % 2

                    @blk.sync
                    def _(sy, do=do, pp=pp, tt=tt):
                        if tt == 0 and do == 0:
                            sy.wait_ge(s.S_CC.h, s.cc_opw)
                        if ("pe_op", tt, do - 2) in mk:
                            sy.wait_ge(s.S_PE.h, mk[("pe_op", tt, do - 2)])
                        elif tt == 1 and ("pe_op", 0, DC - 2 + do) in mk:
                            sy.wait_ge(s.S_PE.h, mk[("pe_op", 0, DC - 2 + do)])
                        src = AP(s.opw_bf, do * 128,
                                 [[D, 128], [D * 128, IC], [1, 128]])
                        dst = AP(opslab, pp * IC * 128, [[2 * IC * 128, 128], [128, IC], [1, 128]])
                        _, n = QOP.issue(sy, dst, src)
                        mk[("ld_op", tt, do)] = (QOP.sems[(QOP.i - 1) % 2], n)

                    @blk.tensor
                    def _(t, do=do, pp=pp, tt=tt):
                        sem, n = mk[("ld_op", tt, do)]
                        t.wait_ge(sem.h, n)
                        if do == 0:
                            t.wait_ge(s.S_DV.h, mk[("dv_gate", tt, IC - 1)])
                        if ("dv_op", tt, do - 2) in mk:
                            t.wait_ge(s.S_DV.h, mk[("dv_op", tt, do - 2)])
                        for ic in range(IC):
                            mm = t.matmul(go[pp][:],
                                          opslab[:, pp * IC * 128 + ic * 128:
                                                pp * IC * 128 + ic * 128 + 128],
                                          gated[:, ic * TH:ic * TH + TH],
                                          start=(ic == 0), stop=(ic == IC - 1))
                        mk[("pe_op", tt, do)] = s.S_PE.inc(mm)

                    @blk.vector
                    def _(v, do=do, pp=pp, tt=tt):
                        v.wait_ge(s.S_PE.h, mk[("pe_op", tt, do)])
                        v.wait_ge(s.S_DV.h, s.S_DV.n)
                        T0r = tt * TH
                        s.S_DV.inc(v.tensor_tensor(
                            s.resid[:, do * TOK + T0r:do * TOK + T0r + TH],
                            s.resid[:, do * TOK + T0r:do * TOK + T0r + TH],
                            go[pp][:], OP.add))
                        mk[("dv_op", tt, do)] = s.S_DV.n

            self.drain_all(blk, extra=(QOP, QZ))

    def p_store(self):
        s = self
        with s.nc.Block() as blk:
            @blk.scalar
            def _(sc):
                dst = AP(s.out_ap.tensor, 0, [[TOK, 128], [TOK * 128, DC], [1, TOK]])
                s.Q_ST.issue(sc, dst, s.resid.ap().rearrange("p (c t) -> p c t", c=DC))
                s.Q_ST.drain(sc)


def _prep_inputs(inputs, c):
    b, hf = c // 2, c % 2
    x = np.asarray(inputs["L0_attn_hidden_states"])
    w1 = np.asarray(inputs["L0_ffn_w1"])
    w2 = np.asarray(inputs["L0_ffn_w2"])
    w3 = np.asarray(inputs["L0_ffn_w3"])
    return dict(
        x_fm=np.ascontiguousarray(x[b, hf * TOK:(hf + 1) * TOK, :].T),
        w1s=np.ascontiguousarray(w1[:, c * 256:(c + 1) * 256]),
        w3s=np.ascontiguousarray(w3[:, c * 256:(c + 1) * 256]),
        w2s=np.ascontiguousarray(w2[c * 256:(c + 1) * 256, :]),
        ipw_sh=np.ascontiguousarray(
            np.asarray(inputs["L0_attn_in_proj_w"])[c * 256:(c + 1) * 256, :]),
        opw_sh=np.ascontiguousarray(
            np.asarray(inputs["L0_attn_out_proj_w"])[c * 512:(c + 1) * 512, :]),
        convw=np.ascontiguousarray(inputs["L0_attn_conv_w"]),
        convb=np.ascontiguousarray(np.asarray(inputs["L0_attn_conv_b"]).reshape(CONV, 1)),
        alog=np.ascontiguousarray(np.asarray(inputs["L0_attn_A_log"]).reshape(HEADS, 1)),
        dtb=np.ascontiguousarray(np.asarray(inputs["L0_attn_dt_bias"]).reshape(HEADS, 1)),
        drep=np.ascontiguousarray(np.repeat(np.asarray(inputs["L0_attn_D"]), PD).reshape(INTER, 1)),
        nw=np.ascontiguousarray(np.asarray(inputs["L0_attn_norm_w"]).reshape(INTER, 1)),
        flag=np.full((128, 1), float(hf), np.float32),
    )


def kernel(**inputs):
    mk = MK()
    nc = mk.build()
    in_maps = [_prep_inputs(inputs, c) for c in range(NCORES)]
    res = run_bass_kernel_spmd(nc, in_maps, core_ids=list(range(NCORES)))
    out = np.empty((4, 2048, 2048), np.float32)
    for c in range(NCORES):
        b, hf = c // 2, c % 2
        out[b, hf * TOK:(hf + 1) * TOK, :] = np.asarray(res.results[c]["out_fm"]).T
    return out


# version 16
# speedup vs baseline: 1.0640x; 1.0487x over previous
import contextlib
import numpy as np
import concourse.bass as bass
import concourse.mybir as mybir
from concourse.ap import AP
from concourse.bass_utils import run_bass_kernel_spmd

F32 = mybir.dt.float32
BF16 = mybir.dt.bfloat16
AF = mybir.ActivationFunctionType
OP = mybir.AluOpType
AX = mybir.AxisListType

MAGIC = float(np.float32(3 * 2**22))
EPS = 1e-5
NCORES = 8
TOK = 1024
TH = 512
D = 2048
DC = 16
HID = 5632
HC = 44
HP = 22
INTER = 4096
IC = 32
GTS = 128
CONV = 4352
CCH = 34
XW = 1027
PROJ = 8512
HEADS = 64
PD = 64
NJ = 8
BLK = 128 * 256
NW_ELEMS = float(HID * D)

W1_SHARD = 2 * HP * BLK
HWH = W1_SHARD // 2
W2_SHARD = HC * BLK
RANK_SH = 2 * W1_SHARD + W2_SHARD
W1_OFF, W3_OFF, W2_OFF = 0, W1_SHARD, 2 * W1_SHARD

STAGE = 4
FFN_PARTS = 9


def K_FFN_PARTS():
    return FFN_PARTS


class Sem:
    def __init__(self, handle, step=1):
        self.h = handle
        self.step = step
        self.n = 0

    def inc(self, instr):
        self.n += self.step
        instr.then_inc(self.h, self.step)
        return self.n

    def wait(self, eng, n=None):
        n = self.n if n is None else n
        if n > 0:
            eng.wait_ge(self.h, n)


class Dma:
    def __init__(self, sems):
        self.sems = sems
        self.i = 0

    def issue(self, eng, out, in_, **kw):
        s = self.sems[self.i % len(self.sems)]
        if s.n > 0:
            eng.wait_ge(s.h, s.n)
        s.inc(eng.dma_start(out=out, in_=in_, **kw))
        self.i += 1
        return s, s.n

    def drain(self, eng):
        for s in self.sems:
            s.wait(eng)


class MK:
    def __init__(self):
        nc = self.nc = bass.Bass("TRN2", target_bir_lowering=False)
        DT = nc.dram_tensor
        self.x_in = DT("x_fm", [D, TOK], F32, kind="ExternalInput").ap()
        self.w1s_in = DT("w1s", [HID, 256], F32, kind="ExternalInput").ap()
        self.w3s_in = DT("w3s", [HID, 256], F32, kind="ExternalInput").ap()
        self.w2s_in = DT("w2s", [256, HID], F32, kind="ExternalInput").ap()
        self.ipw_in = DT("ipw_sh", [256, PROJ], F32, kind="ExternalInput").ap()
        self.opw_in = DT("opw_sh", [512, D], F32, kind="ExternalInput").ap()
        self.convw_in = DT("convw", [CONV, 4], F32, kind="ExternalInput").ap()
        self.convb_in = DT("convb", [CONV, 1], F32, kind="ExternalInput").ap()
        self.alog_in = DT("alog", [HEADS, 1], F32, kind="ExternalInput").ap()
        self.dtb_in = DT("dtb", [HEADS, 1], F32, kind="ExternalInput").ap()
        self.drep_in = DT("drep", [INTER, 1], F32, kind="ExternalInput").ap()
        self.nw_in = DT("nw", [INTER, 1], F32, kind="ExternalInput").ap()
        self.flag_in = DT("flag", [128, 1], F32, kind="ExternalInput").ap()
        self.out_ap = DT("out_fm", [D, TOK], F32, kind="ExternalOutput").ap()

        self.q_in = DT("q_in", [RANK_SH], BF16, kind="Internal")
        self.q1_out = DT("q1_out", [NCORES * W1_SHARD], BF16, kind="Internal")
        self.q3_out = DT("q3_out", [NCORES * W1_SHARD], BF16, kind="Internal")
        self.q2_out = DT("q2_out", [NCORES * W2_SHARD], BF16, kind="Internal")
        self.m_in = DT("m_in", [1, 4], F32, kind="Internal")
        self.m_out = DT("m_out", [1, 4], F32, kind="Internal")
        self.hl_in = DT("hl_in", [CONV, 3], BF16, kind="Internal")
        self.hl_out = DT("hl_out", [2 * CONV, 3], BF16, kind="Internal")
        self.s_in = DT("s_in", [GTS, INTER], F32, kind="Internal")
        self.s_out = DT("s_out", [2 * GTS, INTER], F32, kind="Internal")
        self.z_spill = DT("z_spill", [INTER, TOK], BF16, kind="Internal")
        self.rowb = DT("rowb", [192, TOK], F32, kind="Internal")
        self.ipw_bf = DT("ipw_bf", [D, PROJ], BF16, kind="Internal")
        self.opw_bf = DT("opw_bf", [INTER, D], BF16, kind="Internal")
        self.pc_in = DT("pc_in", [256 * PROJ + 512 * D], BF16, kind="Internal")

    def SB(self, nm, sh, dt=F32):
        return self.ctx.enter_context(self.nc.sbuf_tensor(nm, sh, dt))

    def SEM(self, nm, step=1):
        return Sem(self.ctx.enter_context(self.nc.semaphore(nm)), step)

    def build(self):
        self.ctx = contextlib.ExitStack()
        with self.ctx:
            self.alloc_persistent()
            self.p_consts()
            self.p_quant()
            self.p_ffn()
            if STAGE >= 2:
                with contextlib.ExitStack() as mar:
                    A = lambda nm, sh, dt=F32: mar.enter_context(self.nc.sbuf_tensor(nm, sh, dt))
                    self.xbc = A("xbc", [128, CCH * XW], BF16)
                    self.dtsp = A("dtsp", [HEADS, TOK])
                    self.p_proj_conv()
                    if STAGE >= 3:
                        with contextlib.ExitStack() as sar:
                            self.state = sar.enter_context(
                                self.nc.sbuf_tensor("state", [128, INTER], BF16))
                            self.p_scan()
                            self.p_carry2()
                        self.p_gate_out()
            if STAGE >= 4:
                self.p_ffn()
            self.p_store()
        return self.nc

    def alloc_persistent(self):
        SB, SEM = self.SB, self.SEM
        self.resid = SB("resid", [128, DC * TOK])
        self.ident_b = SB("ident_b", [128, 128], BF16)
        self.ident_f = SB("ident_f", [128, 128], F32)
        self.ones_b = SB("ones_b", [128, 128], BF16)
        self.ones1 = SB("ones1", [1, 128], F32)
        self.ones_f = SB("ones_f", [128, 128], F32)
        self.epscol = SB("epscol", [128, 1], F32)
        self.convw_s = SB("convw_s", [128, CCH * 4])
        self.convb_s = SB("convb_s", [128, CCH])
        self.acol = SB("acol", [HEADS, 1])
        self.dtb_s = SB("dtb_s", [HEADS, 1])
        self.dtbn_s = SB("dtbn_s", [HEADS, 1])
        self.drep_s = SB("drep_s", [128, IC])
        self.nw_s = SB("nw_s", [128, IC])
        self.flag_s = SB("flag_s", [128, 1])
        self.qcols = SB("qcols", [128, 12])
        self.meansn = SB("meansn", [128, 4])
        self.rows = SB("rows", [1, 6 * TH])
        self.rows2 = SB("rows2", [1, 16])
        self.zeros64 = SB("zeros64", [HEADS, 128])

        self.S_PE = SEM("S_PE")
        self.S_DV = SEM("S_DV")
        self.S_AC = SEM("S_AC")
        self.S_GP = SEM("S_GP")
        self.S_CC = SEM("S_CC")
        self.Q_SY = Dma([SEM("sy%d" % i, 16) for i in range(3)])
        self.Q_SC = Dma([SEM("sc%d" % i, 16) for i in range(2)])
        self.Q_GP = Dma([SEM("gp%d" % i, 16) for i in range(2)])
        self.Q_ST = Dma([SEM("qst%d" % i, 16) for i in range(2)])

    ROWMAP = {0: 0, 1: 1, 2: 2, 3: 3, 4: 0, 5: 5, 6: 1, 7: 2, 8: 3, 9: 0,
              10: 1, 11: 4, 12: 0, 13: 2, 14: 3, 15: 0, 16: 2, 17: 1}

    def row(self, k, w=TH):
        k = self.ROWMAP[k]
        return self.rows[0:1, k * TH:k * TH + w]

    def drain_all(self, blk, extra=()):
        s = self

        @blk.gpsimd
        def _(g):
            for q in (s.Q_SY, s.Q_SC, s.Q_GP, s.Q_ST) + tuple(extra):
                q.drain(g)

    def p_consts(self):
        s = self
        with s.nc.Block() as blk:
            @blk.sync
            def _(sy):
                src = AP(s.x_in.tensor, 0, [[TOK, 128], [TOK * 128, DC], [1, TOK]])
                s.Q_SY.issue(sy, s.resid.ap().rearrange("p (c t) -> p c t", c=DC), src)
                s.Q_SY.issue(sy, s.convw_s.ap().rearrange("p (c k) -> p c k", c=CCH),
                             AP(s.convw_in.tensor, 0, [[4, 128], [4 * 128, CCH], [1, 4]]))
                s.Q_SY.issue(sy, s.convb_s.ap().unsqueeze(2),
                             AP(s.convb_in.tensor, 0, [[1, 128], [128, CCH], [1, 1]]), allow_slow_non_contiguous=True)
                s.Q_SY.issue(sy, s.dtb_s[:], s.dtb_in)
                s.Q_SY.issue(sy, s.drep_s.ap().unsqueeze(2),
                             AP(s.drep_in.tensor, 0, [[1, 128], [128, IC], [1, 1]]), allow_slow_non_contiguous=True)
                s.Q_SY.issue(sy, s.nw_s.ap().unsqueeze(2),
                             AP(s.nw_in.tensor, 0, [[1, 128], [128, IC], [1, 1]]), allow_slow_non_contiguous=True)
                s.Q_SY.issue(sy, s.flag_s[:], s.flag_in)

            @blk.scalar
            def _(sc):
                s.Q_SC.issue(sc, s.acol[:], s.alog_in)
                s.Q_SC.drain(sc)
                s.S_AC.inc(sc.activation(s.acol[:], s.acol[:], AF.Exp))

            @blk.gpsimd
            def _(g):
                g.memset(s.ones_b[:], 1.0)
                g.memset(s.ones1[:], 1.0)
                g.memset(s.ones_f[:], 1.0)
                g.memset(s.epscol[:], EPS)
                g.memset(s.zeros64[:], 0.0)
                s.S_GP.inc(g.memset(s.ident_f[:], 1.0))
                g.wait_ge(s.S_GP.h, s.S_GP.n)
                s.S_GP.inc(g.affine_select(s.ident_b[:], s.ones_b[:], [[1, 128]],
                                           OP.is_equal, 0.0, base=0, channel_multiplier=-1))
                s.S_GP.inc(g.affine_select(s.ident_f[:], s.ident_f[:], [[1, 128]],
                                           OP.is_equal, 0.0, base=0, channel_multiplier=-1))

            @blk.vector
            def _(v):
                s.S_AC.wait(v)
                s.S_DV.inc(v.tensor_scalar_mul(s.acol[:], s.acol[:], -1.0))
                s.Q_SY.drain(v)
                v.wait_ge(s.S_DV.h, s.S_DV.n)
                s.S_DV.inc(v.tensor_scalar_mul(s.dtbn_s[:], s.dtb_s[:], -1.0))
        self.gp_idents = s.S_GP.n

    def p_quant(self):
        s = self
        nc = s.nc
        w13 = ([("w1", i) for i in range(HC // 2)] + [("w3", i) for i in range(HC // 2)] +
               [("w1", i) for i in range(HC // 2, HC)] + [("w3", i) for i in range(HC // 2, HC)])

        def w13_src(mat, i):
            t = s.w1s_in.tensor if mat == "w1" else s.w3s_in.tensor
            return AP(t, i * 128 * 256, [[256, 128], [1, 256]])

        def w2_src(r, hhalf):
            return AP(s.w2s_in.tensor, r * 128 * HID + hhalf * (HID // 2),
                      [[HID, 128], [1, HID // 2]])

        with nc.Block() as blk, contextlib.ExitStack() as ar:
            A = lambda nm, sh, dt=F32: ar.enter_context(nc.sbuf_tensor(nm, sh, dt))
            qp = A("qp1b", [128, 2 * 5632])
            qsum = A("qsum", [128, 4])
            qred = A("qred", [128, 4])
            ps_m = ar.enter_context(nc.psum_tensor("ps_m", [128, 512], F32))
            loads = [(m, h, AP(t, h * 5632, [[11264, 128], [1, 5632]]), 5632)
                     for m, t in (("w1", s.w1s_in.tensor), ("w3", s.w3s_in.tensor),
                                  ("w2", s.w2s_in.tensor))
                     for h in range(2)]
            col = {"w1": 0, "w3": 1, "w2": 2}
            ldm, dvm = {}, {}

            @blk.gpsimd
            def _(g):
                s.S_GP.inc(g.memset(qsum[:], 0.0))
            gp_zero = s.S_GP.n

            for k, (mat, i, src, w) in enumerate(loads):
                @blk.sync
                def _(sy, k=k, src=src, w=w):
                    if k - 2 in dvm:
                        sy.wait_ge(s.S_DV.h, dvm[k - 2])
                    _, n = s.Q_SY.issue(sy, qp[:, (k % 2) * 5632:(k % 2) * 5632 + w], src)
                    ldm[k] = (s.Q_SY.sems[(s.Q_SY.i - 1) % 3], n)

                @blk.vector
                def _(v, k=k, mat=mat, w=w):
                    sem, n = ldm[k]
                    v.wait_ge(sem.h, n)
                    if k == 0:
                        v.wait_ge(s.S_GP.h, gp_zero)
                    m = col[mat]
                    v.wait_ge(s.S_DV.h, s.S_DV.n)
                    s.S_DV.inc(v.tensor_reduce(qred[:, m:m + 1],
                                               qp[:, (k % 2) * 5632:(k % 2) * 5632 + w],
                                               AX.X, OP.add, apply_absolute_value=True))
                    v.wait_ge(s.S_DV.h, s.S_DV.n)
                    s.S_DV.inc(v.tensor_tensor(qsum[:, m:m + 1], qsum[:, m:m + 1],
                                               qred[:, m:m + 1], OP.add))
                    dvm[k] = s.S_DV.n

            @blk.tensor
            def _(t):
                t.wait_ge(s.S_DV.h, s.S_DV.n)
                s.S_PE.inc(t.matmul(ps_m[:, 0:4], s.ones_f[:], qsum[:], start=True, stop=True))

            @blk.vector
            def _(v):
                s.S_PE.wait(v)
                s.S_DV.inc(v.tensor_copy(qred[0:1, 0:4], ps_m[0:1, 0:4]))

            @blk.gpsimd
            def _(g):
                g.wait_ge(s.S_DV.h, s.S_DV.n)
                s.Q_GP.issue(g, AP(s.m_in, 0, [[4, 1], [1, 4]]), qred[0:1, 0:4])
                s.Q_GP.drain(g)
                s.S_CC.inc(g.collective_compute(
                    "AllReduce", OP.add, replica_groups=[list(range(NCORES))],
                    ins=[s.m_in.ap().opt()], outs=[s.m_out.ap().opt()]))
                g.wait_ge(s.S_CC.h, s.S_CC.n)
                s.Q_GP.issue(g, s.rows2[0:1, 0:4], AP(s.m_out, 0, [[4, 1], [1, 4]]))
                s.Q_GP.drain(g)
                s.S_GP.inc(g.engine_nop())

            @blk.vector
            def _(v):
                s.S_GP.wait(v)
                s.S_DV.inc(v.tensor_scalar(s.rows2[0:1, 4:8], s.rows2[0:1, 0:4],
                                           1.0 / NW_ELEMS, EPS, OP.mult, OP.max))
                v.wait_ge(s.S_DV.h, s.S_DV.n)
                s.S_DV.inc(v.tensor_scalar_mul(s.rows2[0:1, 8:12], s.rows2[0:1, 4:8], 0.5))
                s.S_DV.inc(v.tensor_scalar_mul(s.rows2[0:1, 12:16], s.rows2[0:1, 4:8], -0.5))

            @blk.tensor
            def _(t):
                t.wait_ge(s.S_DV.h, s.S_DV.n)
                s.S_PE.inc(t.matmul(ps_m[:, 0:12], s.ones1[:], s.rows2[0:1, 4:16],
                                    start=True, stop=True))

            @blk.vector
            def _(v):
                s.S_PE.wait(v)
                s.S_DV.inc(v.tensor_copy(s.qcols[:], ps_m[:, 0:12]))
                v.wait_ge(s.S_DV.h, s.S_DV.n)
                s.S_DV.inc(v.tensor_scalar_mul(s.meansn[:, 0:3], s.qcols[:, 0:3], -1.0))
                s.S_DV.inc(v.tensor_scalar_mul(s.meansn[:, 3:4], s.qcols[:, 2:3], 0.5))

            pcf = ar.enter_context(nc.sbuf_tensor("pcf", [128, 2 * 4256], F32))
            pcb = ar.enter_context(nc.sbuf_tensor("pcb", [128, 2 * 4256], BF16))
            QPC = Dma([s.SEM("qpc%d" % i, 16) for i in range(2)])
            pieces_pc = []
            for dc in range(2):
                for pc in range(2):
                    off = dc * 128 * PROJ + pc * 4256
                    pieces_pc.append((s.ipw_in.tensor, s.pc_in, off, off, PROJ, 4256))
            for k2 in range(2):
                off = k2 * 128 * 4096
                pieces_pc.append((s.opw_in.tensor, s.pc_in, off, 256 * PROJ + off,
                                  4096, 4096))
            pcm = {}
            for k, (srcT, dstT, off, doff, rs, w) in enumerate(pieces_pc):
                pp = k % 2

                @blk.sync
                def _(sy, k=k, srcT=srcT, off=off, rs=rs, w=w, pp=pp):
                    if ("pc_dv", k - 2) in pcm:
                        sy.wait_ge(s.S_DV.h, pcm[("pc_dv", k - 2)])
                    _, n = QPC.issue(sy, pcf.ap()[:, pp * 4256:pp * 4256 + w],
                                     AP(srcT, off, [[rs, 128], [1, w]]))
                    pcm[("pc_ld", k)] = (QPC.sems[(QPC.i - 1) % 2], n)

                @blk.vector
                def _(v, k=k, w=w, pp=pp):
                    sem, n = pcm[("pc_ld", k)]
                    v.wait_ge(sem.h, n)
                    if ("pc_st", k - 2) in pcm:
                        sem2, n2 = pcm[("pc_st", k - 2)]
                        v.wait_ge(sem2.h, n2)
                    v.wait_ge(s.S_DV.h, s.S_DV.n)
                    s.S_DV.inc(v.tensor_copy(pcb[:, pp * 4256:pp * 4256 + w],
                                             pcf[:, pp * 4256:pp * 4256 + w]))
                    pcm[("pc_dv", k)] = s.S_DV.n

                @blk.scalar
                def _(sc, k=k, dstT=dstT, doff=doff, rs=rs, w=w, pp=pp):
                    sc.wait_ge(s.S_DV.h, pcm[("pc_dv", k)])
                    _, n = s.Q_ST.issue(sc, AP(dstT, doff, [[rs, 128], [1, w]]),
                                        pcb.ap()[:, pp * 4256:pp * 4256 + w])
                    pcm[("pc_st", k)] = (s.Q_ST.sems[(s.Q_ST.i - 1) % 2], n)

            self.drain_all(blk, extra=(QPC,))

        with nc.Block() as blk, contextlib.ExitStack() as ar:
            A = lambda nm, sh, dt=F32: ar.enter_context(nc.sbuf_tensor(nm, sh, dt))
            qp = A("qp2", [128, 2 * 256])
            qt = A("qt2", [128, 2 * 256], BF16)
            qa = A("qa2", [128, 256], BF16)
            qasm = A("qasm", [128, 2 * 512], BF16)
            ps_q = ar.enter_context(nc.psum_tensor("ps_q", [128, 1024], BF16))
            ldm, dvm, pem, evm, stm = {}, {}, {}, {}, {}

            for k, (mat, i) in enumerate(w13):
                pp = k % 2

                @blk.sync
                def _(sy, k=k, mat=mat, i=i, pp=pp):
                    if k - 2 in dvm:
                        sy.wait_ge(s.S_DV.h, dvm[k - 2])
                    _, n = s.Q_SY.issue(sy, qp[:, pp * 256:pp * 256 + 256], w13_src(mat, i))
                    ldm[k] = (s.Q_SY.sems[(s.Q_SY.i - 1) % 3], n)

                @blk.vector
                def _(v, k=k, mat=mat, pp=pp):
                    sem, n = ldm[k]
                    v.wait_ge(sem.h, n)
                    m = {"w1": 0, "w3": 1}[mat]
                    if k - 1 in pem:
                        v.wait_ge(s.S_PE.h, pem[k - 1])
                    v.wait_ge(s.S_DV.h, s.S_DV.n)
                    s.S_DV.inc(v.tensor_single_scalar(qa[:], qp[:, pp * 256:pp * 256 + 256],
                                                      s.qcols[:, 4 + m:5 + m], OP.is_gt))
                    v.wait_ge(s.S_DV.h, s.S_DV.n)
                    s.S_DV.inc(v.scalar_tensor_tensor(qt[:, pp * 256:pp * 256 + 256],
                                                      qp[:, pp * 256:pp * 256 + 256],
                                                      s.qcols[:, 8 + m:9 + m], qa[:],
                                                      OP.is_lt, OP.subtract))
                    dvm[k] = s.S_DV.n

                @blk.tensor
                def _(t, k=k, i=i, pp=pp):
                    t.wait_ge(s.S_DV.h, dvm[k])
                    if k == 0:
                        t.wait_ge(s.S_GP.h, self.gp_idents)
                    if i % 2 == 0 and k - 1 in evm:
                        t.wait_ge(s.S_DV.h, evm[k - 1])
                    for dcl in range(2):
                        mm = t.transpose(ps_q[:, dcl * 256 + (i % 2) * 128:
                                              dcl * 256 + (i % 2) * 128 + 128],
                                         qt[:, pp * 256 + dcl * 128:pp * 256 + dcl * 128 + 128],
                                         s.ident_b[:])
                    pem[k] = s.S_PE.inc(mm)

                if i % 2 == 1:
                    @blk.vector
                    def _(v, k=k, i=i):
                        v.wait_ge(s.S_PE.h, pem[k])
                        hp = i // 2
                        half = hp % 2
                        if ("st", half) in stm:
                            sem, n = stm[("st", half)]
                            v.wait_ge(sem.h, n)
                        v.wait_ge(s.S_DV.h, s.S_DV.n)
                        s.S_DV.inc(v.tensor_copy(qasm[:, half * 512:half * 512 + 512], ps_q[:, 0:512]))
                        evm[k] = s.S_DV.n

                    @blk.scalar
                    def _(sc, k=k, mat=mat, i=i):
                        sc.wait_ge(s.S_DV.h, evm[k])
                        hp = i // 2
                        half = hp % 2
                        base = (W1_OFF if mat == "w1" else W3_OFF) + hp * 2 * BLK
                        dst = AP(s.q_in, base, [[512, 128], [256, 2], [1, 256]])
                        _, n = s.Q_ST.issue(sc, dst,
                                            qasm.ap()[:, half * 512:half * 512 + 512]
                                            .rearrange("p (a b) -> p a b", a=2))
                        stm[("st", half)] = (s.Q_ST.sems[(s.Q_ST.i - 1) % 2], n)

                seg = {("w1", HC // 2 - 1): ("q1", 0), ("w3", HC // 2 - 1): ("q3", 0),
                       ("w1", HC - 1): ("q1", 1), ("w3", HC - 1): ("q3", 1)}
                if (mat, i) in seg:
                    which, gb = seg[(mat, i)]

                    @blk.gpsimd
                    def _(g, which=which, gb=gb, mat=mat):
                        for q in s.Q_ST.sems:
                            q.wait(g)
                        src_off = (W1_OFF if mat == "w1" else W3_OFF) + gb * HWH
                        outt = s.q1_out if which == "q1" else s.q3_out
                        s.S_CC.inc(g.collective_compute(
                            "AllGather", OP.bypass, replica_groups=[list(range(NCORES))],
                            ins=[AP(s.q_in, src_off, [[1, HWH]])],
                            outs=[AP(outt, gb * NCORES * HWH, [[1, NCORES * HWH]]).opt()]))
                        setattr(s, "cc_%s%s" % (which, "ab"[gb]), s.S_CC.n)

            self.drain_all(blk)

        with nc.Block() as blk, contextlib.ExitStack() as ar:
            A = lambda nm, sh, dt=F32: ar.enter_context(nc.sbuf_tensor(nm, sh, dt))
            qp = A("qp3", [128, 2816])
            qt0 = A("qt30", [128, 2816], BF16)
            qt1 = A("qt31", [128, 2816], BF16)
            qa = A("qa3", [128, 2816], BF16)
            qasm = A("qasm3", [128, 2 * 256], BF16)
            ps_q = ar.enter_context(nc.psum_tensor("ps_q3", [128, 2048], BF16))
            ldm, dvm, pem, evm, stm = {}, {}, {}, {}, {}
            for hhalf in range(2):
                for r in range(2):
                    @blk.sync
                    def _(sy, r=r, hhalf=hhalf, dvn=None):
                        sy.wait_ge(s.S_DV.h, s.S_DV.n) if s.S_DV.n else None
                        _, n = s.Q_SY.issue(sy, qp[:], w2_src(r, hhalf))
                        ldm[(hhalf, r)] = (s.Q_SY.sems[(s.Q_SY.i - 1) % 3], n)

                    @blk.vector
                    def _(v, r=r, hhalf=hhalf, pen=None):
                        sem, n = ldm[(hhalf, r)]
                        v.wait_ge(sem.h, n)
                        if s.S_PE.n:
                            v.wait_ge(s.S_PE.h, s.S_PE.n)
                        v.wait_ge(s.S_DV.h, s.S_DV.n)
                        s.S_DV.inc(v.tensor_single_scalar(qa[:], qp[:],
                                                          s.qcols[:, 6:7], OP.is_gt))
                        v.wait_ge(s.S_DV.h, s.S_DV.n)
                        qt = qt0 if r == 0 else qt1
                        s.S_DV.inc(v.scalar_tensor_tensor(qt[:], qp[:], s.qcols[:, 10:11],
                                                          qa[:], OP.is_lt, OP.subtract))
                        dvm[(hhalf, r)] = s.S_DV.n

                for sl in range(HP):
                    pp = sl % 2

                    @blk.tensor
                    def _(t, sl=sl, pp=pp, hhalf=hhalf):
                        t.wait_ge(s.S_DV.h, dvm[(hhalf, 1)])
                        if (hhalf, sl - 2) in evm:
                            t.wait_ge(s.S_DV.h, evm[(hhalf, sl - 2)])
                        elif hhalf and (hhalf - 1, HP - 2 + sl) in evm:
                            t.wait_ge(s.S_DV.h, evm[(hhalf - 1, HP - 2 + sl)])
                        mm = t.transpose(ps_q[:, pp * 1024:pp * 1024 + 128],
                                         qt0[:, sl * 128:sl * 128 + 128], s.ident_b[:])
                        mm = t.transpose(ps_q[:, pp * 1024 + 128:pp * 1024 + 256],
                                         qt1[:, sl * 128:sl * 128 + 128], s.ident_b[:])
                        pem[(hhalf, sl)] = s.S_PE.inc(mm)

                    @blk.vector
                    def _(v, sl=sl, pp=pp, hhalf=hhalf):
                        v.wait_ge(s.S_PE.h, pem[(hhalf, sl)])
                        if ("st", pp) in stm:
                            sem, n = stm[("st", pp)]
                            v.wait_ge(sem.h, n)
                        v.wait_ge(s.S_DV.h, s.S_DV.n)
                        s.S_DV.inc(v.tensor_copy(qasm[:, pp * 256:pp * 256 + 256],
                                                 ps_q[:, pp * 1024:pp * 1024 + 256]))
                        evm[(hhalf, sl)] = s.S_DV.n

                    @blk.scalar
                    def _(sc, sl=sl, pp=pp, hhalf=hhalf):
                        sc.wait_ge(s.S_DV.h, evm[(hhalf, sl)])
                        hc = hhalf * HP + sl
                        dst = AP(s.q_in, W2_OFF + (hc // 2) * 2 * BLK + (hc % 2) * 256,
                                 [[512, 128], [1, 256]])
                        _, n = s.Q_ST.issue(sc, dst, qasm[:, pp * 256:pp * 256 + 256])
                        stm[("st", pp)] = (s.Q_ST.sems[(s.Q_ST.i - 1) % 2], n)

            @blk.gpsimd
            def _(g):
                for q in s.Q_ST.sems:
                    q.wait(g)
                s.S_CC.inc(g.collective_compute(
                    "AllGather", OP.bypass, replica_groups=[list(range(NCORES))],
                    ins=[AP(s.q_in, W2_OFF, [[1, W2_SHARD]])],
                    outs=[s.q2_out.ap().opt()]))
                s.cc_w2 = s.S_CC.n
                s.S_CC.inc(g.collective_compute(
                    "AllGather", OP.bypass, replica_groups=[list(range(NCORES))],
                    ins=[AP(s.pc_in, 0, [[1, 256 * PROJ]])],
                    outs=[AP(s.ipw_bf, 0, [[1, D * PROJ]]).opt()]))
                s.cc_ipw = s.S_CC.n

            self.drain_all(blk)

    def p_ffn(self):
        s = self
        nc = s.nc
        with nc.Block() as blk, contextlib.ExitStack() as ar:
            A = lambda nm, sh, dt=F32: ar.enter_context(nc.sbuf_tensor(nm, sh, dt))
            P_ = lambda nm, sh, dt=F32: ar.enter_context(nc.psum_tensor(nm, sh, dt))
            nm0 = "f%d" % s.S_PE.n
            xbf = A(nm0 + "xbf", [128, 2 * TH], BF16)
            x2bf = A(nm0 + "x2bf", [128, 2 * TH], BF16)
            xct = A(nm0 + "xct", [128, 2 * TH])
            xint = A(nm0 + "xint", [128, DC * TH], BF16)
            runmax = A(nm0 + "runmax", [128, TH])
            runmin = A(nm0 + "runmin", [128, TH])
            hmem = A(nm0 + "hmem", [128, HC * TH], BF16)
            tmpq = A(nm0 + "tmpq", [128, 2 * TH])
            t2b = A(nm0 + "t2b", [128, 2 * TH])
            colsb = A(nm0 + "colsb", [128, 8])
            combsb = A(nm0 + "combsb", [128, TH])
            invsqsb = A(nm0 + "invsqsb", [128, TH])
            shsb = A(nm0 + "shsb", [128, TH])
            wslab = A(nm0 + "wslab", [128, 2 * 2 * 4096], BF16)
            pb = [P_(nm0 + "pb%d" % i, [128, 512]) for i in range(6)]

            QW = Dma([s.SEM(nm0 + "qw%d" % i, 16) for i in range(3)])

            prev_peTh = [0]
            for th in range(2):
                T0 = th * TH
                mk = {}

                for dc in range(DC):
                    pp = dc % 2

                    @blk.scalar
                    def _(sc, dc=dc, pp=pp):
                        if ("pe_st", dc - 2) in mk:
                            sc.wait_ge(s.S_PE.h, mk[("pe_st", dc - 2)])
                        s.S_AC.inc(sc.activation(x2bf[:, pp * TH:pp * TH + TH],
                                                 s.resid[:, dc * TOK + T0:dc * TOK + T0 + TH],
                                                 AF.Square))
                        mk[("ac_sq", dc)] = s.S_AC.n

                    @blk.vector
                    def _(v, dc=dc, pp=pp):
                        if ("pe_st", dc - 2) in mk:
                            v.wait_ge(s.S_PE.h, mk[("pe_st", dc - 2)])
                        v.wait_ge(s.S_DV.h, s.S_DV.n)
                        s.S_DV.inc(v.tensor_copy(xbf[:, pp * TH:pp * TH + TH],
                                                 s.resid[:, dc * TOK + T0:dc * TOK + T0 + TH]))
                        mk[("dv_cp", dc)] = s.S_DV.n

                    @blk.vector
                    def _(v, dc=dc):
                        src = s.resid[:, dc * TOK + T0:dc * TOK + T0 + TH]
                        if dc == 0 and prev_peTh[0]:
                            v.wait_ge(s.S_PE.h, prev_peTh[0])
                        v.wait_ge(s.S_DV.h, s.S_DV.n)
                        if dc == 0:
                            s.S_DV.inc(v.tensor_copy(runmax[:], src))
                            s.S_DV.inc(v.tensor_copy(runmin[:], src))
                        else:
                            s.S_DV.inc(v.tensor_tensor(runmax[:], runmax[:], src, OP.max))
                            s.S_DV.inc(v.tensor_tensor(runmin[:], runmin[:], src, OP.min))

                    @blk.tensor
                    def _(t, dc=dc, pp=pp):
                        if dc == 0:
                            t.wait_ge(s.S_DV.h, s.S_DV.n)
                            t.wait_ge(s.S_GP.h, s.S_GP.n)
                        t.wait_ge(s.S_DV.h, mk[("dv_cp", dc)])
                        t.wait_ge(s.S_AC.h, mk[("ac_sq", dc)])
                        t.matmul(pb[0][:], s.ones_b[:], xbf[:, pp * TH:pp * TH + TH],
                                 start=(dc == 0), stop=(dc == DC - 1))
                        mm = t.matmul(pb[1][:], s.ones_b[:], x2bf[:, pp * TH:pp * TH + TH],
                                      start=(dc == 0), stop=(dc == DC - 1))
                        mk[("pe_st", dc)] = s.S_PE.inc(mm)

                @blk.vector
                def _(v):
                    v.wait_ge(s.S_PE.h, mk[("pe_st", DC - 1)])
                    v.wait_ge(s.S_DV.h, s.S_DV.n)
                    s.S_DV.inc(v.tensor_scalar_mul(s.row(0), pb[0][0:1, :], 1.0 / D))
                    s.S_DV.inc(v.tensor_scalar_mul(s.row(1), pb[1][0:1, :], 1.0 / D))
                    s.S_DV.inc(v.tensor_scalar_mul(s.row(11), pb[0][0:1, :], 1.0 / D))
                    mk["r01"] = s.S_DV.n

                @blk.scalar
                def _(sc):
                    sc.wait_ge(s.S_DV.h, mk["r01"])
                    s.S_AC.inc(sc.activation(s.row(2), s.row(0), AF.Square))
                    mk["r2"] = s.S_AC.n

                @blk.vector
                def _(v):
                    v.wait_ge(s.S_AC.h, mk["r2"])
                    v.wait_ge(s.S_DV.h, s.S_DV.n)
                    s.S_DV.inc(v.tensor_tensor(s.row(3), s.row(1), s.row(2), OP.subtract))
                    mk["r3"] = s.S_DV.n

                @blk.scalar
                def _(sc):
                    sc.wait_ge(s.S_DV.h, mk["r3"])
                    s.S_AC.inc(sc.activation(s.row(4), s.row(3), AF.Sqrt, bias=s.epscol[0:1, 0:1]))
                    mk["r4"] = s.S_AC.n

                @blk.vector
                def _(v):
                    v.wait_ge(s.S_AC.h, mk["r4"])
                    v.wait_ge(s.S_DV.h, s.S_DV.n)
                    s.S_DV.inc(v.reciprocal(s.row(5), s.row(4)))
                    mk["r5"] = s.S_DV.n

                if K_FFN_PARTS() < 2:
                    continue
                @blk.vector
                def _(v):
                    v.wait_ge(s.S_DV.h, s.S_DV.n)
                    s.S_DV.inc(v.tensor_scalar_mul(runmin[:], runmin[:], -1.0))
                    mk["xneg"] = s.S_DV.n

                @blk.tensor
                def _(t):
                    t.wait_ge(s.S_DV.h, mk["xneg"])
                    for i in range(4):
                        t.transpose(pb[4][:, i * 128:i * 128 + 128],
                                    runmax[:, i * 128:i * 128 + 128], s.ident_f[:])
                    for i in range(4):
                        mm = t.transpose(pb[2][:, i * 128:i * 128 + 128],
                                         runmin[:, i * 128:i * 128 + 128], s.ident_f[:])
                    mk["peTx"] = s.S_PE.inc(mm)

                @blk.vector
                def _(v):
                    v.wait_ge(s.S_PE.h, mk["peTx"])
                    for i in range(4):
                        v.wait_ge(s.S_DV.h, s.S_DV.n)
                        s.S_DV.inc(v.tensor_reduce(colsb[:, i:i + 1],
                                                   pb[4][:, i * 128:i * 128 + 128],
                                                   AX.X, OP.max))
                    for i in range(4):
                        v.wait_ge(s.S_DV.h, s.S_DV.n)
                        s.S_DV.inc(v.tensor_reduce(colsb[:, 4 + i:5 + i],
                                                   pb[2][:, i * 128:i * 128 + 128],
                                                   AX.X, OP.max))
                    mk["dvRx"] = s.S_DV.n

                @blk.tensor
                def _(t):
                    t.wait_ge(s.S_DV.h, mk["dvRx"])
                    for i in range(4):
                        t.transpose(pb[5][0:1, i * 128:i * 128 + 128],
                                    colsb[:, i:i + 1], s.ident_f[:])
                    for i in range(4):
                        mm = t.transpose(pb[3][0:1, i * 128:i * 128 + 128],
                                         colsb[:, 4 + i:5 + i], s.ident_f[:])
                    mk["peRx"] = s.S_PE.inc(mm)

                @blk.vector
                def _(v):
                    v.wait_ge(s.S_PE.h, mk["peRx"])
                    v.wait_ge(s.S_DV.h, s.S_DV.n)
                    s.S_DV.inc(v.tensor_tensor(s.row(6), pb[5][0:1, 0:TH], s.row(11),
                                               OP.subtract))
                    v.wait_ge(s.S_DV.h, s.S_DV.n)
                    s.S_DV.inc(v.tensor_tensor(s.row(2), pb[3][0:1, 0:TH], s.row(11), OP.add))
                    v.wait_ge(s.S_DV.h, s.S_DV.n)
                    s.S_DV.inc(v.tensor_tensor(s.row(6), s.row(6), s.row(2), OP.max))
                    v.wait_ge(s.S_DV.h, s.S_DV.n)
                    s.S_DV.inc(v.tensor_tensor(s.row(7), s.row(6), s.row(5), OP.mult))
                    v.wait_ge(s.S_DV.h, s.S_DV.n)
                    s.S_DV.inc(v.tensor_scalar_max(s.row(8), s.row(7), EPS))
                    v.wait_ge(s.S_DV.h, s.S_DV.n)
                    s.S_DV.inc(v.reciprocal(s.row(9), s.row(8)))
                    v.wait_ge(s.S_DV.h, s.S_DV.n)
                    s.S_DV.inc(v.tensor_scalar_mul(s.row(9), s.row(9), 127.0))
                    v.wait_ge(s.S_DV.h, s.S_DV.n)
                    s.S_DV.inc(v.tensor_tensor(s.row(10), s.row(5), s.row(9), OP.mult))
                    s.S_DV.inc(v.tensor_scalar_mul(s.row(11), s.row(8), 1.0 / 127.0))
                    mk["rq"] = s.S_DV.n

                @blk.tensor
                def _(t):
                    t.wait_ge(s.S_DV.h, mk["rq"])
                    mm = t.matmul(pb[1][:], s.ones1[:], s.row(10), start=True, stop=True)
                    mk["pe_comb"] = s.S_PE.inc(mm)

                @blk.vector
                def _(v):
                    v.wait_ge(s.S_PE.h, mk["pe_comb"])
                    v.wait_ge(s.S_DV.h, s.S_DV.n)
                    s.S_DV.inc(v.tensor_copy(combsb[:], pb[1][:]))
                    mk["dv_comb"] = s.S_DV.n

                @blk.tensor
                def _(t):
                    t.wait_ge(s.S_DV.h, mk["dv_comb"])
                    mm = t.matmul(pb[1][:], s.ones1[:], s.row(11), start=True, stop=True)
                    mk["pe_invsq"] = s.S_PE.inc(mm)

                @blk.vector
                def _(v):
                    v.wait_ge(s.S_PE.h, mk["pe_invsq"])
                    v.wait_ge(s.S_DV.h, s.S_DV.n)
                    s.S_DV.inc(v.tensor_copy(invsqsb[:], pb[1][:]))
                    mk["dv_invsq"] = s.S_DV.n

                if K_FFN_PARTS() < 3:
                    continue
                for dc in range(DC):
                    pp = dc % 2

                    @blk.vector
                    def _(v, dc=dc, pp=pp):
                        if dc == 0:
                            v.wait_ge(s.S_DV.h, mk["dv_comb"])
                        sl = xct[:, pp * TH:pp * TH + TH]
                        v.wait_ge(s.S_DV.h, s.S_DV.n)
                        s.S_DV.inc(v.scalar_tensor_tensor(
                            sl, pb[0][:], -1.0 / D,
                            s.resid[:, dc * TOK + T0:dc * TOK + T0 + TH], OP.mult, OP.add))
                        v.wait_ge(s.S_DV.h, s.S_DV.n)
                        s.S_DV.inc(v.tensor_tensor(sl, sl, combsb[:], OP.mult))
                        v.wait_ge(s.S_DV.h, s.S_DV.n)
                        s.S_DV.inc(v.tensor_scalar(xint[:, dc * TH:dc * TH + TH], sl,
                                                   MAGIC, MAGIC, OP.add, OP.subtract))
                        mk[("dv_xi", dc)] = s.S_DV.n

                if K_FFN_PARTS() < 4:
                    continue
                def slab_src(mat, hp):
                    t_ = s.q1_out if mat == "w1" else s.q3_out
                    ga, hpl = divmod(hp, HP // 2)
                    base = ga * NCORES * HWH + hpl * 2 * BLK
                    return AP(t_, base, [[512, 128], [HWH, 8], [1, 512]])

                for hp in range(HP):
                    sp = hp % 2

                    @blk.sync
                    def _(sy, hp=hp, sp=sp, pen=None):
                        if ("pe_b", 2 * (hp - 2) + 1) in mk:
                            sy.wait_ge(s.S_PE.h, mk[("pe_b", 2 * (hp - 2) + 1)])
                        if hp <= 1:
                            sy.wait_ge(s.S_PE.h, s.S_PE.n)
                        if hp == 0:
                            sy.wait_ge(s.S_CC.h, s.cc_q3a)
                        if hp == HP // 2:
                            sy.wait_ge(s.S_CC.h, s.cc_q3b)
                        marks = []
                        for mi, mat in enumerate(("w1", "w3")):
                            off = sp * 8192 + mi * 4096
                            dst = AP(wslab, off, [[16384, 128], [512, 8], [1, 512]])
                            _, n = QW.issue(sy, dst, slab_src(mat, hp))
                            marks.append((QW.sems[(QW.i - 1) % 3], n))
                        mk[("ld_w", hp)] = marks

                    for sub in range(2):
                        hc = 2 * hp + sub
                        bq = 2 + 2 * (hc % 2)

                        @blk.tensor
                        def _(t, hp=hp, sp=sp, sub=sub, hc=hc, bq=bq):
                            for sem, n in mk[("ld_w", hp)]:
                                t.wait_ge(sem.h, n)
                            if hc == 0:
                                t.wait_ge(s.S_DV.h, mk[("dv_xi", DC - 1)])
                            if ("ac_t2", hc - 2) in mk:
                                t.wait_ge(s.S_AC.h, mk[("ac_t2", hc - 2)])
                                t.wait_ge(s.S_DV.h, mk[("dv_h", hc - 2)])
                            for dc in range(DC):
                                t.matmul(pb[bq][:],
                                         wslab[:, sp * 8192 + dc * 256 + sub * 128:
                                               sp * 8192 + dc * 256 + sub * 128 + 128],
                                         xint[:, dc * TH:dc * TH + TH],
                                         start=(dc == 0), stop=(dc == DC - 1))
                            for dc in range(DC):
                                mm = t.matmul(pb[bq + 1][:],
                                              wslab[:, sp * 8192 + 4096 + dc * 256 + sub * 128:
                                                    sp * 8192 + 4096 + dc * 256 + sub * 128 + 128],
                                              xint[:, dc * TH:dc * TH + TH],
                                              start=(dc == 0), stop=(dc == DC - 1))
                            mk[("pe_b", hc)] = s.S_PE.inc(mm)

                        @blk.vector
                        def _(v, hc=hc, bq=bq):
                            v.wait_ge(s.S_PE.h, mk[("pe_b", hc)])
                            v.wait_ge(s.S_DV.h, s.S_DV.n)
                            s.S_DV.inc(v.scalar_tensor_tensor(
                                pb[bq][:], pb[bq][:], s.meansn[:, 0:1], invsqsb[:],
                                OP.mult, OP.mult))
                            mk[("dv_b1", hc)] = s.S_DV.n

                        @blk.scalar
                        def _(sc, hc=hc, bq=bq):
                            sc.wait_ge(s.S_DV.h, mk[("dv_b1", hc)])
                            pp = hc % 2
                            s.S_AC.inc(sc.activation(t2b[:, pp * TH:pp * TH + TH],
                                                     pb[bq][:], AF.Silu))
                            mk[("ac_t2", hc)] = s.S_AC.n

                        @blk.vector
                        def _(v, hc=hc, bq=bq):
                            v.wait_ge(s.S_AC.h, mk[("ac_t2", hc)])
                            pp = hc % 2
                            v.wait_ge(s.S_DV.h, s.S_DV.n)
                            s.S_DV.inc(v.tensor_tensor(hmem[:, hc * TH:hc * TH + TH],
                                                       t2b[:, pp * TH:pp * TH + TH],
                                                       pb[bq + 1][:], OP.mult))
                            mk[("dv_h", hc)] = s.S_DV.n

                        @blk.vector
                        def _(v, hc=hc):
                            v.wait_ge(s.S_DV.h, s.S_DV.n)
                            hsl = hmem[:, hc * TH:hc * TH + TH]
                            if hc == 0:
                                s.S_DV.inc(v.tensor_copy(runmax[:], hsl))
                                s.S_DV.inc(v.tensor_copy(runmin[:], hsl))
                            else:
                                s.S_DV.inc(v.tensor_tensor(runmax[:], runmax[:], hsl, OP.max))
                                s.S_DV.inc(v.tensor_tensor(runmin[:], runmin[:], hsl, OP.min))
                            mk[("gp_hf", hc)] = s.S_DV.n

                if K_FFN_PARTS() < 5:
                    continue
                @blk.vector
                def _(v):
                    v.wait_ge(s.S_DV.h, s.S_DV.n)
                    s.S_DV.inc(v.scalar_tensor_tensor(runmax[:], runmin[:], -1.0,
                                                      runmax[:], OP.mult, OP.max))
                    mk["hboth"] = s.S_DV.n

                @blk.tensor
                def _(t):
                    t.wait_ge(s.S_AC.h, mk[("ac_t2", HC - 1)])
                    t.wait_ge(s.S_DV.h, mk["hboth"])
                    for i in range(4):
                        mm = t.transpose(pb[4][:, i * 128:i * 128 + 128],
                                         runmax[:, i * 128:i * 128 + 128], s.ident_f[:])
                    mk["peTh"] = s.S_PE.inc(mm)
                    prev_peTh[0] = mk["peTh"]

                @blk.vector
                def _(v):
                    v.wait_ge(s.S_PE.h, mk["peTh"])
                    for i in range(4):
                        v.wait_ge(s.S_DV.h, s.S_DV.n)
                        s.S_DV.inc(v.tensor_reduce(colsb[:, i:i + 1],
                                                   pb[4][:, i * 128:i * 128 + 128],
                                                   AX.X, OP.max))
                    mk["dvRh"] = s.S_DV.n

                @blk.tensor
                def _(t):
                    t.wait_ge(s.S_DV.h, mk["dvRh"])
                    for i in range(4):
                        mm = t.transpose(pb[5][0:1, i * 128:i * 128 + 128],
                                         colsb[:, i:i + 1], s.ident_f[:])
                    mk["peRh"] = s.S_PE.inc(mm)

                @blk.vector
                def _(v):
                    v.wait_ge(s.S_PE.h, mk["peRh"])
                    v.wait_ge(s.S_DV.h, s.S_DV.n)
                    s.S_DV.inc(v.tensor_scalar_mul(s.row(5), s.row(11), s.qcols[0:1, 1:2]))
                    v.wait_ge(s.S_DV.h, s.S_DV.n)
                    s.S_DV.inc(v.tensor_tensor(s.row(13), pb[5][0:1, 0:TH], s.row(5), OP.mult))
                    v.wait_ge(s.S_DV.h, s.S_DV.n)
                    s.S_DV.inc(v.tensor_scalar_max(s.row(14), s.row(13), EPS))
                    v.wait_ge(s.S_DV.h, s.S_DV.n)
                    s.S_DV.inc(v.reciprocal(s.row(15), s.row(14)))
                    v.wait_ge(s.S_DV.h, s.S_DV.n)
                    s.S_DV.inc(v.tensor_scalar_mul(s.row(15), s.row(15), 127.0))
                    v.wait_ge(s.S_DV.h, s.S_DV.n)
                    s.S_DV.inc(v.tensor_tensor(s.row(16), s.row(15), s.row(5), OP.mult))
                    s.S_DV.inc(v.tensor_scalar_mul(s.row(17), s.row(14), 1.0 / 127.0))
                    mk["rh"] = s.S_DV.n

                @blk.tensor
                def _(t):
                    t.wait_ge(s.S_DV.h, mk["rh"])
                    mm = t.matmul(pb[1][:], s.ones1[:], s.row(16), start=True, stop=True)
                    mk["pe_sh"] = s.S_PE.inc(mm)

                @blk.vector
                def _(v):
                    v.wait_ge(s.S_PE.h, mk["pe_sh"])
                    v.wait_ge(s.S_DV.h, s.S_DV.n)
                    s.S_DV.inc(v.tensor_copy(shsb[:], pb[1][:]))
                    mk["dv_sh"] = s.S_DV.n

                @blk.tensor
                def _(t):
                    t.wait_ge(s.S_DV.h, mk["dv_sh"])
                    mm = t.matmul(pb[1][:], s.ones1[:], s.row(17), start=True, stop=True)
                    mk["pe_ish"] = s.S_PE.inc(mm)

                @blk.vector
                def _(v):
                    v.wait_ge(s.S_PE.h, mk["pe_ish"])
                    v.wait_ge(s.S_DV.h, s.S_DV.n)
                    s.S_DV.inc(v.tensor_copy(invsqsb[:], pb[1][:]))
                    mk["dv_ish"] = s.S_DV.n

                if K_FFN_PARTS() < 6:
                    continue
                for hq in range(HC // 2):
                    @blk.vector
                    def _(v, hq=hq):
                        if hq == 0:
                            v.wait_ge(s.S_DV.h, mk["dv_sh"])
                        v.wait_ge(s.S_DV.h, s.S_DV.n)
                        shrep = shsb.ap().unsqueeze(1).broadcast_to([128, 2, TH])
                        s.S_DV.inc(v.tensor_tensor(
                            tmpq.ap().rearrange("p (a b) -> p a b", a=2),
                            hmem.ap()[:, hq * 2 * TH:hq * 2 * TH + 2 * TH]
                            .rearrange("p (a b) -> p a b", a=2), shrep, OP.mult))
                        v.wait_ge(s.S_DV.h, s.S_DV.n)
                        s.S_DV.inc(v.tensor_scalar(hmem[:, hq * 2 * TH:hq * 2 * TH + 2 * TH],
                                                   tmpq[:],
                                                   MAGIC, MAGIC, OP.add, OP.subtract))
                        mk[("dv_hi", 2 * hq)] = s.S_DV.n
                        mk[("dv_hi", 2 * hq + 1)] = s.S_DV.n

                if K_FFN_PARTS() < 7:
                    continue
                for dq in range(NCORES):
                    bk = 2 + 2 * (dq % 2)
                    for h2 in range(2):

                        @blk.sync
                        def _(sy, dq=dq, h2=h2):
                            if dq == 0:
                                sy.wait_ge(s.S_PE.h,
                                           mk[("pe_b", HC - 3 if h2 == 0 else HC - 1)])
                                sy.wait_ge(s.S_CC.h, s.cc_w2)
                            if ("pe_w2", dq - 1, h2) in mk:
                                sy.wait_ge(s.S_PE.h, mk[("pe_w2", dq - 1, h2)])
                            src = AP(s.q2_out, dq * W2_SHARD + h2 * HP * BLK,
                                     [[512, 128], [2 * BLK, HP // 2], [1, 512]])
                            dst = AP(wslab, h2 * HP * 256,
                                     [[16384, 128], [512, HP // 2], [1, 512]])
                            _, n = QW.issue(sy, dst, src)
                            mk[("ld_w2", dq, h2)] = (QW.sems[(QW.i - 1) % 3], n)

                        @blk.tensor
                        def _(t, dq=dq, h2=h2, bk=bk):
                            sem, n = mk[("ld_w2", dq, h2)]
                            t.wait_ge(sem.h, n)
                            if dq == 0:
                                t.wait_ge(s.S_DV.h, mk[("dv_hi", HP - 1 if h2 == 0 else HC - 1)])
                            if h2 == 0 and ("dv_y2", 2 * (dq - 2) + 1) in mk:
                                t.wait_ge(s.S_DV.h, mk[("dv_y2", 2 * (dq - 2) + 1)])
                            for d2 in range(2):
                                for hl in range(HP):
                                    hcc = h2 * HP + hl
                                    mm = t.matmul(pb[bk + d2][:],
                                                  wslab[:, h2 * HP * 256 + hl * 256 + d2 * 128:
                                                        h2 * HP * 256 + hl * 256 + d2 * 128 + 128],
                                                  hmem[:, hcc * TH:hcc * TH + TH],
                                                  start=(hcc == 0), stop=(hcc == HC - 1))
                            mk[("pe_w2", dq, h2)] = s.S_PE.inc(mm)

                    for d2 in range(2):
                        dco = dq * 2 + d2

                        @blk.vector
                        def _(v, dq=dq, d2=d2, dco=dco, bk=bk):
                            v.wait_ge(s.S_PE.h, mk[("pe_w2", dq, 1)])
                            v.wait_ge(s.S_DV.h, s.S_DV.n)
                            s.S_DV.inc(v.scalar_tensor_tensor(
                                pb[bk + d2][:], pb[bk + d2][:], s.meansn[:, 3:4], invsqsb[:],
                                OP.mult, OP.mult))
                            v.wait_ge(s.S_DV.h, s.S_DV.n)
                            s.S_DV.inc(v.tensor_tensor(
                                s.resid[:, dco * TOK + T0:dco * TOK + T0 + TH],
                                s.resid[:, dco * TOK + T0:dco * TOK + T0 + TH],
                                pb[bk + d2][:], OP.add))
                            mk[("dv_y2", dco)] = s.S_DV.n

            self.drain_all(blk, extra=(QW,))

    def p_proj_conv(self):
        s = self
        nc = s.nc
        with nc.Block() as blk, contextlib.ExitStack() as ar:
            A = lambda nm, sh, dt=F32: ar.enter_context(nc.sbuf_tensor(nm, sh, dt))
            P_ = lambda nm, sh, dt=F32: ar.enter_context(nc.psum_tensor(nm, sh, dt))
            xln = A("xln", [128, DC * TOK], BF16)
            xbf = A("pxbf", [128, TH], BF16)
            x2bf = xbf
            xct = A("pxct", [128, 2 * TH])
            zt = A("pzt", [128, TOK], BF16)
            hstage = A("hstage", [128, CCH * 3], BF16)
            ipslab = A("ipslab", [128, 2 * DC * 256], BF16)
            acc = xct
            pj0 = P_("pj0", [128, 512])
            pj1 = P_("pj1", [128, 512])
            pot = [P_("pot%d" % i, [128, TOK]) for i in range(2)]
            QIP = Dma([s.SEM("qip%d" % i, 16) for i in range(2)])
            mk = {}

            for tt in range(2):
                T0 = tt * TH
                for dc in range(DC):
                    pp = dc % 2

                    @blk.vector
                    def _(v, dc=dc, T0=T0, tt=tt):
                        if ("pe_st2", tt, dc - 1) in mk:
                            v.wait_ge(s.S_PE.h, mk[("pe_st2", tt, dc - 1)])
                        v.wait_ge(s.S_DV.h, s.S_DV.n)
                        s.S_DV.inc(v.tensor_copy(xbf[:],
                                                 s.resid[:, dc * TOK + T0:dc * TOK + T0 + TH]))
                        mk[("dv_cp", tt, dc)] = s.S_DV.n

                    @blk.tensor
                    def _(t, dc=dc, tt=tt):
                        if dc == 0:
                            t.wait_ge(s.S_DV.h, s.S_DV.n)
                        t.wait_ge(s.S_DV.h, mk[("dv_cp", tt, dc)])
                        mm = t.matmul(pj0[:], s.ones_b[:], xbf[:],
                                      start=(dc == 0), stop=(dc == DC - 1))
                        mk[("pe_st1", tt, dc)] = s.S_PE.inc(mm)

                    @blk.scalar
                    def _(sc, dc=dc, T0=T0, tt=tt):
                        sc.wait_ge(s.S_PE.h, mk[("pe_st1", tt, dc)])
                        s.S_AC.inc(sc.activation(x2bf[:],
                                                 s.resid[:, dc * TOK + T0:dc * TOK + T0 + TH],
                                                 AF.Square))
                        mk[("ac_sq", tt, dc)] = s.S_AC.n

                    @blk.tensor
                    def _(t, dc=dc, tt=tt):
                        t.wait_ge(s.S_AC.h, mk[("ac_sq", tt, dc)])
                        mm = t.matmul(pj1[:], s.ones_b[:], x2bf[:],
                                      start=(dc == 0), stop=(dc == DC - 1))
                        mk[("pe_st2", tt, dc)] = s.S_PE.inc(mm)
                        mk[("pe_st", tt, dc)] = s.S_PE.n

                @blk.vector
                def _(v, tt=tt):
                    v.wait_ge(s.S_PE.h, mk[("pe_st", tt, DC - 1)])
                    v.wait_ge(s.S_DV.h, s.S_DV.n)
                    s.S_DV.inc(v.tensor_scalar_mul(s.row(0), pj0[0:1, :], 1.0 / D))
                    s.S_DV.inc(v.tensor_scalar_mul(s.row(1), pj1[0:1, :], 1.0 / D))
                    mk["r01"] = s.S_DV.n

                @blk.scalar
                def _(sc, tt=tt):
                    sc.wait_ge(s.S_DV.h, mk["r01"])
                    s.S_AC.inc(sc.activation(s.row(2), s.row(0), AF.Square))
                    mk["r2"] = s.S_AC.n

                @blk.vector
                def _(v, tt=tt):
                    v.wait_ge(s.S_AC.h, mk["r2"])
                    v.wait_ge(s.S_DV.h, s.S_DV.n)
                    s.S_DV.inc(v.tensor_tensor(s.row(3), s.row(1), s.row(2), OP.subtract))
                    mk["r3"] = s.S_DV.n

                @blk.scalar
                def _(sc, tt=tt):
                    sc.wait_ge(s.S_DV.h, mk["r3"])
                    s.S_AC.inc(sc.activation(s.row(4), s.row(3), AF.Sqrt, bias=s.epscol[0:1, 0:1]))
                    mk["r4"] = s.S_AC.n

                @blk.vector
                def _(v, tt=tt):
                    v.wait_ge(s.S_AC.h, mk["r4"])
                    v.wait_ge(s.S_DV.h, s.S_DV.n)
                    s.S_DV.inc(v.reciprocal(s.row(5), s.row(4)))
                    mk["r5"] = s.S_DV.n

                @blk.tensor
                def _(t, tt=tt):
                    t.wait_ge(s.S_DV.h, mk["r5"])
                    mm = t.matmul(pj1[:], s.ones1[:], s.row(5), start=True, stop=True)
                    mk[("pe_rs", tt)] = s.S_PE.inc(mm)

                for dc in range(DC):
                    pp = dc % 2

                    @blk.vector
                    def _(v, dc=dc, pp=pp, T0=T0, tt=tt):
                        if dc == 0:
                            v.wait_ge(s.S_PE.h, mk[("pe_rs", tt)])
                        v.wait_ge(s.S_DV.h, s.S_DV.n)
                        sl = xct[:, pp * TH:pp * TH + TH]
                        s.S_DV.inc(v.scalar_tensor_tensor(
                            sl, pj0[:], -1.0 / D,
                            s.resid[:, dc * TOK + T0:dc * TOK + T0 + TH], OP.mult, OP.add))
                        v.wait_ge(s.S_DV.h, s.S_DV.n)
                        s.S_DV.inc(v.tensor_tensor(xln[:, dc * TOK + T0:dc * TOK + T0 + TH],
                                                   sl, pj1[:], OP.mult))
                        mk[("dv_xl", tt, dc)] = s.S_DV.n

            for f in range(67):
                wf = 128 if f < 66 else 64
                pp = f % 2
                fp, sub = f // 2, f % 2
                wslab = 256 if fp < 33 else 64
                sp = fp % 2

                if sub == 0:
                    @blk.sync
                    def _(sy, f=f, fp=fp, wslab=wslab, sp=sp):
                        if f == 0:
                            sy.wait_ge(s.S_DV.h, mk[("dv_xl", 1, DC - 1)])
                            sy.wait_ge(s.S_CC.h, s.cc_ipw)
                        if ("pe_ip", 2 * (fp - 2) + 1) in mk:
                            sy.wait_ge(s.S_PE.h, mk[("pe_ip", 2 * (fp - 2) + 1)])
                        elif ("pe_ip", 2 * (fp - 2)) in mk:
                            sy.wait_ge(s.S_PE.h, mk[("pe_ip", 2 * (fp - 2))])
                        src = AP(s.ipw_bf, fp * 256,
                                 [[PROJ, 128], [PROJ * 128, DC], [1, wslab]])
                        dst = AP(ipslab, sp * DC * 256,
                                 [[2 * DC * 256, 128], [wslab, DC], [1, wslab]])
                        _, n = QIP.issue(sy, dst, src)
                        mk[("ld_ip", fp)] = (QIP.sems[(QIP.i - 1) % 2], n)

                @blk.tensor
                def _(t, f=f, wf=wf, pp=pp, fp=fp, sub=sub, wslab=wslab, sp=sp):
                    sem, n = mk[("ld_ip", fp)]
                    t.wait_ge(sem.h, n)
                    if ("ev_ip", f - 2) in mk:
                        eng, v = mk[("ev_ip", f - 2)]
                        t.wait_ge(eng, v)
                    for tt in range(2):
                        for dc in range(DC):
                            mm = t.matmul(pot[pp][0:wf, tt * TH:tt * TH + TH],
                                          ipslab[:, sp * DC * 256 + dc * wslab + sub * 128:
                                                sp * DC * 256 + dc * wslab + sub * 128 + wf],
                                          xln[:, dc * TOK + tt * TH:dc * TOK + tt * TH + TH],
                                          start=(dc == 0), stop=(dc == DC - 1))
                    mk[("pe_ip", f)] = s.S_PE.inc(mm)

                if f < IC:
                    @blk.scalar
                    def _(sc, f=f, pp=pp):
                        sc.wait_ge(s.S_PE.h, mk[("pe_ip", f)])
                        if ("st_z", f - 2) in mk:
                            sem, n = mk[("st_z", f - 2)]
                            sc.wait_ge(sem.h, n)
                        if ("st_z", f - 1) in mk:
                            sem, n = mk[("st_z", f - 1)]
                            sc.wait_ge(sem.h, n)
                        s.S_AC.inc(sc.activation(zt[:], pot[pp][:], AF.Silu, scale=0.25))
                        mk[("ev_ip", f)] = (s.S_AC.h, s.S_AC.n)

                    @blk.scalar
                    def _(sc, f=f, pp=pp):
                        eng, v = mk[("ev_ip", f)]
                        sc.wait_ge(eng, v)
                        dst = AP(s.z_spill, f * 128 * TOK, [[TOK, 128], [1, TOK]])
                        _, n = s.Q_ST.issue(sc, dst, zt[:])
                        mk[("st_z", f)] = (s.Q_ST.sems[(s.Q_ST.i - 1) % 2], n)
                elif f < 66:
                    @blk.vector
                    def _(v, f=f, pp=pp):
                        v.wait_ge(s.S_PE.h, mk[("pe_ip", f)])
                        v.wait_ge(s.S_DV.h, s.S_DV.n)
                        c = f - IC
                        s.S_DV.inc(v.tensor_scalar_mul(
                            s.xbc[:, c * XW + 3:c * XW + 3 + TOK], pot[pp][:], 0.5))
                        v.wait_ge(s.S_DV.h, s.S_DV.n)
                        s.S_DV.inc(v.tensor_copy(hstage[:, c * 3:c * 3 + 3],
                                                 s.xbc[:, c * XW + 1024:c * XW + 1027]))
                        mk[("ev_ip", f)] = (s.S_DV.h, s.S_DV.n)

                    @blk.vector
                    def _(v, f=f):
                        c = f - IC
                        base = c * XW
                        if ("ac_cvm", c - 1) in mk:
                            v.wait_ge(s.S_AC.h, mk[("ac_cvm", c - 1)])
                        v.wait_ge(s.S_DV.h, s.S_DV.n)
                        s.S_DV.inc(v.tensor_scalar_mul(acc[:, 0:1021],
                                                       s.xbc[:, base + 3:base + 3 + 1021],
                                                       s.convw_s[:, 4 * c:4 * c + 1]))
                        for k in range(1, 4):
                            v.wait_ge(s.S_DV.h, s.S_DV.n)
                            s.S_DV.inc(v.scalar_tensor_tensor(
                                acc[:, 0:1021], s.xbc[:, base + 3 + k:base + 3 + k + 1021],
                                s.convw_s[:, 4 * c + k:4 * c + k + 1], acc[:, 0:1021],
                                OP.mult, OP.add))
                        mk[("dv_cvm", c)] = s.S_DV.n

                    @blk.scalar
                    def _(sc, f=f):
                        c = f - IC
                        sc.wait_ge(s.S_DV.h, mk[("dv_cvm", c)])
                        s.S_AC.inc(sc.activation(s.xbc[:, c * XW + 6:c * XW + 6 + 1021],
                                                 acc[:, 0:1021], AF.Silu,
                                                 bias=s.convb_s[:, c:c + 1]))
                        mk[("ac_cvm", c)] = s.S_AC.n
                else:
                    @blk.scalar
                    def _(sc, f=f, pp=pp):
                        sc.wait_ge(s.S_PE.h, mk[("pe_ip", f)])
                        s.S_AC.inc(sc.activation(s.dtsp[:, :], pot[pp][0:HEADS, :],
                                                 AF.Sigmoid, scale=-1.0,
                                                 bias=s.dtbn_s[:]))
                        sc.wait_ge(s.S_AC.h, s.S_AC.n)
                        s.S_AC.inc(sc.activation(s.dtsp[:, :], s.dtsp[:, :], AF.Ln))
                        mk[("ev_ip", f)] = (s.S_AC.h, s.S_AC.n)

            @blk.gpsimd
            def _(g):
                for f in range(IC, 67):
                    eng, v = mk[("ev_ip", f)]
                    g.wait_ge(eng, v)
                xv = s.xbc.ap().rearrange("p (c t) -> p c t", c=CCH)
                s.Q_GP.issue(g, AP(s.hl_in, 0, [[3, 128], [3 * 128, CCH], [1, 3]]),
                             hstage.ap().rearrange("p (c t) -> p c t", c=CCH))
                s.Q_GP.drain(g)
                s.S_CC.inc(g.collective_compute(
                    "AllGather", OP.bypass,
                    replica_groups=[[0, 1], [2, 3], [4, 5], [6, 7]],
                    ins=[s.hl_in.ap().opt()], outs=[s.hl_out.ap().opt()]))
                g.wait_ge(s.S_CC.h, s.S_CC.n)
                s.Q_GP.issue(g, xv[:, :, 0:3],
                             AP(s.hl_out, 0, [[3, 128], [3 * 128, CCH], [1, 3]]))
                s.Q_GP.drain(g)
                s.S_GP.inc(g.engine_nop())
            mk["halo"] = s.S_GP.n

            @blk.vector
            def _(v):
                v.wait_ge(s.S_GP.h, mk["halo"])
                v.wait_ge(s.S_DV.h, s.S_DV.n)
                xv = s.xbc.ap().rearrange("p (c t) -> p c t", c=CCH)
                s.S_DV.inc(v.tensor_scalar_mul(xv[:, :, 0:3], xv[:, :, 0:3], s.flag_s[:]))
                mk["halo_m"] = s.S_DV.n

            for c in range(CCH):
                @blk.vector
                def _(v, c=c):
                    if c == 0:
                        v.wait_ge(s.S_DV.h, mk["halo_m"])
                        v.wait_ge(s.S_AC.h, mk[("ac_cvm", CCH - 1)])
                    if ("ac_cv", c - 1) in mk:
                        v.wait_ge(s.S_AC.h, mk[("ac_cv", c - 1)])
                    v.wait_ge(s.S_DV.h, s.S_DV.n)
                    base = c * XW
                    s.S_DV.inc(v.tensor_scalar_mul(acc[:, 0:3], s.xbc[:, base:base + 3],
                                                   s.convw_s[:, 4 * c:4 * c + 1]))
                    for k in range(1, 4):
                        v.wait_ge(s.S_DV.h, s.S_DV.n)
                        s.S_DV.inc(v.scalar_tensor_tensor(
                            acc[:, 0:3], s.xbc[:, base + k:base + k + 3],
                            s.convw_s[:, 4 * c + k:4 * c + k + 1], acc[:, 0:3],
                            OP.mult, OP.add))
                    mk[("dv_cv", c)] = s.S_DV.n

                @blk.scalar
                def _(sc, c=c):
                    sc.wait_ge(s.S_DV.h, mk[("dv_cv", c)])
                    s.S_AC.inc(sc.activation(s.xbc[:, c * XW + 3:c * XW + 6],
                                             acc[:, 0:3], AF.Silu,
                                             bias=s.convb_s[:, c:c + 1]))
                    mk[("ac_cv", c)] = s.S_AC.n

            self.drain_all(blk, extra=(QIP,))

    def p_scan(self):
        s = self
        nc = s.nc
        with nc.Block() as blk, contextlib.ExitStack() as ar:
            A = lambda nm, sh, dt=F32: ar.enter_context(nc.sbuf_tensor(nm, sh, dt))
            P_ = lambda nm, sh, dt=F32: ar.enter_context(nc.psum_tensor(nm, sh, dt))
            dA = A("dA", [HEADS, TOK])
            Pall = A("Pall", [HEADS, TOK])
            Pp = A("Pp", [HEADS, TOK])
            tdmat = A("tdmat", [HEADS, 8])
            Imat = A("Imat", [HEADS, 8])
            dtT = A("dtT", [128, NJ * HEADS])
            dxo = A("dxo", [128, 2 * 512], BF16)
            dxw = A("dxw", [128, 512], BF16)
            Ebuf = A("Ebuf", [128, 1024])
            Sbuf = A("Sbuf", [128, 1024], BF16)
            CPb = A("CPb", [128, 1024], BF16)
            BT = A("BT", [128, 128], BF16)
            dAz = A("dAz", [HEADS, TOK])
            identrep = A("identrep", [128, 1024], BF16)
            fA = A("fA", [1, 1024])
            fP = A("fP", [1, 1024])
            bAW = P_("bAW", [128, 1024])
            bA = [bAW.ap()[:, 0:512], bAW.ap()[:, 512:1024]]
            bCB = P_("bCB", [128, 512])
            bTR = P_("bTR", [128, 1024], BF16)
            bT2 = P_("bT2", [128, 1024], BF16)
            bY = [P_("bY%d" % i, [128, 512]) for i in range(2)]
            bG = P_("bG", [128, 512])
            QF = Dma([s.SEM("qf%d" % i, 16) for i in range(2)])
            mk = {}
            xv = lambda c, a, w: s.xbc[:, c * XW + a:c * XW + a + w]

            @blk.gpsimd
            def _(g):
                s.S_CC.inc(g.collective_compute(
                    "AllGather", OP.bypass, replica_groups=[list(range(NCORES))],
                    ins=[AP(s.pc_in, 256 * PROJ, [[1, 512 * D]])],
                    outs=[AP(s.opw_bf, 0, [[1, INTER * D]]).opt()]))
                s.cc_opw = s.S_CC.n
                s.S_GP.inc(g.memset(identrep[:], 1.0))
                g.wait_ge(s.S_GP.h, s.S_GP.n)
                s.S_GP.inc(g.affine_select(identrep[:], identrep[:], [[0, 8], [1, 128]],
                                           OP.is_equal, 0.0, base=0, channel_multiplier=-1))
                mk["irep"] = s.S_GP.n

            @blk.vector
            def _(v):
                v.wait_ge(s.S_DV.h, s.S_DV.n)
                s.S_DV.inc(v.tensor_scalar_mul(s.dtsp[:], s.dtsp[:], -1.0))
                v.wait_ge(s.S_DV.h, s.S_DV.n)
                s.S_DV.inc(v.tensor_scalar_mul(dA[:], s.dtsp[:], s.acol[:]))
                mk["dtA"] = s.S_DV.n

            @blk.scalar
            def _(sc):
                sc.wait_ge(s.S_DV.h, mk["dtA"])
                s.S_AC.inc(sc.activation(dA[:], dA[:], AF.Exp))
                mk["dA"] = s.S_AC.n

            @blk.vector
            def _(v):
                v.wait_ge(s.S_AC.h, mk["dA"])
                for j in range(NJ):
                    v.wait_ge(s.S_DV.h, s.S_DV.n)
                    s.S_DV.inc(v.tensor_tensor_scan(
                        Pall[:, j * 128:(j + 1) * 128], dA[:, j * 128:(j + 1) * 128],
                        s.zeros64[:, 0:128], 1.0, OP.mult, OP.add))
                v.wait_ge(s.S_DV.h, s.S_DV.n)
                tdsrc = AP(Pall, 127, [[TOK, HEADS], [128, 8]])
                s.S_DV.inc(v.tensor_copy(tdmat[:], tdsrc))
                v.wait_ge(s.S_DV.h, s.S_DV.n)
                s.S_DV.inc(v.tensor_tensor_scan(Imat[:], tdmat[:], s.zeros64[:, 0:8],
                                                1.0, OP.mult, OP.add))
                v.wait_ge(s.S_DV.h, s.S_DV.n)
                s.S_DV.inc(v.tensor_copy(Pp[:, 0:128], Pall[:, 0:128]))
                for j in range(1, NJ):
                    s.S_DV.inc(v.tensor_scalar_mul(Pp[:, j * 128:(j + 1) * 128],
                                                   Pall[:, j * 128:(j + 1) * 128],
                                                   Imat[:, j - 1:j]))
                mk["pp"] = s.S_DV.n

            for j in range(NJ):
                @blk.tensor
                def _(t, j=j):
                    if j == 0:
                        t.wait_ge(s.S_DV.h, s.S_DV.n)
                    if ("dv_dtT", j - 2) in mk:
                        t.wait_ge(s.S_DV.h, mk[("dv_dtT", j - 2)])
                    mm = t.transpose(bAW[:, (j % 2) * 512:(j % 2) * 512 + 64],
                                     s.dtsp[:, j * 128:(j + 1) * 128],
                                     s.ident_f[0:64, 0:64])
                    mk[("pe_dtT", j)] = s.S_PE.inc(mm)

                @blk.vector
                def _(v, j=j):
                    v.wait_ge(s.S_PE.h, mk[("pe_dtT", j)])
                    v.wait_ge(s.S_DV.h, s.S_DV.n)
                    s.S_DV.inc(v.tensor_copy(dtT[:, j * HEADS:(j + 1) * HEADS],
                                             bAW[:, (j % 2) * 512:(j % 2) * 512 + 64]))
                    mk[("dv_dtT", j)] = s.S_DV.n

            @blk.vector
            def _(v):
                v.wait_ge(s.S_AC.h, mk["dA"])
                v.wait_ge(s.S_DV.h, s.S_DV.n)
                s.S_DV.inc(v.tensor_copy(dAz[:], dA[:]))
                mk["dAz"] = s.S_DV.n

            @blk.gpsimd
            def _(g):
                g.wait_ge(s.S_DV.h, mk["dAz"])
                for j in range(NJ):
                    s.S_GP.inc(g.memset(dAz[:, j * 128:j * 128 + 1], 0.0))
                mk["dAz0"] = s.S_GP.n

            @blk.gpsimd
            def _(g):
                g.wait_ge(s.S_GP.h, mk["dAz0"])
                g.wait_ge(s.S_DV.h, mk["pp"])
                s.Q_GP.issue(g, AP(s.rowb, 0, [[TOK, HEADS], [1, TOK]]), dAz[:])
                s.Q_GP.issue(g, AP(s.rowb, 64 * TOK, [[TOK, HEADS], [1, TOK]]), Pall[:])
                s.Q_GP.issue(g, AP(s.rowb, 128 * TOK, [[TOK, HEADS], [1, TOK]]), Pp[:])
                s.Q_GP.drain(g)
                s.S_GP.inc(g.engine_nop())
            mk["rowb"] = s.S_GP.n

            @blk.gpsimd
            def _(g):
                g.wait_ge(s.S_DV.h, s.S_DV.n)
                s.S_GP.inc(g.memset(s.state[:], 0.0))
            mk["st0"] = s.S_GP.n

            for j in range(NJ):
                jc = 3 + j * 128

                @blk.tensor
                def _(t, j=j, jc=jc):
                    if j == 0:
                        t.wait_ge(s.S_DV.h, mk[("dv_dtT", NJ - 1)])
                        t.wait_ge(s.S_GP.h, mk["st0"])
                    if ("dv_BT", j - 1) in mk:
                        t.wait_ge(s.S_DV.h, mk[("dv_BT", j - 1)])
                    mm = t.transpose(bTR[:, 0:128], xv(32, jc, 128), s.ident_b[:])
                    mk[("pe_BT", j)] = s.S_PE.inc(mm)
                    if ("dv_S", j - 1, 7, 7) in mk:
                        t.wait_ge(s.S_DV.h, mk[("dv_S", j - 1, 7, 7)])
                    mm = t.matmul(bCB[:, 0:128], xv(32, jc, 128), xv(33, jc, 128),
                                  start=True, stop=True)
                    mk[("pe_CB", j)] = s.S_PE.inc(mm)

                @blk.vector
                def _(v, j=j):
                    v.wait_ge(s.S_PE.h, mk[("pe_BT", j)])
                    v.wait_ge(s.S_DV.h, s.S_DV.n)
                    s.S_DV.inc(v.tensor_copy(BT[:], bTR[:, 0:128]))
                    mk[("dv_BT", j)] = s.S_DV.n

                for o in range(8):
                    op = o % 2

                    @blk.gpsimd
                    def _(g, j=j, o=o, op=op):
                        if o == 0 and j == 0:
                            g.wait_ge(s.S_GP.h, mk["rowb"])
                        gk = ("pe_pbc", j, o - 1) if o else ("pe_pbc", j - 1, 7)
                        if gk in mk:
                            g.wait_ge(s.S_PE.h, mk[gk])
                        _, n = QF.issue(g, fA[0:1, :],
                                        AP(s.rowb, (8 * o) * TOK + j * 128,
                                           [[1, 1], [TOK, 8], [1, 128]]))
                        mk[("ld_fA", j, o)] = (QF.sems[(QF.i - 1) % 2], n)
                        _, n = QF.issue(g, fP[0:1, :],
                                        AP(s.rowb, (64 + 8 * o) * TOK + j * 128,
                                           [[1, 1], [TOK, 8], [1, 128]]))
                        mk[("ld_fP", j, o)] = (QF.sems[(QF.i - 1) % 2], n)

                    @blk.tensor
                    def _(t, j=j, o=o, op=op, jc=jc):
                        if o == 0:
                            t.wait_ge(s.S_DV.h, mk[("dv_BT", j)])
                        if ("dv_dx", j, o - 1) in mk:
                            t.wait_ge(s.S_DV.h, mk[("dv_dx", j, o - 1)])
                        tb = bTR if op == 0 else bT2
                        for q in range(4):
                            ic = o * 4 + q
                            mm = t.transpose(tb[:, 128 + q * 128:128 + q * 128 + 128]
                                             if tb is bTR else tb[:, q * 128:q * 128 + 128],
                                             xv(ic, jc, 128), s.ident_b[:])
                        mk[("pe_tr", j, o)] = s.S_PE.inc(mm)

                    @blk.vector
                    def _(v, j=j, o=o, op=op):
                        v.wait_ge(s.S_PE.h, mk[("pe_tr", j, o)])
                        tb = bTR if op == 0 else bT2
                        base = 128 if op == 0 else 0
                        v.wait_ge(s.S_DV.h, s.S_DV.n)
                        dtrep = dtT.ap()[:, j * HEADS + 8 * o:j * HEADS + 8 * o + 8] \
                            .unsqueeze(2).broadcast_to([128, 8, 64])
                        s.S_DV.inc(v.tensor_tensor(
                            dxo.ap()[:, op * 512:op * 512 + 512]
                            .rearrange("p (a b) -> p a b", a=8),
                            tb.ap()[:, base:base + 512].rearrange("p (a b) -> p a b", a=8),
                            dtrep, OP.mult))
                        mk[("dv_dx", j, o)] = s.S_DV.n

                    @blk.tensor
                    def _(t, j=j, o=o, op=op):
                        sem, n = mk[("ld_fA", j, o)]
                        t.wait_ge(sem.h, n)
                        for key in ((("dv_ch", j, o - 1, 7) if o else ("dv_ch", j - 1, 7, 7)),
                                    (("dv_CP", j, o - 1, 7) if o else ("dv_CP", j - 1, 7, 7))):
                            if key in mk:
                                t.wait_ge(s.S_DV.h, mk[key])
                        for q in range(2):
                            mm = t.matmul(bA[q][:], s.ones1[:],
                                          fA[0:1, q * 512:q * 512 + 512],
                                          start=True, stop=True)
                        mk[("pe_abc", j, o)] = s.S_PE.inc(mm)

                    @blk.vector
                    def _(v, j=j, o=o, op=op):
                        v.wait_ge(s.S_PE.h, mk[("pe_abc", j, o)])
                        v.wait_ge(s.S_PE.h, mk[("pe_CB", j)])
                        v.wait_ge(s.S_DV.h, mk[("dv_dx", j, o)])
                        gk = ("pe_G", j, o - 1) if o else ("pe_G", j - 1, 7)
                        if gk in mk:
                            v.wait_ge(s.S_PE.h, mk[gk])
                        v.wait_ge(s.S_DV.h, s.S_DV.n)
                        if j == 0 and o == 0:
                            v.wait_ge(s.S_GP.h, mk["irep"])
                        s.S_DV.inc(v.tensor_tensor_scan(
                            Ebuf[:], bAW[:], identrep[:], 0.0, OP.mult, OP.add))
                        v.wait_ge(s.S_DV.h, s.S_DV.n)
                        cbrep = bCB.ap()[:, 0:128].unsqueeze(1).broadcast_to([128, 8, 128])
                        s.S_DV.inc(v.tensor_tensor(
                            Sbuf.ap().rearrange("p (a b) -> p a b", a=8),
                            Ebuf.ap().rearrange("p (a b) -> p a b", a=8), cbrep, OP.mult))
                        v.wait_ge(s.S_DV.h, s.S_DV.n)
                        wrep = AP(Ebuf, 127, [[1024, 128], [128, 8]]) \
                            .unsqueeze(2).broadcast_to([128, 8, 64])
                        s.S_DV.inc(v.tensor_tensor(
                            dxw.ap().rearrange("p (a b) -> p a b", a=8),
                            dxo.ap()[:, op * 512:op * 512 + 512]
                            .rearrange("p (a b) -> p a b", a=8), wrep, OP.mult))
                        n = s.S_DV.n
                        for m in range(8):
                            mk[("dv_E", j, o, m)] = n
                            mk[("dv_S", j, o, m)] = n

                    @blk.tensor
                    def _(t, j=j, o=o, op=op):
                        sem, n = mk[("ld_fP", j, o)]
                        t.wait_ge(sem.h, n)
                        t.wait_ge(s.S_DV.h, mk[("dv_E", j, o, 7)])
                        for q in range(2):
                            mm = t.matmul(bA[q][:], s.ones1[:],
                                          fP[0:1, q * 512:q * 512 + 512],
                                          start=True, stop=True)
                        mk[("pe_pbc", j, o)] = s.S_PE.inc(mm)

                    @blk.vector
                    def _(v, j=j, o=o, op=op, jc=jc):
                        v.wait_ge(s.S_PE.h, mk[("pe_pbc", j, o)])
                        v.wait_ge(s.S_DV.h, s.S_DV.n)
                        crep = s.xbc.ap()[:, 33 * XW + jc:33 * XW + jc + 128] \
                            .unsqueeze(1).broadcast_to([128, 8, 128])
                        s.S_DV.inc(v.tensor_tensor(
                            CPb.ap().rearrange("p (a b) -> p a b", a=8), crep,
                            bAW.ap().rearrange("p (a b) -> p a b", a=8), OP.mult))
                        n = s.S_DV.n
                        for m in range(8):
                            mk[("dv_CP", j, o, m)] = n

                    for m in range(8):
                        h = 8 * o + m

                        @blk.tensor
                        def _(t, j=j, o=o, op=op, m=m, h=h):
                            if m == 0:
                                t.wait_ge(s.S_DV.h, mk[("dv_S", j, o, 7)])
                                t.wait_ge(s.S_DV.h, mk[("dv_CP", j, o, 7)])
                            yb = bY[op]
                            reg = yb[(m % 2) * 64:(m % 2) * 64 + 64,
                                     (m // 2) * 128:(m // 2) * 128 + 128]
                            tp = (0, 64) if m % 2 else None
                            mm = t.matmul(reg, dxo[:, op * 512 + m * 64:op * 512 + m * 64 + 64],
                                          Sbuf[:, m * 128:m * 128 + 128],
                                          start=True, stop=(j == 0), tile_position=tp)
                            if j > 0:
                                mm = t.matmul(reg, s.state[:, h * 64:h * 64 + 64],
                                              CPb[:, m * 128:m * 128 + 128],
                                              start=False, stop=True, tile_position=tp)
                            mk[("pe_Y", j, o, m)] = s.S_PE.inc(mm)

                    @blk.tensor
                    def _(t, j=j, o=o, op=op):
                        t.wait_ge(s.S_DV.h, mk[("dv_E", j, o, 7)])
                        if ("dv_ch", j, o - 1, 7) in mk:
                            t.wait_ge(s.S_DV.h, mk[("dv_ch", j, o - 1, 7)])
                        mm = t.matmul(bG[:], BT[:], dxw[:],
                                      start=True, stop=True)
                        mk[("pe_G", j, o)] = s.S_PE.inc(mm)

                    @blk.vector
                    def _(v, j=j, o=o, op=op):
                        v.wait_ge(s.S_PE.h, mk[("pe_G", j, o)])
                        v.wait_ge(s.S_DV.h, s.S_DV.n)
                        tdrep = AP(bAW, 127, [[1024, 128], [128, 8]]) \
                            .unsqueeze(2).broadcast_to([128, 8, 64])
                        stv = s.state.ap()[:, o * 512:o * 512 + 512] \
                            .rearrange("p (a b) -> p a b", a=8)
                        s.S_DV.inc(v.tensor_tensor(stv, stv, tdrep, OP.mult))
                        v.wait_ge(s.S_DV.h, s.S_DV.n)
                        s.S_DV.inc(v.tensor_tensor(
                            s.state[:, o * 512:o * 512 + 512],
                            s.state[:, o * 512:o * 512 + 512], bG[:], OP.add))
                        n = s.S_DV.n
                        for m in range(8):
                            mk[("dv_ch", j, o, m)] = n

                    for pr in range(4):
                        ic = o * 4 + pr

                        @blk.vector
                        def _(v, j=j, o=o, op=op, pr=pr, ic=ic, jc=jc):
                            v.wait_ge(s.S_PE.h, mk[("pe_Y", j, o, 7)])
                            v.wait_ge(s.S_DV.h, s.S_DV.n)
                            s.S_DV.inc(v.scalar_tensor_tensor(
                                xv(ic, jc, 128), xv(ic, jc, 128),
                                s.drep_s[:, ic:ic + 1],
                                bY[op][:, pr * 128:pr * 128 + 128], OP.mult, OP.add))
                            mk[("dv_Y", j, o, pr)] = s.S_DV.n

            @blk.gpsimd
            def _(g):
                g.wait_ge(s.S_DV.h, s.S_DV.n)
                s.Q_GP.issue(g, AP(s.s_in, 0, [[INTER, GTS], [1, INTER]]), s.state[:])
                s.Q_GP.drain(g)
                s.S_CC.inc(g.collective_compute(
                    "AllGather", OP.bypass,
                    replica_groups=[[0, 1], [2, 3], [4, 5], [6, 7]],
                    ins=[s.s_in.ap().opt()], outs=[s.s_out.ap().opt()]))
                g.wait_ge(s.S_CC.h, s.S_CC.n)
                s.Q_GP.issue(g, s.state[:], AP(s.s_out, 0, [[INTER, GTS], [1, INTER]]))
                s.Q_GP.drain(g)
                s.S_GP.inc(g.engine_nop())
            self.mk_ag2 = s.S_GP.n

            @blk.vector
            def _(v):
                v.wait_ge(s.S_GP.h, self.mk_ag2)
                v.wait_ge(s.S_DV.h, s.S_DV.n)
                s.S_DV.inc(v.tensor_scalar_mul(s.state[:], s.state[:], s.flag_s[:]))
            self.mk_smask = s.S_DV.n
            self.drain_all(blk, extra=(QF,))

    def p_carry2(self):
        s = self
        nc = s.nc
        with nc.Block() as blk, contextlib.ExitStack() as ar:
            A = lambda nm, sh, dt=F32: ar.enter_context(nc.sbuf_tensor(nm, sh, dt))
            P_ = lambda nm, sh, dt=F32: ar.enter_context(nc.psum_tensor(nm, sh, dt))
            fP = A("c_fP", [1, 1024])
            CPb4 = A("c_CPb4", [128, 8 * 512], BF16)
            bAW = P_("c_bAW", [128, 1024])
            bA = [bAW.ap()[:, 0:512], bAW.ap()[:, 512:1024]]
            bY = [P_("c_bY%d" % i, [128, 512]) for i in range(4)]
            QF = Dma([s.SEM("cqf%d" % i, 16) for i in range(2)])
            mk = {}
            xv = lambda c, a, w: s.xbc[:, c * XW + a:c * XW + a + w]

            for jh in range(2):
                for o in range(8):
                    for jl in range(4):
                        j = jh * 4 + jl
                        jc = 3 + j * 128

                        @blk.gpsimd
                        def _(g, j=j, o=o, jl=jl):
                            gk = (("pe_pbc", o, jl - 1) if jl else
                                  (("pe_pbc", o - 1, 3) if o else ("pe_pbc_l", jh - 1)))
                            if gk in mk:
                                g.wait_ge(s.S_PE.h, mk[gk])
                            _, n = QF.issue(g, fP[0:1, :],
                                            AP(s.rowb, (128 + 8 * o) * TOK + j * 128,
                                               [[1, 1], [TOK, 8], [1, 128]]))
                            mk[("ld_fP", o, jl)] = (QF.sems[(QF.i - 1) % 2], n)

                        @blk.tensor
                        def _(t, j=j, o=o, jl=jl):
                            sem, n = mk[("ld_fP", o, jl)]
                            t.wait_ge(sem.h, n)
                            if j == 0 and o == 0 and jh == 0:
                                t.wait_ge(s.S_DV.h, s.mk_smask)
                            gk = (("dv_CP", o, jl - 1) if jl else
                                  (("dv_CP", o - 1, 3) if o else ("dv_CP_l", jh - 1)))
                            if gk in mk:
                                t.wait_ge(s.S_DV.h, mk[gk])
                            for q in range(2):
                                mm = t.matmul(bA[q][:], s.ones1[:],
                                              fP[0:1, q * 512:q * 512 + 512],
                                              start=True, stop=True)
                            mk[("pe_pbc", o, jl)] = s.S_PE.inc(mm)
                            if jl == 3:
                                mk[("pe_pbc_l", jh)] = s.S_PE.n

                        @blk.vector
                        def _(v, j=j, o=o, jl=jl, jc=jc):
                            v.wait_ge(s.S_PE.h, mk[("pe_pbc", o, jl)])
                            if jl == 0:
                                gk = ("pe_Y", o - 1, 7) if o else ("pe_Y_l", jh - 1)
                                if gk in mk:
                                    v.wait_ge(s.S_PE.h, mk[gk])
                            v.wait_ge(s.S_DV.h, s.S_DV.n)
                            crep = s.xbc.ap()[:, 33 * XW + jc:33 * XW + jc + 128] \
                                .unsqueeze(1).broadcast_to([128, 8, 128])
                            dst = AP(CPb4, jl * 128, [[8 * 512, 128], [512, 8], [1, 128]])
                            s.S_DV.inc(v.tensor_tensor(
                                dst, crep,
                                bAW.ap().rearrange("p (a b) -> p a b", a=8), OP.mult))
                            mk[("dv_CP", o, jl)] = s.S_DV.n

                    for m in range(8):
                        h = 8 * o + m

                        @blk.tensor
                        def _(t, o=o, m=m, h=h, jh=jh):
                            if m == 0:
                                t.wait_ge(s.S_DV.h, mk[("dv_CP", o, 3)])
                                gk = ("dv_Y", o - 1, 3) if o else ("dv_Y_l", jh - 1)
                                if gk in mk:
                                    t.wait_ge(s.S_DV.h, mk[gk])
                            reg = bY[m // 2][(m % 2) * 64:(m % 2) * 64 + 64, :]
                            tp = (0, 64) if m % 2 else None
                            mm = t.matmul(reg, s.state[:, h * 64:h * 64 + 64],
                                          CPb4[:, m * 512:m * 512 + 512],
                                          start=True, stop=True, tile_position=tp)
                            mk[("pe_Y", o, m)] = s.S_PE.inc(mm)
                            if m == 7:
                                mk[("pe_Y_l", jh)] = s.S_PE.n

                    for pr in range(4):
                        ic = o * 4 + pr

                        @blk.vector
                        def _(v, o=o, pr=pr, ic=ic, jh=jh):
                            v.wait_ge(s.S_PE.h, mk[("pe_Y", o, 7)])
                            v.wait_ge(s.S_DV.h, s.S_DV.n)
                            s.S_DV.inc(v.tensor_tensor(
                                xv(ic, 3 + jh * 512, 512), xv(ic, 3 + jh * 512, 512),
                                bY[pr][:], OP.add))
                            mk[("dv_Y", o, pr)] = s.S_DV.n
                            if pr == 3:
                                mk[("dv_Y_l", jh)] = s.S_DV.n

            self.drain_all(blk, extra=(QF,))

    def p_gate_out(self):
        s = self
        nc = s.nc
        with nc.Block() as blk, contextlib.ExitStack() as ar:
            A = lambda nm, sh, dt=F32: ar.enter_context(nc.sbuf_tensor(nm, sh, dt))
            P_ = lambda nm, sh, dt=F32: ar.enter_context(nc.psum_tensor(nm, sh, dt))
            gated = A("gated", [128, IC * TH], BF16)
            y2b = A("y2b", [128, 2 * TH], BF16)
            zt = A("g_zt", [128, 2 * TH], BF16)
            tq = A("g_tq", [128, TH])
            opslab = A("opslab", [128, 2 * IC * 128], BF16)
            g0 = P_("g0", [128, 512])
            g1 = P_("g1", [128, 512])
            go = [P_("go%d" % i, [128, 512]) for i in range(2)]
            QOP = Dma([s.SEM("qop%d" % i, 16) for i in range(2)])
            QZ = Dma([s.SEM("qz%d" % i, 16) for i in range(2)])
            mk = {}
            xv = lambda c, a, w: s.xbc[:, c * XW + a:c * XW + a + w]

            for tt in range(2):
                T0 = 3 + tt * TH
                for ic in range(IC):
                    pp = ic % 2

                    @blk.scalar
                    def _(sc, ic=ic, pp=pp, T0=T0, tt=tt):
                        if tt == 0 and ic == 0:
                            sc.wait_ge(s.S_DV.h, s.S_DV.n)
                        if ("pe_rms", tt, ic - 2) in mk:
                            sc.wait_ge(s.S_PE.h, mk[("pe_rms", tt, ic - 2)])
                        s.S_AC.inc(sc.activation(y2b[:, pp * TH:pp * TH + TH],
                                                 xv(ic, T0, TH), AF.Square))
                        mk[("ac_y2", tt, ic)] = s.S_AC.n

                    @blk.tensor
                    def _(t, ic=ic, pp=pp, tt=tt):
                        t.wait_ge(s.S_AC.h, mk[("ac_y2", tt, ic)])
                        if tt == 1 and ic == 0:
                            t.wait_ge(s.S_DV.h, s.S_DV.n)
                        mm = t.matmul(g0[:], s.ones_b[:], y2b[:, pp * TH:pp * TH + TH],
                                      start=(ic == 0), stop=(ic == IC - 1))
                        mk[("pe_rms", tt, ic)] = s.S_PE.inc(mm)

                @blk.vector
                def _(v, tt=tt):
                    v.wait_ge(s.S_PE.h, mk[("pe_rms", tt, IC - 1)])
                    v.wait_ge(s.S_DV.h, s.S_DV.n)
                    s.S_DV.inc(v.tensor_scalar_mul(s.row(0), g0[0:1, :], 1.0 / INTER))
                    mk["g_r0"] = s.S_DV.n

                @blk.scalar
                def _(sc, tt=tt):
                    sc.wait_ge(s.S_DV.h, mk["g_r0"])
                    s.S_AC.inc(sc.activation(s.row(1), s.row(0), AF.Sqrt,
                                             bias=s.epscol[0:1, 0:1]))
                    mk["g_r1"] = s.S_AC.n

                @blk.vector
                def _(v, tt=tt):
                    v.wait_ge(s.S_AC.h, mk["g_r1"])
                    v.wait_ge(s.S_DV.h, s.S_DV.n)
                    s.S_DV.inc(v.reciprocal(s.row(2), s.row(1)))
                    mk["g_r2"] = s.S_DV.n

                @blk.tensor
                def _(t, tt=tt):
                    t.wait_ge(s.S_DV.h, mk["g_r2"])
                    mm = t.matmul(g1[:], s.ones1[:], s.row(2), start=True, stop=True)
                    mk[("pe_g1", tt)] = s.S_PE.inc(mm)

                for ic in range(IC):
                    pp = ic % 2

                    @blk.sync
                    def _(sy, ic=ic, pp=pp, tt=tt):
                        gk = ("dv_gate", tt, ic - 2) if ic >= 2 else ("dv_gate", tt - 1, IC - 2 + ic)
                        if gk in mk:
                            sy.wait_ge(s.S_DV.h, mk[gk])
                        src = AP(s.z_spill, ic * 128 * TOK + tt * TH, [[TOK, 128], [1, TH]])
                        _, n = QZ.issue(sy, zt[:, pp * TH:pp * TH + TH], src)
                        mk[("ld_z", tt, ic)] = (QZ.sems[(QZ.i - 1) % 2], n)

                    @blk.vector
                    def _(v, ic=ic, pp=pp, T0=T0, tt=tt):
                        if ic == 0:
                            v.wait_ge(s.S_PE.h, mk[("pe_g1", tt)])
                        sem, n = mk[("ld_z", tt, ic)]
                        v.wait_ge(sem.h, n)
                        v.wait_ge(s.S_DV.h, s.S_DV.n)
                        s.S_DV.inc(v.scalar_tensor_tensor(
                            tq[:], xv(ic, T0, TH),
                            s.nw_s[:, ic:ic + 1], g1[:], OP.mult, OP.mult))
                        v.wait_ge(s.S_DV.h, s.S_DV.n)
                        s.S_DV.inc(v.tensor_tensor(gated[:, ic * TH:ic * TH + TH],
                                                   tq[:],
                                                   zt[:, pp * TH:pp * TH + TH], OP.mult))
                        mk[("dv_gate", tt, ic)] = s.S_DV.n

                for do in range(DC):
                    pp = do % 2

                    @blk.sync
                    def _(sy, do=do, pp=pp, tt=tt):
                        if tt == 0 and do == 0:
                            sy.wait_ge(s.S_CC.h, s.cc_opw)
                        if ("pe_op", tt, do - 2) in mk:
                            sy.wait_ge(s.S_PE.h, mk[("pe_op", tt, do - 2)])
                        elif tt == 1 and ("pe_op", 0, DC - 2 + do) in mk:
                            sy.wait_ge(s.S_PE.h, mk[("pe_op", 0, DC - 2 + do)])
                        src = AP(s.opw_bf, do * 128,
                                 [[D, 128], [D * 128, IC], [1, 128]])
                        dst = AP(opslab, pp * IC * 128, [[2 * IC * 128, 128], [128, IC], [1, 128]])
                        _, n = QOP.issue(sy, dst, src)
                        mk[("ld_op", tt, do)] = (QOP.sems[(QOP.i - 1) % 2], n)

                    @blk.tensor
                    def _(t, do=do, pp=pp, tt=tt):
                        sem, n = mk[("ld_op", tt, do)]
                        t.wait_ge(sem.h, n)
                        if do == 0:
                            t.wait_ge(s.S_DV.h, mk[("dv_gate", tt, IC - 1)])
                        if ("dv_op", tt, do - 2) in mk:
                            t.wait_ge(s.S_DV.h, mk[("dv_op", tt, do - 2)])
                        for ic in range(IC):
                            mm = t.matmul(go[pp][:],
                                          opslab[:, pp * IC * 128 + ic * 128:
                                                pp * IC * 128 + ic * 128 + 128],
                                          gated[:, ic * TH:ic * TH + TH],
                                          start=(ic == 0), stop=(ic == IC - 1))
                        mk[("pe_op", tt, do)] = s.S_PE.inc(mm)

                    @blk.vector
                    def _(v, do=do, pp=pp, tt=tt):
                        v.wait_ge(s.S_PE.h, mk[("pe_op", tt, do)])
                        v.wait_ge(s.S_DV.h, s.S_DV.n)
                        T0r = tt * TH
                        s.S_DV.inc(v.tensor_tensor(
                            s.resid[:, do * TOK + T0r:do * TOK + T0r + TH],
                            s.resid[:, do * TOK + T0r:do * TOK + T0r + TH],
                            go[pp][:], OP.add))
                        mk[("dv_op", tt, do)] = s.S_DV.n

            self.drain_all(blk, extra=(QOP, QZ))

    def p_store(self):
        s = self
        with s.nc.Block() as blk:
            @blk.scalar
            def _(sc):
                dst = AP(s.out_ap.tensor, 0, [[TOK, 128], [TOK * 128, DC], [1, TOK]])
                s.Q_ST.issue(sc, dst, s.resid.ap().rearrange("p (c t) -> p c t", c=DC))
                s.Q_ST.drain(sc)


def _prep_inputs(inputs, c):
    b, hf = c // 2, c % 2
    x = np.asarray(inputs["L0_attn_hidden_states"])
    w1 = np.asarray(inputs["L0_ffn_w1"])
    w2 = np.asarray(inputs["L0_ffn_w2"])
    w3 = np.asarray(inputs["L0_ffn_w3"])
    return dict(
        x_fm=np.ascontiguousarray(x[b, hf * TOK:(hf + 1) * TOK, :].T),
        w1s=np.ascontiguousarray(w1[:, c * 256:(c + 1) * 256]),
        w3s=np.ascontiguousarray(w3[:, c * 256:(c + 1) * 256]),
        w2s=np.ascontiguousarray(w2[c * 256:(c + 1) * 256, :]),
        ipw_sh=np.ascontiguousarray(
            np.asarray(inputs["L0_attn_in_proj_w"])[c * 256:(c + 1) * 256, :]),
        opw_sh=np.ascontiguousarray(
            np.asarray(inputs["L0_attn_out_proj_w"])[c * 512:(c + 1) * 512, :]),
        convw=np.ascontiguousarray(inputs["L0_attn_conv_w"]),
        convb=np.ascontiguousarray(np.asarray(inputs["L0_attn_conv_b"]).reshape(CONV, 1)),
        alog=np.ascontiguousarray(np.asarray(inputs["L0_attn_A_log"]).reshape(HEADS, 1)),
        dtb=np.ascontiguousarray(np.asarray(inputs["L0_attn_dt_bias"]).reshape(HEADS, 1)),
        drep=np.ascontiguousarray(np.repeat(np.asarray(inputs["L0_attn_D"]), PD).reshape(INTER, 1)),
        nw=np.ascontiguousarray(np.asarray(inputs["L0_attn_norm_w"]).reshape(INTER, 1)),
        flag=np.full((128, 1), float(hf), np.float32),
    )


def kernel(**inputs):
    mk = MK()
    nc = mk.build()
    in_maps = [_prep_inputs(inputs, c) for c in range(NCORES)]
    res = run_bass_kernel_spmd(nc, in_maps, core_ids=list(range(NCORES)))
    out = np.empty((4, 2048, 2048), np.float32)
    for c in range(NCORES):
        b, hf = c // 2, c % 2
        out[b, hf * TOK:(hf + 1) * TOK, :] = np.asarray(res.results[c]["out_fm"]).T
    return out
